# version 16
# baseline (speedup 1.0000x reference)
import numpy as np
import ml_dtypes
from contextlib import ExitStack

import concourse.bass as bass
import concourse.mybir as mybir
from concourse.bass_utils import run_bass_kernel_spmd

F32 = mybir.dt.float32
BF16 = mybir.dt.bfloat16
AF = mybir.ActivationFunctionType
ALU = mybir.AluOpType
AX = mybir.AxisListType

NCORES = 8
NSEQ = 2
S = 2048
D = 1024
H = 16
NT = 16
QB = 4
NQB = 4
NSLOT = 8
EPS = 1e-5


def build_program(n_seq=NSEQ, n_qb=NQB, stage="full", races=True):
    nc = bass.Bass("TRN2", target_bir_lowering=False, detect_race_conditions=races)

    x_d = nc.dram_tensor("x", [NSEQ * S, D], F32, kind="ExternalInput")
    wqkv_d = nc.dram_tensor("wqkv_r", [D, 3 * D], F32, kind="ExternalInput")
    bqkv_d = nc.dram_tensor("bqkv_r", [1, 3 * D], F32, kind="ExternalInput")
    wout_d = nc.dram_tensor("wout", [D, D], F32, kind="ExternalInput")
    bout_d = nc.dram_tensor("bout", [1, D], F32, kind="ExternalInput")
    kern_d = nc.dram_tensor("kern", [D, D], F32, kind="ExternalInput")
    fbias_d = nc.dram_tensor("fbias", [1, D], F32, kind="ExternalInput")
    al_d = nc.dram_tensor("a_l", [128, 128], F32, kind="ExternalInput")
    bl_d = nc.dram_tensor("b_l", [16, D], F32, kind="ExternalInput")
    idf_d = nc.dram_tensor("identf", [128, 128], F32, kind="ExternalInput")
    onesf_d = nc.dram_tensor("ones_f", [1, 128], F32, kind="ExternalInput")
    onesbr_d = nc.dram_tensor("ones_bfr", [1, 128], BF16, kind="ExternalInput")
    onesb_d = nc.dram_tensor("ones_bf", [128, 64], BF16, kind="ExternalInput")
    tri_d = nc.dram_tensor("tri", [128, 128], BF16, kind="ExternalInput")
    out_d = nc.dram_tensor("out", [NSEQ * S, D], F32, kind="ExternalOutput")
    wscr_d = nc.dram_tensor("wqkv_bf", [D, 3 * D], BF16, kind="Internal")
    if stage != "full":
        dbgb_d = nc.dram_tensor("dbgb", [128, 16384], BF16, kind="ExternalOutput")
        dbgf_d = nc.dram_tensor("dbgf", [128, 1024], F32, kind="ExternalOutput")

    ctx = ExitStack()
    with ctx:
        sbt = nc.sbuf_tensor
        xbuf = ctx.enter_context(sbt("xbuf", [128, NSLOT * D], F32))
        KT = ctx.enter_context(sbt("KT", [128, 8 * S], BF16))
        V = ctx.enter_context(sbt("V", [128, NT * D], BF16))
        QT = ctx.enter_context(sbt("QT", [128, 8 * 512], BF16))
        y1T = ctx.enter_context(sbt("y1T", [128, 8 * D], BF16))
        y2T = ctx.enter_context(sbt("y2T", [128, 8 * 512], BF16))
        ystage = ctx.enter_context(sbt("ystage", [128, 4 * D], BF16))
        wstage = ctx.enter_context(sbt("wstage", [128, 2 * D], BF16))
        weff = ctx.enter_context(sbt("weff", [128, 8 * D], BF16))
        wout_s = ctx.enter_context(sbt("wout_s", [128, 8 * D], BF16))
        probsT = ctx.enter_context(sbt("probsT", [128, 2 * 1024], BF16))
        attoT = ctx.enter_context(sbt("attoT", [128, 8 * 512], BF16))
        recip = ctx.enter_context(sbt("recip", [64, 1024], F32))
        sqd = ctx.enter_context(sbt("sqd", [128, 4 * D], BF16))
        bqkv = ctx.enter_context(sbt("bqv_sb", [1, D], BF16))
        beff = ctx.enter_context(sbt("beff_sb", [1, D], BF16))
        bout = ctx.enter_context(sbt("bout_sb", [1, D], BF16))
        bqk8 = ctx.enter_context(sbt("bqk8", [128, 16], F32))
        idf = ctx.enter_context(sbt("idf", [128, 128], F32))
        onesf = ctx.enter_context(sbt("onesf", [1, 128], F32))
        onesbr = ctx.enter_context(sbt("onesbr", [1, 128], BF16))
        onesb = ctx.enter_context(sbt("onesb", [128, 64], BF16))
        identb = ctx.enter_context(sbt("identb", [128, 128], BF16))
        tri = ctx.enter_context(sbt("tri_sb", [128, 128], BF16))
        lns = ctx.enter_context(sbt("lns", [128, 64], F32))
        nrm = ctx.enter_context(sbt("nrm", [128, 64], F32))
        mqk = ctx.enter_context(sbt("mqk", [16, 8], F32))
        crow = ctx.enter_context(sbt("crow", [1, 32], F32))
        cbias = ctx.enter_context(sbt("cbias", [128, 8], F32))
        scal = ctx.enter_context(sbt("scal", [128, 8], F32))
        PS = ctx.enter_context(nc.psum_tensor("PS", [128, 4096], F32))

        semnames = ["pe", "ve", "ac", "po"]
        for pool, npool in (("dx", 2), ("dt", 4), ("ds", 2), ("dw", 2), ("du", 2)):
            semnames += [f"{pool}{i}" for i in range(npool)]
        semnames = tuple(semnames)
        sems = {s: ctx.enter_context(nc.semaphore(s)) for s in semnames}
        block = ctx.enter_context(nc.Block())

        q = {"sync": [], "tensor": [], "vector": [], "scalar": [], "gpsimd": []}
        cnt = {k: 0 for k in semnames}

        def wt(eng, sem, val):
            if val and val > 0:
                q[eng].append(("w", sem, val))

        def op(eng, fn, sem=None, amt=1):
            if sem is not None:
                cnt[sem] += amt
                q[eng].append(("o", fn, sem, amt))
                return cnt[sem]
            q[eng].append(("o", fn, None, 0))
            return None

        class DmaPool:
            def __init__(self, prefix, n):
                self.names = [f"{prefix}{i}" for i in range(n)]
                self.uses = [0] * n
                self.i = 0

            def issue(self, eng, fn, extra_waits=()):
                n = len(self.names)
                p = self.i % n
                self.i += 1
                nm = self.names[p]
                if self.uses[p] > 0:
                    wt(eng, nm, 16 * self.uses[p])
                for (wnm, wval) in extra_waits:
                    if wval:
                        wt(eng, wnm, wval)
                op(eng, fn, nm, 16)
                self.uses[p] += 1
                return (nm, 16 * self.uses[p])

            def barrier(self):
                return [(nm, 16 * u) for nm, u in zip(self.names, self.uses) if u > 0]

        PX = DmaPool("dx", 2)
        PT = DmaPool("dt", 4)
        PSTO = DmaPool("ds", 2)
        PW = DmaPool("dw", 2)
        PU = DmaPool("du", 2)

        def wtl(eng, pairs):
            for nm, v in pairs:
                wt(eng, nm, v)

        def opw(eng, fn, sem):
            v = op(eng, fn, sem)
            wt(eng, sem, v)
            return v

        xrows = x_d.ap()
        orows = out_d.ap()

        PX.issue("sync", lambda e: e.dma_start(idf[:], idf_d.ap()))
        PX.issue("sync", lambda e: e.dma_start(onesf[:], onesf_d.ap()))
        PX.issue("sync", lambda e: e.dma_start(onesbr[:], onesbr_d.ap()))
        PX.issue("sync", lambda e: e.dma_start(onesb[:], onesb_d.ap()))
        PX.issue("sync", lambda e: e.dma_start(tri[:], tri_d.ap()))
        consts_ld = PX.barrier()

        sw_bqkv = PU.issue("gpsimd", lambda e: e.dma_start(bqkv[:], bqkv_d.ap()[0:1, 2 * D:3 * D]))
        sw_beff = PU.issue("gpsimd", lambda e: e.dma_start(beff[:], fbias_d.ap()))
        PU.issue("gpsimd", lambda e: e.dma_start(bout[:], bout_d.ap()))
        sw_al = PU.issue("gpsimd", lambda e: e.dma_start(
            xbuf[:, 7 * D:7 * D + 128], al_d.ap()))
        sw_bl = PU.issue("gpsimd", lambda e: e.dma_start(V[0:16, 0:1024], bl_d.ap()))
        for c in range(8):
            PU.issue("gpsimd", lambda e, c=c: e.dma_start(
                wout_s[:, c * D:(c + 1) * D], wout_d.ap()[c * 128:(c + 1) * 128, :]))
        wld = {}
        wst = {}
        for i in range(24):
            c, h3 = divmod(i, 3)
            sl = i % 2
            extra = [wst[i - 2]] if (i - 2) in wst else []
            wld[i] = PU.issue("gpsimd", lambda e, c=c, h3=h3, sl=sl: e.dma_start(
                wstage[:, sl * D:(sl + 1) * D],
                wqkv_d.ap()[c * 128:(c + 1) * 128, h3 * D:(h3 + 1) * D]), extra)
            wst[i] = PU.issue("gpsimd", lambda e, c=c, h3=h3, sl=sl: e.dma_start(
                wscr_d.ap()[c * 128:(c + 1) * 128, h3 * D:(h3 + 1) * D],
                wstage[:, sl * D:(sl + 1) * D]), [wld[i]])
        wscr_done = [wst[22], wst[23]]

        wtl("vector", [sw_beff])
        op("vector", lambda e: e.tensor_scalar(beff[:], beff[:], 0.5, None, ALU.mult), "ve")
        wtl("vector", consts_ld)
        op("vector", lambda e: e.tensor_copy(identb[:], idf[:]), "ve")

        bqrow = PX.issue("sync", lambda e: e.dma_start(
            xbuf[0:1, 4 * D:4 * D + 1024], bqkv_d.ap()[0:1, 0:D]))
        bkrow = PX.issue("sync", lambda e: e.dma_start(
            recip[0:1, 0:1024], bqkv_d.ap()[0:1, D:2 * D]))
        wtl("tensor", [bqrow, bkrow])
        wtl("tensor", consts_ld)
        for m in range(8):
            for r in range(2):
                hc = (2 * m + r) * 64
                op("tensor", lambda e, m=m, r=r, hc=hc: e.transpose(
                    PS[0:64, r * 16 + m:r * 16 + m + 1], xbuf[0:1, 4 * D + hc:4 * D + hc + 64], idf[0:1, 0:1]))
                pe_bias = op("tensor", lambda e, m=m, r=r, hc=hc: e.transpose(
                    PS[0:64, 32 + r * 16 + m:32 + r * 16 + m + 1], recip[0:1, hc:hc + 64],
                    idf[0:1, 0:1]), "pe")
        wt("vector", "pe", pe_bias)
        op("vector", lambda e: e.tensor_scalar(bqk8[0:64, 0:8], PS[0:64, 0:8], 0.125, None, ALU.mult), "ve")
        op("vector", lambda e: e.tensor_scalar(bqk8[64:128, 0:8], PS[0:64, 16:24], 0.125, None, ALU.mult), "ve")
        op("vector", lambda e: e.tensor_copy(bqk8[0:64, 8:16], PS[0:64, 32:40]), "ve")
        op("vector", lambda e: e.tensor_copy(bqk8[64:128, 8:16], PS[0:64, 48:56]), "ve")

        prev_red = 0
        red4 = {}
        for c in range(8):
            sl = c % 4
            kld = PX.issue("sync", lambda e, c=c, sl=sl: e.dma_start(
                xbuf[:, sl * D:(sl + 1) * D], kern_d.ap()[c * 128:(c + 1) * 128, :]),
                [("ve", red4.get(c - 4, 0))])
            wtl("vector", [kld])
            prev_red = op("vector", lambda e, c=c, sl=sl: e.reduce_max(
                nrm[:, c:c + 1], xbuf[:, sl * D:(sl + 1) * D], axis=AX.X,
                apply_absolute_value=True), "ve")
            red4[c] = prev_red
        wt("tensor", "ve", prev_red)
        op("tensor", lambda e: e.transpose(PS[0:8, 3072:3200], nrm[:, 0:8], idf[:]), "pe")
        wt("vector", "pe", cnt["pe"])
        ve1 = op("vector", lambda e: e.reduce_max(mqk[0:8, 0:1], PS[0:8, 3072:3200], axis=AX.X), "ve")
        wt("tensor", "ve", ve1)
        op("tensor", lambda e: e.transpose(PS[0:1, 3200:3208], mqk[0:8, 0:1], idf[0:8, 0:8]), "pe")
        wt("vector", "pe", cnt["pe"])
        opw("vector", lambda e: e.reduce_max(scal[0:1, 0:1], PS[0:1, 3200:3208], axis=AX.X), "ve")
        opw("vector", lambda e: e.memset(scal[0:1, 1:2], 7.0), "ve")
        opw("vector", lambda e: e.reciprocal(scal[0:1, 2:3], scal[0:1, 1:2]), "ve")
        opw("vector", lambda e: e.reciprocal(scal[0:1, 3:4], scal[0:1, 0:1]), "ve")
        opw("vector", lambda e: e.tensor_scalar(scal[0:1, 4:5], scal[0:1, 3:4], 7.0, None, ALU.mult), "ve")
        opw("vector", lambda e: e.tensor_tensor(scal[0:1, 5:6], scal[0:1, 0:1], scal[0:1, 2:3], ALU.mult), "ve")
        ve_sc = opw("vector", lambda e: e.tensor_scalar(scal[0:1, 5:6], scal[0:1, 5:6], 0.5, None, ALU.mult), "ve")
        wt("tensor", "ve", ve_sc)
        op("tensor", lambda e: e.matmul(PS[:, 3584:3585], onesf[0:1, :], scal[0:1, 4:5],
                                        start=True, stop=True))
        pe_bc = op("tensor", lambda e: e.matmul(PS[:, 3585:3586], onesf[0:1, :], scal[0:1, 5:6],
                                                start=True, stop=True), "pe")
        wt("vector", "pe", pe_bc)
        ve_bc = op("vector", lambda e: e.tensor_copy(scal[:, 6:8], PS[:, 3584:3586]), "ve")
        wtl("tensor", [sw_al])
        for c in range(8):
            pe_aT = op("tensor", lambda e, c=c: e.transpose(
                PS[0:16, 1024 + c * 128:1024 + (c + 1) * 128],
                xbuf[:, 7 * D + c * 16:7 * D + (c + 1) * 16], idf[:]), "pe")
        wt("vector", "pe", pe_aT)
        ve_aT = op("vector", lambda e: e.tensor_copy(y1T[0:16, 0:1024], PS[0:16, 1024:2048]), "ve")

        prev_w = 0
        prev_w2 = {0: 0, 1: 0}
        for c in range(8):
            sl = c % 2
            par = c % 2
            eng = "vector" if par == 0 else "gpsimd"
            sem = "ve" if par == 0 else "po"
            s2 = 2 * D if par == 0 else 5 * D
            s3 = 3 * D if par == 0 else 6 * D
            gate = cnt["ve"] if c == 0 else (prev_red if c == 1 else prev_w2[sl])
            kp = PX.issue("sync", lambda e, c=c, sl=sl: e.dma_start(
                xbuf[:, sl * D:(sl + 1) * D], kern_d.ap()[c * 128:(c + 1) * 128, :]),
                [("ve", gate)])
            if c == 0:
                wt("tensor", "ve", ve_aT)
                wtl("tensor", [sw_bl])
            else:
                wt("tensor", "ve", prev_w)
            op("tensor", lambda e, c=c: e.matmul(
                PS[:, 2048:2560], y1T[0:16, c * 128:(c + 1) * 128],
                V[0:16, 0:512], start=True, stop=True))
            pe_ab = op("tensor", lambda e, c=c: e.matmul(
                PS[:, 2560:3072], y1T[0:16, c * 128:(c + 1) * 128],
                V[0:16, 512:1024], start=True, stop=True), "pe")
            wtl(eng, [kp])
            if c < 2:
                wt(eng, "ve", ve_bc)
            op(eng, lambda e, sl=sl: e.tensor_scalar(
                xbuf[:, sl * D:(sl + 1) * D], xbuf[:, sl * D:(sl + 1) * D],
                scal[:, 6:7], 8.5, ALU.mult, ALU.add), sem)
            op(eng, lambda e, sl=sl, s2=s2: e.tensor_copy(
                xbuf[:, s2:s2 + D], xbuf[:, sl * D:(sl + 1) * D]), sem)
            for bbit in (8.0, 4.0, 2.0, 1.0):
                op(eng, lambda e, b=bbit, s2=s2, s3=s3: e.tensor_scalar(
                    xbuf[:, s3:s3 + D], xbuf[:, s2:s2 + D], b, b, ALU.is_ge, ALU.mult), sem)
                op(eng, lambda e, s2=s2, s3=s3: e.tensor_tensor(
                    xbuf[:, s2:s2 + D], xbuf[:, s2:s2 + D], xbuf[:, s3:s3 + D],
                    ALU.subtract), sem)
            op(eng, lambda e, sl=sl, s2=s2: e.tensor_tensor(
                xbuf[:, sl * D:(sl + 1) * D], xbuf[:, sl * D:(sl + 1) * D],
                xbuf[:, s2:s2 + D], ALU.subtract), sem)
            op(eng, lambda e, sl=sl: e.tensor_scalar(
                xbuf[:, sl * D:(sl + 1) * D], xbuf[:, sl * D:(sl + 1) * D],
                15.0, 0.0, ALU.min, ALU.max), sem)
            chain_last = op(eng, lambda e, sl=sl: e.tensor_scalar(
                xbuf[:, sl * D:(sl + 1) * D], xbuf[:, sl * D:(sl + 1) * D],
                8.0, scal[:, 7:8], ALU.subtract, ALU.mult), sem)
            wt("vector", "pe", pe_ab)
            if par == 1:
                wt("vector", "po", chain_last)
            prev_w = op("vector", lambda e, c=c, sl=sl: e.scalar_tensor_tensor(
                weff[:, c * D:(c + 1) * D], PS[:, 2048:3072], 0.5,
                xbuf[:, sl * D:(sl + 1) * D], ALU.mult, ALU.add), "ve")
            prev_w2[sl] = prev_w
        weff_done = prev_w

        if stage == "setup":
            wt("sync", "ve", weff_done)
            PSTO.issue("sync", lambda e: e.dma_start(dbgb_d.ap()[:, 0:8192], weff[:]))
            PSTO.issue("sync", lambda e: e.dma_start(dbgf_d.ap()[:, 0:16], bqk8[:]))
            PSTO.issue("sync", lambda e: e.dma_start(dbgf_d.ap()[0:1, 16:24], scal[0:1, :]))
            wtl("sync", PSTO.barrier())

        store_ms = {}
        xload_ms = {}
        ystage_guard = [0] * 4
        y1T_guard = [0] * 8
        y2T_guard = 0
        QT_guard = 0
        attoT_guard = 0
        KTV_guard = 0
        bank_guard = [("ve", weff_done)] * 8
        probs_guard = [0, 0]
        av_guard = ("ve", 0)
        recip_guard = 0
        q2_guard = [0, 0]
        cbias_guard = 0
        nrm_guard = 0
        wstage_pe = [0, 0]

        def bank_wait(eng, banks):
            best = {}
            for b in banks:
                smn, v = bank_guard[b]
                if smn is not None and v > best.get(smn, 0):
                    best[smn] = v
            for smn, v in best.items():
                wt(eng, smn, v)

        def bank_set(banks, sem, val):
            for b in banks:
                bank_guard[b] = (sem, val)

        if stage != "setup":
            for gt in range(min(NSLOT, n_seq * NT)):
                xload_ms[gt] = PX.issue("sync", lambda e, gt=gt: e.dma_start(
                    xbuf[:, (gt % NSLOT) * D:((gt % NSLOT) + 1) * D],
                    xrows[gt * 128:(gt + 1) * 128, :]), [("ve", weff_done)])

        def emit_ln_chain(li, need_sum, xs, xguard):
            sl4 = li % 4
            for (gnm, gv) in xguard:
                wt("scalar", gnm, gv)
            wt("scalar", "ac", cnt["ac"])
            if need_sum:
                opw("scalar", lambda e: e.activation(
                    sqd[:, sl4 * D:(sl4 + 1) * D], xs(), AF.Identity,
                    accum_out=lns[:, li * 8:li * 8 + 1]), "ac")
            ac_sq = opw("scalar", lambda e: e.activation(
                sqd[:, sl4 * D:(sl4 + 1) * D], xs(), AF.Square,
                accum_out=lns[:, li * 8 + 1:li * 8 + 2]), "ac")
            wt("gpsimd", "ac", ac_sq)
            opw("gpsimd", lambda e: e.tensor_scalar(
                lns[:, li * 8 + 2:li * 8 + 3], lns[:, li * 8:li * 8 + 1],
                -1.0 / D, None, ALU.mult), "po")
            opw("gpsimd", lambda e: e.tensor_tensor(
                lns[:, li * 8 + 3:li * 8 + 4], lns[:, li * 8 + 2:li * 8 + 3],
                lns[:, li * 8 + 2:li * 8 + 3], ALU.mult), "po")
            opw("gpsimd", lambda e: e.tensor_scalar(
                lns[:, li * 8 + 4:li * 8 + 5], lns[:, li * 8 + 1:li * 8 + 2],
                1.0 / D, EPS, ALU.mult, ALU.add), "po")
            po_v = opw("gpsimd", lambda e: e.tensor_tensor(
                lns[:, li * 8 + 4:li * 8 + 5], lns[:, li * 8 + 4:li * 8 + 5],
                lns[:, li * 8 + 3:li * 8 + 4], ALU.subtract), "po")
            wt("scalar", "po", po_v)
            opw("scalar", lambda e: e.activation(
                lns[:, li * 8 + 5:li * 8 + 6], lns[:, li * 8 + 4:li * 8 + 5],
                AF.Ln), "ac")
            ac_r = opw("scalar", lambda e: e.activation(
                lns[:, li * 8 + 6:li * 8 + 7], lns[:, li * 8 + 5:li * 8 + 6],
                AF.Exp, scale=-0.5), "ac")
            wt("gpsimd", "ac", ac_r)
            opw("gpsimd", lambda e: e.tensor_tensor(
                lns[:, li * 8 + 7:li * 8 + 8], lns[:, li * 8 + 2:li * 8 + 3],
                lns[:, li * 8 + 6:li * 8 + 7], ALU.mult), "po")

        def emit_cast_and_transpose(li, xs, dest_ap, dest_pe_guard, pair):
            sl4 = li % 4
            if ystage_guard[sl4]:
                wt("gpsimd", "pe", ystage_guard[sl4])
            po_y = op("gpsimd", lambda e: e.tensor_scalar(
                ystage[:, sl4 * D:(sl4 + 1) * D], xs(), lns[:, li * 8 + 6:li * 8 + 7],
                lns[:, li * 8 + 7:li * 8 + 8], ALU.mult, ALU.add), "po")
            pcol = pair * 1024
            bank_wait("tensor", [2 * pair, 2 * pair + 1])
            wt("tensor", "po", po_y)
            for c in range(8):
                ipe = op("tensor", lambda e, c=c, sl4=sl4, pcol=pcol: e.matmul(
                    PS[:, pcol + c * 128:pcol + (c + 1) * 128],
                    ystage[:, sl4 * D + c * 128:sl4 * D + (c + 1) * 128],
                    identb[:, :], start=True, stop=True),
                    "pe" if c == 7 else None)
            ystage_guard[sl4] = ipe
            wt("vector", "pe", ipe)
            if dest_pe_guard:
                wt("vector", "pe", dest_pe_guard)
            ve_evac = op("vector", lambda e, pcol=pcol: e.tensor_copy(
                dest_ap(), PS[:, pcol:pcol + 1024]), "ve")
            bank_set([2 * pair, 2 * pair + 1], "ve", ve_evac)
            return ve_evac

        def emit_ffn(li, bli, xs, accum_sum):
            banks = (bli * 2, bli * 2 + 1)
            pcol = banks[0] * 512
            bank_wait("tensor", banks)
            for c in range(8):
                for hh in range(2):
                    op("tensor", lambda e, c=c, hh=hh: e.matmul(
                        PS[:, pcol + hh * 512: pcol + hh * 512 + 512],
                        y1T[:, li * D + c * 128:li * D + (c + 1) * 128],
                        weff[:, c * D + hh * 512:c * D + hh * 512 + 512],
                        start=(c == 0), stop=False))
            op("tensor", lambda e: e.matmul(
                PS[:, pcol:pcol + 512], onesbr[0:1, :], beff[0:1, 0:512],
                start=False, stop=True))
            pe_f = op("tensor", lambda e: e.matmul(
                PS[:, pcol + 512:pcol + 1024], onesbr[0:1, :], beff[0:1, 512:1024],
                start=False, stop=True), "pe")
            y1T_guard[li] = pe_f
            wt("vector", "pe", pe_f)
            if accum_sum:
                ve_ep = op("vector", lambda e: e.scalar_tensor_tensor(
                    xs(), PS[:, pcol:pcol + 1024], 1.0, xs(), ALU.mult, ALU.add,
                    accum_out=lns[:, li * 8:li * 8 + 1]), "ve")
            else:
                ve_ep = op("vector", lambda e: e.scalar_tensor_tensor(
                    xs(), PS[:, pcol:pcol + 1024], 1.0, xs(), ALU.mult, ALU.add), "ve")
            bank_set(banks, "ve", ve_ep)
            return ve_ep

        units = [(sq_, qb_) for sq_ in range(n_seq) for qb_ in range(n_qb)]
        KTV_prev_seq = 0

        def emit_S1(seq, qb, tl, tinfo):
            gt = seq * NT + qb * QB + tl
            sl = gt % NSLOT
            li = tl
            xs = (lambda sl=sl: xbuf[:, sl * D:(sl + 1) * D])
            emit_ln_chain(li, True, xs, [xload_ms[gt]])
            tp1 = emit_cast_and_transpose(
                li, xs, lambda li=li: y1T[:, li * D:li * D + 1024], y1T_guard[li],
                tl % 2)
            tinfo[tl] = (gt, sl, li, xs, tp1)

        def emit_S2a(seq, qb, tl, tinfo, bli):
            gt, sl, li, xs, tp1 = tinfo[tl]
            wt("tensor", "ve", tp1)
            epi = emit_ffn(li, bli, xs, True)
            emit_ln_chain(li, False, xs, [("ve", epi)])

        def emit_TPS2(seq, qb, tl, tinfo):
            gt, sl, li, xs, tp1 = tinfo[tl]
            y2ev[0] = emit_cast_and_transpose(
                li, xs, lambda tl=tl: bass.AP(y2T, tl * 128,
                                              [[4096, 128], [512, 8], [1, 128]]),
                y2T_guard if tl == 0 else 0, 2 + tl % 2)

        def emit_P1mm(seq, qb, tl, pinfo, first):
            gt = seq * NT + qb * QB + tl
            sl = gt % NSLOT
            li = 4 + tl
            xs = (lambda sl=sl: xbuf[:, sl * D:(sl + 1) * D])
            ob = 2048 if (tl % 2) else 3072
            bank_wait("tensor", [ob // 512, ob // 512 + 1])
            if first:
                wt("tensor", "ve", ve_att_prev[0])
            for j in range(8):
                for hh in range(2):
                    op("tensor", lambda e, j=j, hh=hh, tl=tl, ob=ob: e.matmul(
                        PS[:, ob + hh * 512:ob + hh * 512 + 512],
                        attoT[:, j * 512 + tl * 128:j * 512 + (tl + 1) * 128],
                        wout_s[:, j * D + hh * 512:j * D + hh * 512 + 512],
                        start=(j == 0), stop=False))
            op("tensor", lambda e, ob=ob: e.matmul(
                PS[:, ob:ob + 512], onesbr[0:1, :], bout[0:1, 0:512],
                start=False, stop=True))
            pe_op = op("tensor", lambda e, ob=ob: e.matmul(
                PS[:, ob + 512:ob + 1024], onesbr[0:1, :], bout[0:1, 512:1024],
                start=False, stop=True), "pe")
            pinfo[tl] = [gt, sl, li, xs, pe_op, ob, 0, 0]

        def emit_P1epi(tl, pinfo):
            gt, sl, li, xs, pe_op, ob, _, _ = pinfo[tl]
            wt("vector", "pe", pe_op)
            ve_opep = op("vector", lambda e, xs=xs, li=li, ob=ob: e.scalar_tensor_tensor(
                xs(), PS[:, ob:ob + 1024], 1.0, xs(), ALU.mult, ALU.add,
                accum_out=lns[:, li * 8:li * 8 + 1]), "ve")
            bank_set([ob // 512, ob // 512 + 1], "ve", ve_opep)
            pinfo[tl][6] = ve_opep

        def emit_P2a(tl, pinfo):
            gt, sl, li, xs, pe_op, ob, ve_opep, _ = pinfo[tl]
            emit_ln_chain(li, False, xs, [("ve", ve_opep)])
            tp3 = emit_cast_and_transpose(
                li, xs, lambda li=li: y1T[:, li * D:li * D + 1024], y1T_guard[li],
                2 + tl % 2)
            pinfo[tl][7] = tp3

        def emit_P2b(tl, pinfo, bli):
            gt, sl, li, xs, pe_op, ob, ve_opep, tp3 = pinfo[tl]
            wt("tensor", "ve", tp3)
            ve_f2 = emit_ffn(li, bli, xs, False)
            store_ms[gt] = PSTO.issue("sync", lambda e, gt=gt, sl=sl: e.dma_start(
                orows[gt * 128:(gt + 1) * 128, :],
                xbuf[:, sl * D:(sl + 1) * D]), [("ve", ve_f2)])
            ngt2 = gt + NSLOT
            if ngt2 < n_seq * NT:
                xload_ms[ngt2] = PX.issue("sync", lambda e, ngt2=ngt2, sl=sl: e.dma_start(
                    xbuf[:, sl * D:(sl + 1) * D],
                    xrows[ngt2 * 128:(ngt2 + 1) * 128, :]), [store_ms[gt]])

        ve_att_prev = [0]
        y2ev = [0]
        post_pend = None

        for ui in range(len(units) + 1):
            if stage == "setup":
                break
            unit = units[ui] if ui < len(units) else None
            tinfo = {}
            pinfo = {}
            if unit is not None:
                seq, qb = unit
                if qb == 0:
                    op("vector", lambda e: e.memset(mqk[:, 2:3], 0.0), "ve")
            if post_pend is not None:
                pseq, pqb = post_pend

            swq = {}
            if unit is not None:
                for c in range(2):
                    extra = [("pe", wstage_pe[c])]
                    if ui == 0:
                        extra = wscr_done + extra
                    swq[c] = PW.issue("gpsimd", lambda e, c=c: e.dma_start(
                        wstage[:, c * D:(c + 1) * D],
                        wscr_d.ap()[c * 128:(c + 1) * 128, 0:D]), extra)

            def S1(tl):
                emit_S1(seq, qb, tl, tinfo)

            def S2a(tl, bli=0):
                emit_S2a(seq, qb, tl, tinfo, bli)

            def TPS2(tl):
                emit_TPS2(seq, qb, tl, tinfo)

            def P1(tl):
                emit_P1mm(pseq, pqb, tl, pinfo, tl == 0)

            def P1e(tl):
                emit_P1epi(tl, pinfo)

            def P2a(tl):
                emit_P2a(tl, pinfo)

            def P2b(tl, bli=1):
                emit_P2b(tl, pinfo, bli)

            if unit is not None and post_pend is not None:
                P1(0); S1(0); P1e(0)
                P1(1); S1(1); P1e(1); P2a(0)
                P1(2); S1(2); P1e(2); P2a(1)
                P1(3); S1(3); P1e(3); P2a(2)
                S2a(0, 0); S2a(1, 1); P2a(3); P2b(0, 0); TPS2(0)
                S2a(2, 1); P2b(1, 0); TPS2(1)
                S2a(3, 1); P2b(2, 0); TPS2(2)
                P2b(3, 1); TPS2(3)
            elif unit is not None:
                S1(0); S1(1); S2a(0, 0); S1(2); S2a(1, 1); TPS2(0)
                S1(3); S2a(2, 0); TPS2(1); S2a(3, 1); TPS2(2); TPS2(3)
            else:
                P1(0); P1e(0); P1(1); P1e(1); P2a(0)
                P1(2); P1e(2); P2a(1); P2b(0, 1)
                P1(3); P1e(3); P2a(2); P2b(1, 0)
                P2a(3); P2b(2, 1); P2b(3, 0)
            if post_pend is not None:
                attoT_guard = pinfo[QB - 1][4]
            ac_sb = cnt["ac"]
            if unit is None:
                break

            for c in range(8):
                par = c % 2
                swc = swq[c] if c < 2 else PW.issue(
                    "gpsimd", lambda e, c=c: e.dma_start(
                        wstage[:, (c % 2) * D:((c % 2) + 1) * D],
                        wscr_d.ap()[c * 128:(c + 1) * 128, 0:D]),
                    [("pe", wstage_pe[par])])
                wtl("tensor", [swc])
                if c == 0:
                    wt("tensor", "ve", y2ev[0])
                    bank_wait("tensor", list(range(8)))
                for m in range(8):
                    ipe = op("tensor", lambda e, c=c, m=m, par=par: e.matmul(
                        PS[:, m * 512:(m + 1) * 512],
                        wstage[:, par * D + m * 128:par * D + (m + 1) * 128],
                        y2T[:, c * 512:(c + 1) * 512],
                        start=(c == 0), stop=(c == 7)), "pe" if m == 7 else None)
                wstage_pe[par] = ipe
            pe_qwave = cnt["pe"]
            swk = {}
            for c in range(2):
                swk[c] = PW.issue("gpsimd", lambda e, c=c: e.dma_start(
                    wstage[:, c * D:(c + 1) * D],
                    wscr_d.ap()[c * 128:(c + 1) * 128, D:2 * D]),
                    [("pe", wstage_pe[c])])
            if QT_guard:
                wt("vector", "pe", QT_guard)
            wt("vector", "pe", pe_qwave)
            for m in range(8):
                qev = op("vector", lambda e, m=m: e.tensor_scalar(
                    QT[:, m * 512:(m + 1) * 512], PS[:, m * 512:(m + 1) * 512],
                    0.125, bqk8[:, m:m + 1], ALU.mult, ALU.add), "ve")
                bank_guard[m] = ("ve", qev)
            ve_qevac = cnt["ve"]
            if q2_guard[0]:
                wt("vector", "pe", q2_guard[0])
            wt("vector", "ac", ac_sb)
            po_q2 = op("vector", lambda e: e.tensor_tensor(
                sqd[:, 0:4096], QT[:, 0:4096], QT[:, 0:4096], ALU.mult), "ve")

            for c in range(8):
                par = c % 2
                swc = swk[c] if c < 2 else PW.issue(
                    "gpsimd", lambda e, c=c: e.dma_start(
                        wstage[:, (c % 2) * D:((c % 2) + 1) * D],
                        wscr_d.ap()[c * 128:(c + 1) * 128, D:2 * D]),
                    [("pe", wstage_pe[par])])
                wtl("tensor", [swc])
                for m in range(8):
                    if c == 0:
                        bank_wait("tensor", [m])
                    ipe = op("tensor", lambda e, c=c, m=m, par=par: e.matmul(
                        PS[:, m * 512:(m + 1) * 512],
                        wstage[:, par * D + m * 128:par * D + (m + 1) * 128],
                        y2T[:, c * 512:(c + 1) * 512],
                        start=(c == 0), stop=(c == 7)), "pe" if m == 7 else None)
                wstage_pe[par] = ipe
            pe_kwave = cnt["pe"]
            swv = {}
            for c in range(2):
                swv[c] = PW.issue("gpsimd", lambda e, c=c: e.dma_start(
                    wstage[:, c * D:(c + 1) * D],
                    wscr_d.ap()[c * 128:(c + 1) * 128, 2 * D:3 * D]),
                    [("pe", wstage_pe[c])])
            if seq == 1 and KTV_prev_seq:
                wt("vector", "pe", KTV_prev_seq)
            wt("vector", "pe", pe_kwave)
            for m in range(8):
                kev = op("vector", lambda e, m=m, qb=qb: e.tensor_scalar(
                    KT[:, m * S + qb * 512:m * S + (qb + 1) * 512],
                    PS[:, m * 512:(m + 1) * 512],
                    bqk8[:, 8 + m:9 + m], None, ALU.add), "ve")
                bank_guard[m] = ("ve", kev)
            ve_kevac = cnt["ve"]
            po_k2 = op("vector", lambda e, qb=qb: e.tensor_tensor(
                ystage[:, 0:4096],
                bass.AP(KT, qb * 512, [[8 * S, 128], [S, 8], [1, 512]]),
                bass.AP(KT, qb * 512, [[8 * S, 128], [S, 8], [1, 512]]), ALU.mult), "ve")

            for c in range(8):
                par = c % 2
                swc = swv[c] if c < 2 else PW.issue(
                    "gpsimd", lambda e, c=c: e.dma_start(
                        wstage[:, (c % 2) * D:((c % 2) + 1) * D],
                        wscr_d.ap()[c * 128:(c + 1) * 128, 2 * D:3 * D]),
                    [("pe", wstage_pe[par])])
                wtl("tensor", [swc])
                for tl in range(QB):
                    for hh in range(2):
                        if c == 0:
                            bank_wait("tensor", [2 * tl + hh])
                        ipe = op("tensor", lambda e, c=c, tl=tl, hh=hh, par=par: e.matmul(
                            PS[:, tl * 1024 + hh * 512:tl * 1024 + hh * 512 + 512],
                            y2T[:, c * 512 + tl * 128:c * 512 + (tl + 1) * 128],
                            wstage[:, par * D + hh * 512:par * D + hh * 512 + 512],
                            start=(c == 0), stop=False),
                            "pe" if (tl == QB - 1 and hh == 1) else None)
                wstage_pe[par] = ipe
            for tl in range(QB):
                for hh in range(2):
                    ipe = op("tensor", lambda e, tl=tl, hh=hh: e.matmul(
                        PS[:, tl * 1024 + hh * 512:tl * 1024 + hh * 512 + 512],
                        onesbr[0:1, :], bqkv[0:1, hh * 512:hh * 512 + 512],
                        start=False, stop=True), "pe" if (tl == QB - 1 and hh == 1) else None)
            pe_vwave = ipe
            y2T_guard = pe_vwave
            if seq == 1 and KTV_prev_seq:
                wt("vector", "pe", KTV_prev_seq)
            wt("vector", "pe", pe_vwave)
            for tl in range(QB):
                t = qb * QB + tl
                vev = op("vector", lambda e, tl=tl, t=t: e.tensor_copy(
                    V[:, t * D:(t + 1) * D], PS[:, tl * 1024:(tl + 1) * 1024]), "ve")
                bank_guard[2 * tl] = ("ve", vev)
                bank_guard[2 * tl + 1] = ("ve", vev)
            ve_vevac = cnt["ve"]

            wt("tensor", "ve", po_q2)
            bank_wait("tensor", [4, 5])
            for m in range(8):
                for tc in range(4):
                    for r in range(2):
                        ipe = op("tensor", lambda e, m=m, tc=tc, r=r: e.matmul(
                            PS[:, 2048 + tc * 16 + 2 * m + r: 2048 + tc * 16 + 2 * m + r + 1],
                            sqd[r * 64:r * 64 + 64,
                                m * 512 + tc * 128:m * 512 + (tc + 1) * 128],
                            onesb[r * 64:r * 64 + 64, 0:1],
                            start=True, stop=True, tile_position=(r * 64, 0)),
                            "pe" if (m == 7 and tc == 3 and r == 1) else None)
            q2_guard[0] = ipe
            wt("vector", "pe", ipe)
            ve_nrm = op("vector", lambda e: e.tensor_copy(nrm[:, 0:64], PS[:, 2048:2112]), "ve")
            wt("tensor", "ve", ve_nrm)
            for tc in range(4):
                ipe = op("tensor", lambda e, tc=tc: e.transpose(
                    PS[0:16, 2560 + tc * 128:2560 + (tc + 1) * 128],
                    nrm[:, tc * 16:(tc + 1) * 16], idf[:]), "pe" if tc == 3 else None)
            wt("vector", "pe", ipe)
            ve_mq = opw("vector", lambda e: e.reduce_max(
                mqk[:, 0:1], PS[0:16, 2560:3072], axis=AX.X), "ve")
            wt("tensor", "ve", po_k2)
            wt("tensor", "ve", ve_nrm)
            for m in range(8):
                for tc in range(4):
                    for r in range(2):
                        ipe = op("tensor", lambda e, m=m, tc=tc, r=r: e.matmul(
                            PS[:, 2048 + tc * 16 + 2 * m + r: 2048 + tc * 16 + 2 * m + r + 1],
                            ystage[r * 64:r * 64 + 64,
                                   m * 512 + tc * 128:m * 512 + (tc + 1) * 128],
                            onesb[r * 64:r * 64 + 64, 0:1],
                            start=True, stop=True, tile_position=(r * 64, 0)),
                            "pe" if (m == 7 and tc == 3 and r == 1) else None)
            knorm_pe = ipe
            wt("vector", "pe", knorm_pe)
            ve_nrm2 = op("vector", lambda e: e.tensor_copy(nrm[:, 0:64], PS[:, 2048:2112]), "ve")
            wt("tensor", "ve", ve_nrm2)
            for tc in range(4):
                ipe = op("tensor", lambda e, tc=tc: e.transpose(
                    PS[0:16, 2560 + tc * 128:2560 + (tc + 1) * 128],
                    nrm[:, tc * 16:(tc + 1) * 16], idf[:]), "pe" if tc == 3 else None)
            wt("vector", "pe", ipe)
            opw("vector", lambda e: e.reduce_max(mqk[:, 1:2], PS[0:16, 2560:3072], axis=AX.X), "ve")
            ve_mk = opw("vector", lambda e: e.tensor_tensor(
                mqk[:, 2:3], mqk[:, 2:3], mqk[:, 1:2], ALU.max), "ve")

            wt("vector", "ve", ve_mk)
            opw("vector", lambda e: e.tensor_tensor(
                mqk[:, 3:4], mqk[:, 0:1], mqk[:, 2:3], ALU.mult), "ve")
            ve_prod = opw("vector", lambda e: e.tensor_scalar(
                mqk[:, 3:4], mqk[:, 3:4], 1e-30, None, ALU.add), "ve")
            wt("scalar", "ve", ve_prod)
            opw("scalar", lambda e: e.activation(mqk[:, 4:5], mqk[:, 3:4], AF.Ln), "ac")
            ac_c = opw("scalar", lambda e: e.activation(mqk[:, 5:6], mqk[:, 4:5], AF.Exp,
                                                        scale=0.5), "ac")
            wt("tensor", "ac", ac_c)
            pe_crow = op("tensor", lambda e: e.transpose(PS[0:1, 2576:2592], mqk[0:16, 5:6],
                                                         idf[0:16, 0:16]), "pe")
            wt("vector", "pe", pe_crow)
            opw("vector", lambda e: e.tensor_copy(crow[0:1, 0:16], PS[0:1, 2576:2592]), "ve")
            opw("vector", lambda e: e.tensor_tensor(
                crow[0:1, 16:24],
                bass.AP(crow, 0, [[32, 1], [2, 8]]),
                bass.AP(crow, 1, [[32, 1], [2, 8]]), ALU.max), "ve")
            ve_crow = opw("vector", lambda e: e.tensor_scalar(
                crow[0:1, 16:24], crow[0:1, 16:24], -1.0, None, ALU.mult), "ve")
            wt("tensor", "ve", ve_crow)
            pe_cb = op("tensor", lambda e: e.matmul(PS[:, 2048:2056], onesf[0:1, :],
                                                    crow[0:1, 16:24],
                                                    start=True, stop=True), "pe")
            wt("vector", "pe", pe_cb)
            if cbias_guard:
                wt("vector", "ac", cbias_guard)
            ve_cb = op("vector", lambda e: e.tensor_copy(cbias[:], PS[:, 2048:2056]), "ve")
            bank_set([4, 5], "ve", ve_cb)

            nch = 4 * (qb + 1)
            wt("scalar", "ve", ve_cb)
            gw = 0
            for jp in range(4):
                prs = (2 * jp, 2 * jp + 1)
                avb = {prs[0]: 2048, prs[1]: 3072}
                pend = None

                def emit_av(p):
                    j, w, par, off, last, ab, ac_exp, ve_mask = p
                    if w == 0:
                        bank_wait("tensor", [ab // 512, ab // 512 + 1])
                    wt("tensor", "ac", ac_exp)
                    if ve_mask:
                        wt("tensor", "ve", ve_mask)
                    for r in range(2):
                        hidx = 2 * j + r
                        op("tensor", lambda e, w=w, hidx=hidx, r=r, par=par, off=off,
                           last=last, ab=ab, qb=qb: e.matmul(
                            PS[0:64, ab + r * 512 + off:ab + r * 512 + 512],
                            V[:, w * D + hidx * 64:w * D + (hidx + 1) * 64],
                            probsT[:, par * 1024 + r * 512 + off:par * 1024 + (r + 1) * 512],
                            start=(w == 0), stop=last, tile_position=(0, 0)))
                        ipe = op("tensor", lambda e, w=w, r=r, par=par, off=off,
                                 last=last, ab=ab: e.matmul(
                            PS[64:128, ab + r * 512 + off:ab + r * 512 + 512],
                            onesb[:, :],
                            probsT[:, par * 1024 + r * 512 + off:par * 1024 + (r + 1) * 512],
                            start=(w == 0), stop=last, tile_position=(0, 64)), "pe" if r == 1 else None)
                    probs_guard[par] = ipe

                for w in range(nch):
                    for j in prs:
                        kc = w
                        par = gw % 2
                        gw += 1
                        lc = kc * 128 - qb * 512
                        diag = lc >= 0
                        last = (w == nch - 1)
                        ab = avb[j]
                        bank_wait("tensor", [par * 2, par * 2 + 1])
                        op("tensor", lambda e, j=j, kc=kc, par=par: e.matmul(
                            PS[:, par * 1024:par * 1024 + 512],
                            KT[0:64, j * S + kc * 128:j * S + (kc + 1) * 128],
                            QT[0:64, j * 512:(j + 1) * 512],
                            start=True, stop=True, tile_position=(0, 0)))
                        pe_sc = op("tensor", lambda e, j=j, kc=kc, par=par: e.matmul(
                            PS[:, par * 1024 + 512:par * 1024 + 1024],
                            KT[64:128, j * S + kc * 128:j * S + (kc + 1) * 128],
                            QT[64:128, j * 512:(j + 1) * 512],
                            start=True, stop=True, tile_position=(64, 0)), "pe")
                        wt("scalar", "pe", pe_sc)
                        if probs_guard[par]:
                            wt("scalar", "pe", probs_guard[par])
                        ac_exp = op("scalar", lambda e, j=j, par=par: e.activation(
                            probsT[:, par * 1024:(par + 1) * 1024],
                            PS[:, par * 1024:(par + 1) * 1024], AF.Exp,
                            bias=cbias[:, j:j + 1]), "ac")
                        bank_set([par * 2, par * 2 + 1], "ac", ac_exp)
                        ve_mask = 0
                        if diag:
                            wt("vector", "ac", ac_exp)
                            for r in range(2):
                                op("vector", lambda e, r=r, par=par, lc=lc: e.tensor_tensor(
                                    probsT[:, par * 1024 + r * 512 + lc:
                                           par * 1024 + r * 512 + lc + 128],
                                    probsT[:, par * 1024 + r * 512 + lc:
                                           par * 1024 + r * 512 + lc + 128],
                                    tri[:], ALU.mult), "ve")
                            ve_mask = cnt["ve"]
                        if pend is not None:
                            emit_av(pend)
                        pend = (j, w, par, max(lc, 0), last, ab, ac_exp, ve_mask)
                emit_av(pend)
                pe_av = cnt["pe"]
                for j in prs:
                    ab = avb[j]
                    wt("scalar", "pe", pe_av)
                    if recip_guard:
                        wt("scalar", "ve", recip_guard)
                    op("scalar", lambda e, ab=ab: e.activation(
                        recip[:], PS[64:128, ab:ab + 1024], AF.Ln), "ac")
                    ac_rec = op("scalar", lambda e: e.activation(recip[:], recip[:], AF.Exp,
                                                                 scale=-1.0), "ac")
                    wt("vector", "ac", ac_rec)
                    if attoT_guard:
                        wt("vector", "pe", attoT_guard)
                    op("vector", lambda e, j=j, ab=ab: e.tensor_tensor(
                        attoT[0:64, j * 512:(j + 1) * 512], PS[0:64, ab:ab + 512],
                        recip[0:64, 0:512], ALU.mult), "ve")
                    ve_resc = op("vector", lambda e, j=j, ab=ab: e.tensor_tensor(
                        attoT[64:128, j * 512:(j + 1) * 512], PS[0:64, ab + 512:ab + 1024],
                        recip[0:64, 512:1024], ALU.mult), "ve")
                    recip_guard = ve_resc
                    bank_set([ab // 512, ab // 512 + 1], "ve", ve_resc)
            ve_att_prev[0] = cnt["ve"]
            cbias_guard = cnt["ac"]
            QT_guard = cnt["pe"]
            if qb == n_qb - 1:
                KTV_prev_seq = cnt["pe"]
            post_pend = unit

        wtl("sync", PSTO.barrier())

        nc._sched_q = {k: list(v) for k, v in q.items()}
        nc._sched_cnt = dict(cnt)

        def replay(name, eng):
            for item in q[name]:
                if item[0] == "w":
                    eng.wait_ge(sems[item[1]], item[2])
                else:
                    _, fn, smn, amt = item
                    ins = fn(eng)
                    if smn is not None:
                        ins.then_inc(sems[smn], amt)

        @block.sync
        def _(e):
            replay("sync", e)

        @block.tensor
        def _(e):
            replay("tensor", e)

        @block.vector
        def _(e):
            replay("vector", e)

        @block.scalar
        def _(e):
            replay("scalar", e)

        @block.gpsimd
        def _(e):
            replay("gpsimd", e)

    return nc


_NC_CACHE = None


def _get_nc():
    global _NC_CACHE
    if _NC_CACHE is None:
        _NC_CACHE = build_program()
    return _NC_CACHE


def kernel(L0_attn_x, L0_attn_w_qkv, L0_attn_b_qkv, L0_attn_w_out, L0_attn_b_out,
           L0_ffn_kernel, L0_ffn_bias, L0_ffn_a, L0_ffn_b):
    x = np.ascontiguousarray(np.asarray(L0_attn_x, dtype=np.float32))
    wqkv = np.asarray(L0_attn_w_qkv, dtype=np.float32)
    bqkv = np.asarray(L0_attn_b_qkv, dtype=np.float32)
    wout = np.ascontiguousarray(np.asarray(L0_attn_w_out, dtype=np.float32))
    bout = np.ascontiguousarray(np.asarray(L0_attn_b_out, dtype=np.float32).reshape(1, D))
    kern = np.ascontiguousarray(np.asarray(L0_ffn_kernel, dtype=np.float32))
    fbias = np.ascontiguousarray(np.asarray(L0_ffn_bias, dtype=np.float32).reshape(1, D))
    a = np.asarray(L0_ffn_a, dtype=np.float32)
    b = np.ascontiguousarray(np.asarray(L0_ffn_b, dtype=np.float32))

    wq3 = wqkv.reshape(D, H, 3, 64)
    wqkv_r = np.ascontiguousarray(np.concatenate(
        [wq3[:, :, 0, :].reshape(D, D), wq3[:, :, 1, :].reshape(D, D),
         wq3[:, :, 2, :].reshape(D, D)], axis=1))
    bq3 = bqkv.reshape(H, 3, 64)
    bqkv_r = np.ascontiguousarray(np.concatenate(
        [bq3[:, 0, :].reshape(-1), bq3[:, 1, :].reshape(-1),
         bq3[:, 2, :].reshape(-1)])).reshape(1, 3 * D)
    a_l = np.ascontiguousarray(a.reshape(8, 128, 16).transpose(1, 0, 2).reshape(128, 128))

    identf = np.eye(128, dtype=np.float32)
    ones_f = np.ones((1, 128), dtype=np.float32)
    ones_bfr = np.ones((1, 128), dtype=ml_dtypes.bfloat16)
    ones_bf = np.ones((128, 64), dtype=ml_dtypes.bfloat16)
    kk, qq = np.meshgrid(np.arange(128), np.arange(128), indexing="ij")
    tri_m = np.ascontiguousarray((qq >= kk).astype(ml_dtypes.bfloat16))

    nc = _get_nc()
    in_maps = []
    for cidx in range(NCORES):
        in_maps.append({
            "x": np.ascontiguousarray(x[2 * cidx:2 * cidx + 2].reshape(NSEQ * S, D)),
            "wqkv_r": wqkv_r, "bqkv_r": bqkv_r, "wout": wout, "bout": bout,
            "kern": kern, "fbias": fbias, "a_l": a_l, "b_l": b,
            "identf": identf, "ones_f": ones_f, "ones_bfr": ones_bfr,
            "ones_bf": ones_bf, "tri": tri_m,
        })
    res = run_bass_kernel_spmd(nc, in_maps, core_ids=list(range(NCORES)))
    out = np.concatenate([r["out"].reshape(NSEQ, S, D) for r in res.results], axis=0)
    return out


# version 17
# speedup vs baseline: 1.0090x; 1.0090x over previous
import numpy as np
import ml_dtypes
from contextlib import ExitStack

import concourse.bass as bass
import concourse.mybir as mybir
from concourse.bass_utils import run_bass_kernel_spmd

F32 = mybir.dt.float32
BF16 = mybir.dt.bfloat16
AF = mybir.ActivationFunctionType
ALU = mybir.AluOpType
AX = mybir.AxisListType

NCORES = 8
NSEQ = 2
S = 2048
D = 1024
H = 16
NT = 16
QB = 4
NQB = 4
NSLOT = 8
EPS = 1e-5


def build_program(n_seq=NSEQ, n_qb=NQB, stage="full", races=True):
    nc = bass.Bass("TRN2", target_bir_lowering=False, detect_race_conditions=races)

    x_d = nc.dram_tensor("x", [NSEQ * S, D], F32, kind="ExternalInput")
    wqkv_d = nc.dram_tensor("wqkv_r", [D, 3 * D], F32, kind="ExternalInput")
    bqkv_d = nc.dram_tensor("bqkv_r", [1, 3 * D], F32, kind="ExternalInput")
    wout_d = nc.dram_tensor("wout", [D, D], F32, kind="ExternalInput")
    bout_d = nc.dram_tensor("bout", [1, D], F32, kind="ExternalInput")
    kern_d = nc.dram_tensor("kern", [D, D], F32, kind="ExternalInput")
    fbias_d = nc.dram_tensor("fbias", [1, D], F32, kind="ExternalInput")
    al_d = nc.dram_tensor("a_l", [128, 128], F32, kind="ExternalInput")
    bl_d = nc.dram_tensor("b_l", [16, D], F32, kind="ExternalInput")
    idf_d = nc.dram_tensor("identf", [128, 128], F32, kind="ExternalInput")
    onesf_d = nc.dram_tensor("ones_f", [1, 128], F32, kind="ExternalInput")
    onesbr_d = nc.dram_tensor("ones_bfr", [1, 128], BF16, kind="ExternalInput")
    onesb_d = nc.dram_tensor("ones_bf", [128, 64], BF16, kind="ExternalInput")
    tri_d = nc.dram_tensor("tri", [128, 128], BF16, kind="ExternalInput")
    out_d = nc.dram_tensor("out", [NSEQ * S, D], F32, kind="ExternalOutput")
    if stage != "full":
        dbgb_d = nc.dram_tensor("dbgb", [128, 16384], BF16, kind="ExternalOutput")
        dbgf_d = nc.dram_tensor("dbgf", [128, 1024], F32, kind="ExternalOutput")

    ctx = ExitStack()
    with ctx:
        sbt = nc.sbuf_tensor
        xbuf = ctx.enter_context(sbt("xbuf", [128, NSLOT * D], F32))
        KT = ctx.enter_context(sbt("KT", [128, 8 * S], BF16))
        V = ctx.enter_context(sbt("V", [128, NT * D], BF16))
        QT = ctx.enter_context(sbt("QT", [128, 8 * 512], BF16))
        y1T = ctx.enter_context(sbt("y1T", [128, 8 * D], BF16))
        y2T = ctx.enter_context(sbt("y2T", [128, 8 * 512], BF16))
        ystage = ctx.enter_context(sbt("ystage", [128, 4 * D], BF16))
        wstage = ctx.enter_context(sbt("wstage", [128, 2 * D], BF16))
        weff = ctx.enter_context(sbt("weff", [128, 8 * D], BF16))
        wout_s = ctx.enter_context(sbt("wout_s", [128, 8 * D], BF16))
        probsT = ctx.enter_context(sbt("probsT", [128, 2 * 1024], BF16))
        attoT = ctx.enter_context(sbt("attoT", [128, 8 * 512], BF16))
        recip = ctx.enter_context(sbt("recip", [64, 1024], F32))
        sqd = ctx.enter_context(sbt("sqd", [128, 4 * D], BF16))
        bqkv = ctx.enter_context(sbt("bqv_sb", [1, D], BF16))
        beff = ctx.enter_context(sbt("beff_sb", [1, D], BF16))
        bout = ctx.enter_context(sbt("bout_sb", [1, D], BF16))
        bqk8 = ctx.enter_context(sbt("bqk8", [128, 16], F32))
        idf = ctx.enter_context(sbt("idf", [128, 128], F32))
        onesf = ctx.enter_context(sbt("onesf", [1, 128], F32))
        onesbr = ctx.enter_context(sbt("onesbr", [1, 128], BF16))
        onesb = ctx.enter_context(sbt("onesb", [128, 64], BF16))
        identb = ctx.enter_context(sbt("identb", [128, 128], BF16))
        tri = ctx.enter_context(sbt("tri_sb", [128, 128], BF16))
        lns = ctx.enter_context(sbt("lns", [128, 64], F32))
        nrm = ctx.enter_context(sbt("nrm", [128, 64], F32))
        mqk = ctx.enter_context(sbt("mqk", [16, 8], F32))
        crow = ctx.enter_context(sbt("crow", [1, 32], F32))
        cbias = ctx.enter_context(sbt("cbias", [128, 8], F32))
        scal = ctx.enter_context(sbt("scal", [128, 8], F32))
        PS = ctx.enter_context(nc.psum_tensor("PS", [128, 4096], F32))

        semnames = ["pe", "ve", "ac", "po"]
        for pool, npool in (("dx", 2), ("dt", 4), ("ds", 2), ("dw", 2), ("du", 2)):
            semnames += [f"{pool}{i}" for i in range(npool)]
        semnames = tuple(semnames)
        sems = {s: ctx.enter_context(nc.semaphore(s)) for s in semnames}
        block = ctx.enter_context(nc.Block())

        q = {"sync": [], "tensor": [], "vector": [], "scalar": [], "gpsimd": []}
        cnt = {k: 0 for k in semnames}

        def wt(eng, sem, val):
            if val and val > 0:
                q[eng].append(("w", sem, val))

        def op(eng, fn, sem=None, amt=1):
            if sem is not None:
                cnt[sem] += amt
                q[eng].append(("o", fn, sem, amt))
                return cnt[sem]
            q[eng].append(("o", fn, None, 0))
            return None

        class DmaPool:
            def __init__(self, prefix, n):
                self.names = [f"{prefix}{i}" for i in range(n)]
                self.uses = [0] * n
                self.i = 0

            def issue(self, eng, fn, extra_waits=()):
                n = len(self.names)
                p = self.i % n
                self.i += 1
                nm = self.names[p]
                if self.uses[p] > 0:
                    wt(eng, nm, 16 * self.uses[p])
                for (wnm, wval) in extra_waits:
                    if wval:
                        wt(eng, wnm, wval)
                op(eng, fn, nm, 16)
                self.uses[p] += 1
                return (nm, 16 * self.uses[p])

            def barrier(self):
                return [(nm, 16 * u) for nm, u in zip(self.names, self.uses) if u > 0]

        PX = DmaPool("dx", 2)
        PT = DmaPool("dt", 4)
        PSTO = DmaPool("ds", 2)
        PW = DmaPool("dw", 2)
        PU = DmaPool("du", 2)

        def wtl(eng, pairs):
            for nm, v in pairs:
                wt(eng, nm, v)

        def opw(eng, fn, sem):
            v = op(eng, fn, sem)
            wt(eng, sem, v)
            return v

        xrows = x_d.ap()
        orows = out_d.ap()

        PX.issue("sync", lambda e: e.dma_start(idf[:], idf_d.ap()))
        PX.issue("sync", lambda e: e.dma_start(onesf[:], onesf_d.ap()))
        PX.issue("sync", lambda e: e.dma_start(onesbr[:], onesbr_d.ap()))
        PX.issue("sync", lambda e: e.dma_start(onesb[:], onesb_d.ap()))
        PX.issue("sync", lambda e: e.dma_start(tri[:], tri_d.ap()))
        consts_ld = PX.barrier()

        sw_bqkv = PU.issue("gpsimd", lambda e: e.dma_start(bqkv[:], bqkv_d.ap()[0:1, 2 * D:3 * D]))
        sw_beff = PU.issue("gpsimd", lambda e: e.dma_start(beff[:], fbias_d.ap()))
        PU.issue("gpsimd", lambda e: e.dma_start(bout[:], bout_d.ap()))
        sw_al = PU.issue("gpsimd", lambda e: e.dma_start(
            xbuf[:, 7 * D:7 * D + 128], al_d.ap()))
        sw_bl = PU.issue("gpsimd", lambda e: e.dma_start(V[0:16, 0:1024], bl_d.ap()))
        for c in range(8):
            PU.issue("gpsimd", lambda e, c=c: e.dma_start(
                wout_s[:, c * D:(c + 1) * D], wout_d.ap()[c * 128:(c + 1) * 128, :]))

        wtl("vector", [sw_beff])
        op("vector", lambda e: e.tensor_scalar(beff[:], beff[:], 0.5, None, ALU.mult), "ve")
        wtl("vector", consts_ld)
        op("vector", lambda e: e.tensor_copy(identb[:], idf[:]), "ve")

        bqrow = PX.issue("sync", lambda e: e.dma_start(
            xbuf[0:1, 4 * D:4 * D + 1024], bqkv_d.ap()[0:1, 0:D]))
        bkrow = PX.issue("sync", lambda e: e.dma_start(
            recip[0:1, 0:1024], bqkv_d.ap()[0:1, D:2 * D]))
        wtl("tensor", [bqrow, bkrow])
        wtl("tensor", consts_ld)
        for m in range(8):
            for r in range(2):
                hc = (2 * m + r) * 64
                op("tensor", lambda e, m=m, r=r, hc=hc: e.transpose(
                    PS[0:64, r * 16 + m:r * 16 + m + 1], xbuf[0:1, 4 * D + hc:4 * D + hc + 64], idf[0:1, 0:1]))
                pe_bias = op("tensor", lambda e, m=m, r=r, hc=hc: e.transpose(
                    PS[0:64, 32 + r * 16 + m:32 + r * 16 + m + 1], recip[0:1, hc:hc + 64],
                    idf[0:1, 0:1]), "pe")
        wt("vector", "pe", pe_bias)
        op("vector", lambda e: e.tensor_scalar(bqk8[0:64, 0:8], PS[0:64, 0:8], 0.125, None, ALU.mult), "ve")
        op("vector", lambda e: e.tensor_scalar(bqk8[64:128, 0:8], PS[0:64, 16:24], 0.125, None, ALU.mult), "ve")
        op("vector", lambda e: e.tensor_copy(bqk8[0:64, 8:16], PS[0:64, 32:40]), "ve")
        op("vector", lambda e: e.tensor_copy(bqk8[64:128, 8:16], PS[0:64, 48:56]), "ve")

        prev_red = 0
        red4 = {}
        for c in range(8):
            sl = c % 4
            kld = PX.issue("sync", lambda e, c=c, sl=sl: e.dma_start(
                xbuf[:, sl * D:(sl + 1) * D], kern_d.ap()[c * 128:(c + 1) * 128, :]),
                [("ve", red4.get(c - 4, 0))])
            wtl("vector", [kld])
            prev_red = op("vector", lambda e, c=c, sl=sl: e.reduce_max(
                nrm[:, c:c + 1], xbuf[:, sl * D:(sl + 1) * D], axis=AX.X,
                apply_absolute_value=True), "ve")
            red4[c] = prev_red
        wt("tensor", "ve", prev_red)
        op("tensor", lambda e: e.transpose(PS[0:8, 3072:3200], nrm[:, 0:8], idf[:]), "pe")
        wt("vector", "pe", cnt["pe"])
        ve1 = op("vector", lambda e: e.reduce_max(mqk[0:8, 0:1], PS[0:8, 3072:3200], axis=AX.X), "ve")
        wt("tensor", "ve", ve1)
        op("tensor", lambda e: e.transpose(PS[0:1, 3200:3208], mqk[0:8, 0:1], idf[0:8, 0:8]), "pe")
        wt("vector", "pe", cnt["pe"])
        opw("vector", lambda e: e.reduce_max(scal[0:1, 0:1], PS[0:1, 3200:3208], axis=AX.X), "ve")
        opw("vector", lambda e: e.memset(scal[0:1, 1:2], 7.0), "ve")
        opw("vector", lambda e: e.reciprocal(scal[0:1, 2:3], scal[0:1, 1:2]), "ve")
        opw("vector", lambda e: e.reciprocal(scal[0:1, 3:4], scal[0:1, 0:1]), "ve")
        opw("vector", lambda e: e.tensor_scalar(scal[0:1, 4:5], scal[0:1, 3:4], 7.0, None, ALU.mult), "ve")
        opw("vector", lambda e: e.tensor_tensor(scal[0:1, 5:6], scal[0:1, 0:1], scal[0:1, 2:3], ALU.mult), "ve")
        ve_sc = opw("vector", lambda e: e.tensor_scalar(scal[0:1, 5:6], scal[0:1, 5:6], 0.5, None, ALU.mult), "ve")
        wt("tensor", "ve", ve_sc)
        op("tensor", lambda e: e.matmul(PS[:, 3584:3585], onesf[0:1, :], scal[0:1, 4:5],
                                        start=True, stop=True))
        pe_bc = op("tensor", lambda e: e.matmul(PS[:, 3585:3586], onesf[0:1, :], scal[0:1, 5:6],
                                                start=True, stop=True), "pe")
        wt("vector", "pe", pe_bc)
        ve_bc = op("vector", lambda e: e.tensor_copy(scal[:, 6:8], PS[:, 3584:3586]), "ve")
        wtl("tensor", [sw_al])
        for c in range(8):
            pe_aT = op("tensor", lambda e, c=c: e.transpose(
                PS[0:16, 1024 + c * 128:1024 + (c + 1) * 128],
                xbuf[:, 7 * D + c * 16:7 * D + (c + 1) * 16], idf[:]), "pe")
        wt("vector", "pe", pe_aT)
        ve_aT = op("vector", lambda e: e.tensor_copy(y1T[0:16, 0:1024], PS[0:16, 1024:2048]), "ve")

        prev_w = 0
        prev_w2 = {0: 0, 1: 0}
        for c in range(8):
            sl = c % 2
            par = c % 2
            eng = "vector" if par == 0 else "gpsimd"
            sem = "ve" if par == 0 else "po"
            s2 = 2 * D if par == 0 else 5 * D
            s3 = 3 * D if par == 0 else 6 * D
            gate = cnt["ve"] if c == 0 else (prev_red if c == 1 else prev_w2[sl])
            kp = PX.issue("sync", lambda e, c=c, sl=sl: e.dma_start(
                xbuf[:, sl * D:(sl + 1) * D], kern_d.ap()[c * 128:(c + 1) * 128, :]),
                [("ve", gate)])
            if c == 0:
                wt("tensor", "ve", ve_aT)
                wtl("tensor", [sw_bl])
            else:
                wt("tensor", "ve", prev_w)
            op("tensor", lambda e, c=c: e.matmul(
                PS[:, 2048:2560], y1T[0:16, c * 128:(c + 1) * 128],
                V[0:16, 0:512], start=True, stop=True))
            pe_ab = op("tensor", lambda e, c=c: e.matmul(
                PS[:, 2560:3072], y1T[0:16, c * 128:(c + 1) * 128],
                V[0:16, 512:1024], start=True, stop=True), "pe")
            wtl(eng, [kp])
            if c < 2:
                wt(eng, "ve", ve_bc)
            op(eng, lambda e, sl=sl: e.tensor_scalar(
                xbuf[:, sl * D:(sl + 1) * D], xbuf[:, sl * D:(sl + 1) * D],
                scal[:, 6:7], 8.5, ALU.mult, ALU.add), sem)
            op(eng, lambda e, sl=sl, s2=s2: e.tensor_copy(
                xbuf[:, s2:s2 + D], xbuf[:, sl * D:(sl + 1) * D]), sem)
            for bbit in (8.0, 4.0, 2.0, 1.0):
                op(eng, lambda e, b=bbit, s2=s2, s3=s3: e.tensor_scalar(
                    xbuf[:, s3:s3 + D], xbuf[:, s2:s2 + D], b, b, ALU.is_ge, ALU.mult), sem)
                op(eng, lambda e, s2=s2, s3=s3: e.tensor_tensor(
                    xbuf[:, s2:s2 + D], xbuf[:, s2:s2 + D], xbuf[:, s3:s3 + D],
                    ALU.subtract), sem)
            op(eng, lambda e, sl=sl, s2=s2: e.tensor_tensor(
                xbuf[:, sl * D:(sl + 1) * D], xbuf[:, sl * D:(sl + 1) * D],
                xbuf[:, s2:s2 + D], ALU.subtract), sem)
            op(eng, lambda e, sl=sl: e.tensor_scalar(
                xbuf[:, sl * D:(sl + 1) * D], xbuf[:, sl * D:(sl + 1) * D],
                15.0, 0.0, ALU.min, ALU.max), sem)
            chain_last = op(eng, lambda e, sl=sl: e.tensor_scalar(
                xbuf[:, sl * D:(sl + 1) * D], xbuf[:, sl * D:(sl + 1) * D],
                8.0, scal[:, 7:8], ALU.subtract, ALU.mult), sem)
            wt("vector", "pe", pe_ab)
            if par == 1:
                wt("vector", "po", chain_last)
            prev_w = op("vector", lambda e, c=c, sl=sl: e.scalar_tensor_tensor(
                weff[:, c * D:(c + 1) * D], PS[:, 2048:3072], 0.5,
                xbuf[:, sl * D:(sl + 1) * D], ALU.mult, ALU.add), "ve")
            prev_w2[sl] = prev_w
        weff_done = prev_w

        if stage == "setup":
            wt("sync", "ve", weff_done)
            PSTO.issue("sync", lambda e: e.dma_start(dbgb_d.ap()[:, 0:8192], weff[:]))
            PSTO.issue("sync", lambda e: e.dma_start(dbgf_d.ap()[:, 0:16], bqk8[:]))
            PSTO.issue("sync", lambda e: e.dma_start(dbgf_d.ap()[0:1, 16:24], scal[0:1, :]))
            wtl("sync", PSTO.barrier())

        store_ms = {}
        xload_ms = {}
        ystage_guard = [0] * 4
        y1T_guard = [0] * 8
        y2T_guard = 0
        QT_guard = 0
        attoT_guard = 0
        KTV_guard = 0
        bank_guard = [("ve", weff_done)] * 8
        probs_guard = [0, 0]
        av_guard = ("ve", 0)
        recip_guard = 0
        q2_guard = [0, 0]
        cbias_guard = 0
        nrm_guard = 0
        wstage_pe = [0, 0]

        def bank_wait(eng, banks):
            best = {}
            for b in banks:
                smn, v = bank_guard[b]
                if smn is not None and v > best.get(smn, 0):
                    best[smn] = v
            for smn, v in best.items():
                wt(eng, smn, v)

        def bank_set(banks, sem, val):
            for b in banks:
                bank_guard[b] = (sem, val)

        if stage != "setup":
            for gt in range(min(NSLOT, n_seq * NT)):
                xload_ms[gt] = PX.issue("sync", lambda e, gt=gt: e.dma_start(
                    xbuf[:, (gt % NSLOT) * D:((gt % NSLOT) + 1) * D],
                    xrows[gt * 128:(gt + 1) * 128, :]), [("ve", weff_done)])

        def emit_ln_chain(li, need_sum, xs, xguard):
            sl4 = li % 4
            for (gnm, gv) in xguard:
                wt("scalar", gnm, gv)
            wt("scalar", "ac", cnt["ac"])
            if need_sum:
                opw("scalar", lambda e: e.activation(
                    sqd[:, sl4 * D:(sl4 + 1) * D], xs(), AF.Identity,
                    accum_out=lns[:, li * 8:li * 8 + 1]), "ac")
            ac_sq = opw("scalar", lambda e: e.activation(
                sqd[:, sl4 * D:(sl4 + 1) * D], xs(), AF.Square,
                accum_out=lns[:, li * 8 + 1:li * 8 + 2]), "ac")
            wt("gpsimd", "ac", ac_sq)
            opw("gpsimd", lambda e: e.tensor_scalar(
                lns[:, li * 8 + 2:li * 8 + 3], lns[:, li * 8:li * 8 + 1],
                -1.0 / D, None, ALU.mult), "po")
            opw("gpsimd", lambda e: e.tensor_tensor(
                lns[:, li * 8 + 3:li * 8 + 4], lns[:, li * 8 + 2:li * 8 + 3],
                lns[:, li * 8 + 2:li * 8 + 3], ALU.mult), "po")
            opw("gpsimd", lambda e: e.tensor_scalar(
                lns[:, li * 8 + 4:li * 8 + 5], lns[:, li * 8 + 1:li * 8 + 2],
                1.0 / D, EPS, ALU.mult, ALU.add), "po")
            po_v = opw("gpsimd", lambda e: e.tensor_tensor(
                lns[:, li * 8 + 4:li * 8 + 5], lns[:, li * 8 + 4:li * 8 + 5],
                lns[:, li * 8 + 3:li * 8 + 4], ALU.subtract), "po")
            wt("scalar", "po", po_v)
            opw("scalar", lambda e: e.activation(
                lns[:, li * 8 + 5:li * 8 + 6], lns[:, li * 8 + 4:li * 8 + 5],
                AF.Ln), "ac")
            ac_r = opw("scalar", lambda e: e.activation(
                lns[:, li * 8 + 6:li * 8 + 7], lns[:, li * 8 + 5:li * 8 + 6],
                AF.Exp, scale=-0.5), "ac")
            wt("gpsimd", "ac", ac_r)
            opw("gpsimd", lambda e: e.tensor_tensor(
                lns[:, li * 8 + 7:li * 8 + 8], lns[:, li * 8 + 2:li * 8 + 3],
                lns[:, li * 8 + 6:li * 8 + 7], ALU.mult), "po")

        def emit_cast_and_transpose(li, xs, dest_ap, dest_pe_guard, pair):
            sl4 = li % 4
            if ystage_guard[sl4]:
                wt("gpsimd", "pe", ystage_guard[sl4])
            po_y = op("gpsimd", lambda e: e.tensor_scalar(
                ystage[:, sl4 * D:(sl4 + 1) * D], xs(), lns[:, li * 8 + 6:li * 8 + 7],
                lns[:, li * 8 + 7:li * 8 + 8], ALU.mult, ALU.add), "po")
            pcol = pair * 1024
            bank_wait("tensor", [2 * pair, 2 * pair + 1])
            wt("tensor", "po", po_y)
            for c in range(8):
                ipe = op("tensor", lambda e, c=c, sl4=sl4, pcol=pcol: e.matmul(
                    PS[:, pcol + c * 128:pcol + (c + 1) * 128],
                    ystage[:, sl4 * D + c * 128:sl4 * D + (c + 1) * 128],
                    identb[:, :], start=True, stop=True),
                    "pe" if c == 7 else None)
            ystage_guard[sl4] = ipe
            wt("vector", "pe", ipe)
            if dest_pe_guard:
                wt("vector", "pe", dest_pe_guard)
            ve_evac = op("vector", lambda e, pcol=pcol: e.tensor_copy(
                dest_ap(), PS[:, pcol:pcol + 1024]), "ve")
            bank_set([2 * pair, 2 * pair + 1], "ve", ve_evac)
            return ve_evac

        def emit_ffn(li, bli, xs, accum_sum):
            banks = (bli * 2, bli * 2 + 1)
            pcol = banks[0] * 512
            bank_wait("tensor", banks)
            for c in range(8):
                for hh in range(2):
                    op("tensor", lambda e, c=c, hh=hh: e.matmul(
                        PS[:, pcol + hh * 512: pcol + hh * 512 + 512],
                        y1T[:, li * D + c * 128:li * D + (c + 1) * 128],
                        weff[:, c * D + hh * 512:c * D + hh * 512 + 512],
                        start=(c == 0), stop=False))
            op("tensor", lambda e: e.matmul(
                PS[:, pcol:pcol + 512], onesbr[0:1, :], beff[0:1, 0:512],
                start=False, stop=True))
            pe_f = op("tensor", lambda e: e.matmul(
                PS[:, pcol + 512:pcol + 1024], onesbr[0:1, :], beff[0:1, 512:1024],
                start=False, stop=True), "pe")
            y1T_guard[li] = pe_f
            wt("vector", "pe", pe_f)
            if accum_sum:
                ve_ep = op("vector", lambda e: e.scalar_tensor_tensor(
                    xs(), PS[:, pcol:pcol + 1024], 1.0, xs(), ALU.mult, ALU.add,
                    accum_out=lns[:, li * 8:li * 8 + 1]), "ve")
            else:
                ve_ep = op("vector", lambda e: e.scalar_tensor_tensor(
                    xs(), PS[:, pcol:pcol + 1024], 1.0, xs(), ALU.mult, ALU.add), "ve")
            bank_set(banks, "ve", ve_ep)
            return ve_ep

        units = [(sq_, qb_) for sq_ in range(n_seq) for qb_ in range(n_qb)]
        KTV_prev_seq = 0

        def emit_S1(seq, qb, tl, tinfo):
            gt = seq * NT + qb * QB + tl
            sl = gt % NSLOT
            li = tl
            xs = (lambda sl=sl: xbuf[:, sl * D:(sl + 1) * D])
            emit_ln_chain(li, True, xs, [xload_ms[gt]])
            tp1 = emit_cast_and_transpose(
                li, xs, lambda li=li: y1T[:, li * D:li * D + 1024], y1T_guard[li],
                tl % 2)
            tinfo[tl] = (gt, sl, li, xs, tp1)

        def emit_S2a(seq, qb, tl, tinfo, bli):
            gt, sl, li, xs, tp1 = tinfo[tl]
            wt("tensor", "ve", tp1)
            epi = emit_ffn(li, bli, xs, True)
            emit_ln_chain(li, False, xs, [("ve", epi)])

        def emit_TPS2(seq, qb, tl, tinfo):
            gt, sl, li, xs, tp1 = tinfo[tl]
            y2ev[0] = emit_cast_and_transpose(
                li, xs, lambda tl=tl: bass.AP(y2T, tl * 128,
                                              [[4096, 128], [512, 8], [1, 128]]),
                y2T_guard if tl == 0 else 0, 2 + tl % 2)

        def emit_P1mm(seq, qb, tl, pinfo, first):
            gt = seq * NT + qb * QB + tl
            sl = gt % NSLOT
            li = 4 + tl
            xs = (lambda sl=sl: xbuf[:, sl * D:(sl + 1) * D])
            ob = 2048 if (tl % 2) else 3072
            bank_wait("tensor", [ob // 512, ob // 512 + 1])
            if first:
                wt("tensor", "ve", ve_att_prev[0])
            for j in range(8):
                for hh in range(2):
                    op("tensor", lambda e, j=j, hh=hh, tl=tl, ob=ob: e.matmul(
                        PS[:, ob + hh * 512:ob + hh * 512 + 512],
                        attoT[:, j * 512 + tl * 128:j * 512 + (tl + 1) * 128],
                        wout_s[:, j * D + hh * 512:j * D + hh * 512 + 512],
                        start=(j == 0), stop=False))
            op("tensor", lambda e, ob=ob: e.matmul(
                PS[:, ob:ob + 512], onesbr[0:1, :], bout[0:1, 0:512],
                start=False, stop=True))
            pe_op = op("tensor", lambda e, ob=ob: e.matmul(
                PS[:, ob + 512:ob + 1024], onesbr[0:1, :], bout[0:1, 512:1024],
                start=False, stop=True), "pe")
            pinfo[tl] = [gt, sl, li, xs, pe_op, ob, 0, 0]

        def emit_P1epi(tl, pinfo):
            gt, sl, li, xs, pe_op, ob, _, _ = pinfo[tl]
            wt("vector", "pe", pe_op)
            ve_opep = op("vector", lambda e, xs=xs, li=li, ob=ob: e.scalar_tensor_tensor(
                xs(), PS[:, ob:ob + 1024], 1.0, xs(), ALU.mult, ALU.add,
                accum_out=lns[:, li * 8:li * 8 + 1]), "ve")
            bank_set([ob // 512, ob // 512 + 1], "ve", ve_opep)
            pinfo[tl][6] = ve_opep

        def emit_P2a(tl, pinfo):
            gt, sl, li, xs, pe_op, ob, ve_opep, _ = pinfo[tl]
            emit_ln_chain(li, False, xs, [("ve", ve_opep)])
            tp3 = emit_cast_and_transpose(
                li, xs, lambda li=li: y1T[:, li * D:li * D + 1024], y1T_guard[li],
                2 + tl % 2)
            pinfo[tl][7] = tp3

        def emit_P2b(tl, pinfo, bli):
            gt, sl, li, xs, pe_op, ob, ve_opep, tp3 = pinfo[tl]
            wt("tensor", "ve", tp3)
            ve_f2 = emit_ffn(li, bli, xs, False)
            store_ms[gt] = PSTO.issue("sync", lambda e, gt=gt, sl=sl: e.dma_start(
                orows[gt * 128:(gt + 1) * 128, :],
                xbuf[:, sl * D:(sl + 1) * D]), [("ve", ve_f2)])
            ngt2 = gt + NSLOT
            if ngt2 < n_seq * NT:
                xload_ms[ngt2] = PX.issue("sync", lambda e, ngt2=ngt2, sl=sl: e.dma_start(
                    xbuf[:, sl * D:(sl + 1) * D],
                    xrows[ngt2 * 128:(ngt2 + 1) * 128, :]), [store_ms[gt]])

        ve_att_prev = [0]
        y2ev = [0]
        post_pend = None

        for ui in range(len(units) + 1):
            if stage == "setup":
                break
            unit = units[ui] if ui < len(units) else None
            tinfo = {}
            pinfo = {}
            if unit is not None:
                seq, qb = unit
                if qb == 0:
                    op("vector", lambda e: e.memset(mqk[:, 2:3], 0.0), "ve")
            if post_pend is not None:
                pseq, pqb = post_pend

            swq = {}
            if unit is not None:
                for c in range(2):
                    swq[c] = PW.issue("gpsimd", lambda e, c=c: e.dma_start(
                        wstage[:, c * D:(c + 1) * D],
                        wqkv_d.ap()[c * 128:(c + 1) * 128, 0:D]),
                        [("pe", wstage_pe[c])])

            def S1(tl):
                emit_S1(seq, qb, tl, tinfo)

            def S2a(tl, bli=0):
                emit_S2a(seq, qb, tl, tinfo, bli)

            def TPS2(tl):
                emit_TPS2(seq, qb, tl, tinfo)

            def P1(tl):
                emit_P1mm(pseq, pqb, tl, pinfo, tl == 0)

            def P1e(tl):
                emit_P1epi(tl, pinfo)

            def P2a(tl):
                emit_P2a(tl, pinfo)

            def P2b(tl, bli=1):
                emit_P2b(tl, pinfo, bli)

            if unit is not None and post_pend is not None:
                P1(0); S1(0); P1e(0)
                P1(1); S1(1); P1e(1); P2a(0)
                P1(2); S1(2); P1e(2); P2a(1)
                P1(3); S1(3); P1e(3); P2a(2)
                S2a(0, 0); S2a(1, 1); P2a(3); P2b(0, 0); TPS2(0)
                S2a(2, 1); P2b(1, 0); TPS2(1)
                S2a(3, 1); P2b(2, 0); TPS2(2)
                P2b(3, 1); TPS2(3)
            elif unit is not None:
                S1(0); S1(1); S2a(0, 0); S1(2); S2a(1, 1); TPS2(0)
                S1(3); S2a(2, 0); TPS2(1); S2a(3, 1); TPS2(2); TPS2(3)
            else:
                P1(0); P1e(0); P1(1); P1e(1); P2a(0)
                P1(2); P1e(2); P2a(1); P2b(0, 1)
                P1(3); P1e(3); P2a(2); P2b(1, 0)
                P2a(3); P2b(2, 1); P2b(3, 0)
            if post_pend is not None:
                attoT_guard = pinfo[QB - 1][4]
            ac_sb = cnt["ac"]
            if unit is None:
                break

            for c in range(8):
                par = c % 2
                swc = swq[c] if c < 2 else PW.issue(
                    "gpsimd", lambda e, c=c: e.dma_start(
                        wstage[:, (c % 2) * D:((c % 2) + 1) * D],
                        wqkv_d.ap()[c * 128:(c + 1) * 128, 0:D]),
                    [("pe", wstage_pe[par])])
                wtl("tensor", [swc])
                if c == 0:
                    wt("tensor", "ve", y2ev[0])
                    bank_wait("tensor", list(range(8)))
                for m in range(8):
                    ipe = op("tensor", lambda e, c=c, m=m, par=par: e.matmul(
                        PS[:, m * 512:(m + 1) * 512],
                        wstage[:, par * D + m * 128:par * D + (m + 1) * 128],
                        y2T[:, c * 512:(c + 1) * 512],
                        start=(c == 0), stop=(c == 7)), "pe" if m == 7 else None)
                wstage_pe[par] = ipe
            pe_qwave = cnt["pe"]
            swk = {}
            for c in range(2):
                swk[c] = PW.issue("gpsimd", lambda e, c=c: e.dma_start(
                    wstage[:, c * D:(c + 1) * D],
                    wqkv_d.ap()[c * 128:(c + 1) * 128, D:2 * D]),
                    [("pe", wstage_pe[c])])
            if QT_guard:
                wt("vector", "pe", QT_guard)
            wt("vector", "pe", pe_qwave)
            for m in range(8):
                qev = op("vector", lambda e, m=m: e.tensor_scalar(
                    QT[:, m * 512:(m + 1) * 512], PS[:, m * 512:(m + 1) * 512],
                    0.125, bqk8[:, m:m + 1], ALU.mult, ALU.add), "ve")
                bank_guard[m] = ("ve", qev)
            ve_qevac = cnt["ve"]
            if q2_guard[0]:
                wt("vector", "pe", q2_guard[0])
            wt("vector", "ac", ac_sb)
            po_q2 = op("vector", lambda e: e.tensor_tensor(
                sqd[:, 0:4096], QT[:, 0:4096], QT[:, 0:4096], ALU.mult), "ve")

            for c in range(8):
                par = c % 2
                swc = swk[c] if c < 2 else PW.issue(
                    "gpsimd", lambda e, c=c: e.dma_start(
                        wstage[:, (c % 2) * D:((c % 2) + 1) * D],
                        wqkv_d.ap()[c * 128:(c + 1) * 128, D:2 * D]),
                    [("pe", wstage_pe[par])])
                wtl("tensor", [swc])
                for m in range(8):
                    if c == 0:
                        bank_wait("tensor", [m])
                    ipe = op("tensor", lambda e, c=c, m=m, par=par: e.matmul(
                        PS[:, m * 512:(m + 1) * 512],
                        wstage[:, par * D + m * 128:par * D + (m + 1) * 128],
                        y2T[:, c * 512:(c + 1) * 512],
                        start=(c == 0), stop=(c == 7)), "pe" if m == 7 else None)
                wstage_pe[par] = ipe
            pe_kwave = cnt["pe"]
            swv = {}
            for c in range(2):
                swv[c] = PW.issue("gpsimd", lambda e, c=c: e.dma_start(
                    wstage[:, c * D:(c + 1) * D],
                    wqkv_d.ap()[c * 128:(c + 1) * 128, 2 * D:3 * D]),
                    [("pe", wstage_pe[c])])
            if seq == 1 and KTV_prev_seq:
                wt("vector", "pe", KTV_prev_seq)
            wt("vector", "pe", pe_kwave)
            for m in range(8):
                kev = op("vector", lambda e, m=m, qb=qb: e.tensor_scalar(
                    KT[:, m * S + qb * 512:m * S + (qb + 1) * 512],
                    PS[:, m * 512:(m + 1) * 512],
                    bqk8[:, 8 + m:9 + m], None, ALU.add), "ve")
                bank_guard[m] = ("ve", kev)
            ve_kevac = cnt["ve"]
            po_k2 = op("vector", lambda e, qb=qb: e.tensor_tensor(
                ystage[:, 0:4096],
                bass.AP(KT, qb * 512, [[8 * S, 128], [S, 8], [1, 512]]),
                bass.AP(KT, qb * 512, [[8 * S, 128], [S, 8], [1, 512]]), ALU.mult), "ve")

            for c in range(8):
                par = c % 2
                swc = swv[c] if c < 2 else PW.issue(
                    "gpsimd", lambda e, c=c: e.dma_start(
                        wstage[:, (c % 2) * D:((c % 2) + 1) * D],
                        wqkv_d.ap()[c * 128:(c + 1) * 128, 2 * D:3 * D]),
                    [("pe", wstage_pe[par])])
                wtl("tensor", [swc])
                for tl in range(QB):
                    for hh in range(2):
                        if c == 0:
                            bank_wait("tensor", [2 * tl + hh])
                        ipe = op("tensor", lambda e, c=c, tl=tl, hh=hh, par=par: e.matmul(
                            PS[:, tl * 1024 + hh * 512:tl * 1024 + hh * 512 + 512],
                            y2T[:, c * 512 + tl * 128:c * 512 + (tl + 1) * 128],
                            wstage[:, par * D + hh * 512:par * D + hh * 512 + 512],
                            start=(c == 0), stop=False),
                            "pe" if (tl == QB - 1 and hh == 1) else None)
                wstage_pe[par] = ipe
            for tl in range(QB):
                for hh in range(2):
                    ipe = op("tensor", lambda e, tl=tl, hh=hh: e.matmul(
                        PS[:, tl * 1024 + hh * 512:tl * 1024 + hh * 512 + 512],
                        onesbr[0:1, :], bqkv[0:1, hh * 512:hh * 512 + 512],
                        start=False, stop=True), "pe" if (tl == QB - 1 and hh == 1) else None)
            pe_vwave = ipe
            y2T_guard = pe_vwave
            if seq == 1 and KTV_prev_seq:
                wt("vector", "pe", KTV_prev_seq)
            wt("vector", "pe", pe_vwave)
            for tl in range(QB):
                t = qb * QB + tl
                vev = op("vector", lambda e, tl=tl, t=t: e.tensor_copy(
                    V[:, t * D:(t + 1) * D], PS[:, tl * 1024:(tl + 1) * 1024]), "ve")
                bank_guard[2 * tl] = ("ve", vev)
                bank_guard[2 * tl + 1] = ("ve", vev)
            ve_vevac = cnt["ve"]

            wt("tensor", "ve", po_q2)
            bank_wait("tensor", [4, 5])
            for m in range(8):
                for tc in range(4):
                    for r in range(2):
                        ipe = op("tensor", lambda e, m=m, tc=tc, r=r: e.matmul(
                            PS[:, 2048 + tc * 16 + 2 * m + r: 2048 + tc * 16 + 2 * m + r + 1],
                            sqd[r * 64:r * 64 + 64,
                                m * 512 + tc * 128:m * 512 + (tc + 1) * 128],
                            onesb[r * 64:r * 64 + 64, 0:1],
                            start=True, stop=True, tile_position=(r * 64, 0)),
                            "pe" if (m == 7 and tc == 3 and r == 1) else None)
            q2_guard[0] = ipe
            wt("vector", "pe", ipe)
            ve_nrm = op("vector", lambda e: e.tensor_copy(nrm[:, 0:64], PS[:, 2048:2112]), "ve")
            wt("tensor", "ve", ve_nrm)
            for tc in range(4):
                ipe = op("tensor", lambda e, tc=tc: e.transpose(
                    PS[0:16, 2560 + tc * 128:2560 + (tc + 1) * 128],
                    nrm[:, tc * 16:(tc + 1) * 16], idf[:]), "pe" if tc == 3 else None)
            wt("vector", "pe", ipe)
            ve_mq = opw("vector", lambda e: e.reduce_max(
                mqk[:, 0:1], PS[0:16, 2560:3072], axis=AX.X), "ve")
            wt("tensor", "ve", po_k2)
            wt("tensor", "ve", ve_nrm)
            for m in range(8):
                for tc in range(4):
                    for r in range(2):
                        ipe = op("tensor", lambda e, m=m, tc=tc, r=r: e.matmul(
                            PS[:, 2048 + tc * 16 + 2 * m + r: 2048 + tc * 16 + 2 * m + r + 1],
                            ystage[r * 64:r * 64 + 64,
                                   m * 512 + tc * 128:m * 512 + (tc + 1) * 128],
                            onesb[r * 64:r * 64 + 64, 0:1],
                            start=True, stop=True, tile_position=(r * 64, 0)),
                            "pe" if (m == 7 and tc == 3 and r == 1) else None)
            knorm_pe = ipe
            wt("vector", "pe", knorm_pe)
            ve_nrm2 = op("vector", lambda e: e.tensor_copy(nrm[:, 0:64], PS[:, 2048:2112]), "ve")
            wt("tensor", "ve", ve_nrm2)
            for tc in range(4):
                ipe = op("tensor", lambda e, tc=tc: e.transpose(
                    PS[0:16, 2560 + tc * 128:2560 + (tc + 1) * 128],
                    nrm[:, tc * 16:(tc + 1) * 16], idf[:]), "pe" if tc == 3 else None)
            wt("vector", "pe", ipe)
            opw("vector", lambda e: e.reduce_max(mqk[:, 1:2], PS[0:16, 2560:3072], axis=AX.X), "ve")
            ve_mk = opw("vector", lambda e: e.tensor_tensor(
                mqk[:, 2:3], mqk[:, 2:3], mqk[:, 1:2], ALU.max), "ve")

            wt("vector", "ve", ve_mk)
            opw("vector", lambda e: e.tensor_tensor(
                mqk[:, 3:4], mqk[:, 0:1], mqk[:, 2:3], ALU.mult), "ve")
            ve_prod = opw("vector", lambda e: e.tensor_scalar(
                mqk[:, 3:4], mqk[:, 3:4], 1e-30, None, ALU.add), "ve")
            wt("scalar", "ve", ve_prod)
            opw("scalar", lambda e: e.activation(mqk[:, 4:5], mqk[:, 3:4], AF.Ln), "ac")
            ac_c = opw("scalar", lambda e: e.activation(mqk[:, 5:6], mqk[:, 4:5], AF.Exp,
                                                        scale=0.5), "ac")
            wt("tensor", "ac", ac_c)
            pe_crow = op("tensor", lambda e: e.transpose(PS[0:1, 2576:2592], mqk[0:16, 5:6],
                                                         idf[0:16, 0:16]), "pe")
            wt("vector", "pe", pe_crow)
            opw("vector", lambda e: e.tensor_copy(crow[0:1, 0:16], PS[0:1, 2576:2592]), "ve")
            opw("vector", lambda e: e.tensor_tensor(
                crow[0:1, 16:24],
                bass.AP(crow, 0, [[32, 1], [2, 8]]),
                bass.AP(crow, 1, [[32, 1], [2, 8]]), ALU.max), "ve")
            ve_crow = opw("vector", lambda e: e.tensor_scalar(
                crow[0:1, 16:24], crow[0:1, 16:24], -1.0, None, ALU.mult), "ve")
            wt("tensor", "ve", ve_crow)
            pe_cb = op("tensor", lambda e: e.matmul(PS[:, 2048:2056], onesf[0:1, :],
                                                    crow[0:1, 16:24],
                                                    start=True, stop=True), "pe")
            wt("vector", "pe", pe_cb)
            if cbias_guard:
                wt("vector", "ac", cbias_guard)
            ve_cb = op("vector", lambda e: e.tensor_copy(cbias[:], PS[:, 2048:2056]), "ve")
            bank_set([4, 5], "ve", ve_cb)

            nch = 4 * (qb + 1)
            wt("scalar", "ve", ve_cb)
            gw = 0
            for jp in range(4):
                prs = (2 * jp, 2 * jp + 1)
                avb = {prs[0]: 2048, prs[1]: 3072}
                pend = None

                def emit_av(p):
                    j, w, par, off, last, ab, ac_exp, ve_mask = p
                    if w == 0:
                        bank_wait("tensor", [ab // 512, ab // 512 + 1])
                    wt("tensor", "ac", ac_exp)
                    if ve_mask:
                        wt("tensor", "ve", ve_mask)
                    for r in range(2):
                        hidx = 2 * j + r
                        op("tensor", lambda e, w=w, hidx=hidx, r=r, par=par, off=off,
                           last=last, ab=ab, qb=qb: e.matmul(
                            PS[0:64, ab + r * 512 + off:ab + r * 512 + 512],
                            V[:, w * D + hidx * 64:w * D + (hidx + 1) * 64],
                            probsT[:, par * 1024 + r * 512 + off:par * 1024 + (r + 1) * 512],
                            start=(w == 0), stop=last, tile_position=(0, 0)))
                        ipe = op("tensor", lambda e, w=w, r=r, par=par, off=off,
                                 last=last, ab=ab: e.matmul(
                            PS[64:128, ab + r * 512 + off:ab + r * 512 + 512],
                            onesb[:, :],
                            probsT[:, par * 1024 + r * 512 + off:par * 1024 + (r + 1) * 512],
                            start=(w == 0), stop=last, tile_position=(0, 64)), "pe" if r == 1 else None)
                    probs_guard[par] = ipe

                for w in range(nch):
                    for j in prs:
                        kc = w
                        par = gw % 2
                        gw += 1
                        lc = kc * 128 - qb * 512
                        diag = lc >= 0
                        last = (w == nch - 1)
                        ab = avb[j]
                        bank_wait("tensor", [par * 2, par * 2 + 1])
                        op("tensor", lambda e, j=j, kc=kc, par=par: e.matmul(
                            PS[:, par * 1024:par * 1024 + 512],
                            KT[0:64, j * S + kc * 128:j * S + (kc + 1) * 128],
                            QT[0:64, j * 512:(j + 1) * 512],
                            start=True, stop=True, tile_position=(0, 0)))
                        pe_sc = op("tensor", lambda e, j=j, kc=kc, par=par: e.matmul(
                            PS[:, par * 1024 + 512:par * 1024 + 1024],
                            KT[64:128, j * S + kc * 128:j * S + (kc + 1) * 128],
                            QT[64:128, j * 512:(j + 1) * 512],
                            start=True, stop=True, tile_position=(64, 0)), "pe")
                        wt("scalar", "pe", pe_sc)
                        if probs_guard[par]:
                            wt("scalar", "pe", probs_guard[par])
                        ac_exp = op("scalar", lambda e, j=j, par=par: e.activation(
                            probsT[:, par * 1024:(par + 1) * 1024],
                            PS[:, par * 1024:(par + 1) * 1024], AF.Exp,
                            bias=cbias[:, j:j + 1]), "ac")
                        bank_set([par * 2, par * 2 + 1], "ac", ac_exp)
                        ve_mask = 0
                        if diag:
                            wt("vector", "ac", ac_exp)
                            for r in range(2):
                                op("vector", lambda e, r=r, par=par, lc=lc: e.tensor_tensor(
                                    probsT[:, par * 1024 + r * 512 + lc:
                                           par * 1024 + r * 512 + lc + 128],
                                    probsT[:, par * 1024 + r * 512 + lc:
                                           par * 1024 + r * 512 + lc + 128],
                                    tri[:], ALU.mult), "ve")
                            ve_mask = cnt["ve"]
                        if pend is not None:
                            emit_av(pend)
                        pend = (j, w, par, max(lc, 0), last, ab, ac_exp, ve_mask)
                emit_av(pend)
                pe_av = cnt["pe"]
                for j in prs:
                    ab = avb[j]
                    wt("scalar", "pe", pe_av)
                    if recip_guard:
                        wt("scalar", "ve", recip_guard)
                    op("scalar", lambda e, ab=ab: e.activation(
                        recip[:], PS[64:128, ab:ab + 1024], AF.Ln), "ac")
                    ac_rec = op("scalar", lambda e: e.activation(recip[:], recip[:], AF.Exp,
                                                                 scale=-1.0), "ac")
                    wt("vector", "ac", ac_rec)
                    if attoT_guard:
                        wt("vector", "pe", attoT_guard)
                    op("vector", lambda e, j=j, ab=ab: e.tensor_tensor(
                        attoT[0:64, j * 512:(j + 1) * 512], PS[0:64, ab:ab + 512],
                        recip[0:64, 0:512], ALU.mult), "ve")
                    ve_resc = op("vector", lambda e, j=j, ab=ab: e.tensor_tensor(
                        attoT[64:128, j * 512:(j + 1) * 512], PS[0:64, ab + 512:ab + 1024],
                        recip[0:64, 512:1024], ALU.mult), "ve")
                    recip_guard = ve_resc
                    bank_set([ab // 512, ab // 512 + 1], "ve", ve_resc)
            ve_att_prev[0] = cnt["ve"]
            cbias_guard = cnt["ac"]
            QT_guard = cnt["pe"]
            if qb == n_qb - 1:
                KTV_prev_seq = cnt["pe"]
            post_pend = unit

        wtl("sync", PSTO.barrier())

        nc._sched_q = {k: list(v) for k, v in q.items()}
        nc._sched_cnt = dict(cnt)

        def replay(name, eng):
            for item in q[name]:
                if item[0] == "w":
                    eng.wait_ge(sems[item[1]], item[2])
                else:
                    _, fn, smn, amt = item
                    ins = fn(eng)
                    if smn is not None:
                        ins.then_inc(sems[smn], amt)

        @block.sync
        def _(e):
            replay("sync", e)

        @block.tensor
        def _(e):
            replay("tensor", e)

        @block.vector
        def _(e):
            replay("vector", e)

        @block.scalar
        def _(e):
            replay("scalar", e)

        @block.gpsimd
        def _(e):
            replay("gpsimd", e)

    return nc


_NC_CACHE = None


def _get_nc():
    global _NC_CACHE
    if _NC_CACHE is None:
        _NC_CACHE = build_program()
    return _NC_CACHE


def kernel(L0_attn_x, L0_attn_w_qkv, L0_attn_b_qkv, L0_attn_w_out, L0_attn_b_out,
           L0_ffn_kernel, L0_ffn_bias, L0_ffn_a, L0_ffn_b):
    x = np.ascontiguousarray(np.asarray(L0_attn_x, dtype=np.float32))
    wqkv = np.asarray(L0_attn_w_qkv, dtype=np.float32)
    bqkv = np.asarray(L0_attn_b_qkv, dtype=np.float32)
    wout = np.ascontiguousarray(np.asarray(L0_attn_w_out, dtype=np.float32))
    bout = np.ascontiguousarray(np.asarray(L0_attn_b_out, dtype=np.float32).reshape(1, D))
    kern = np.ascontiguousarray(np.asarray(L0_ffn_kernel, dtype=np.float32))
    fbias = np.ascontiguousarray(np.asarray(L0_ffn_bias, dtype=np.float32).reshape(1, D))
    a = np.asarray(L0_ffn_a, dtype=np.float32)
    b = np.ascontiguousarray(np.asarray(L0_ffn_b, dtype=np.float32))

    wq3 = wqkv.reshape(D, H, 3, 64)
    wqkv_r = np.ascontiguousarray(np.concatenate(
        [wq3[:, :, 0, :].reshape(D, D), wq3[:, :, 1, :].reshape(D, D),
         wq3[:, :, 2, :].reshape(D, D)], axis=1))
    bq3 = bqkv.reshape(H, 3, 64)
    bqkv_r = np.ascontiguousarray(np.concatenate(
        [bq3[:, 0, :].reshape(-1), bq3[:, 1, :].reshape(-1),
         bq3[:, 2, :].reshape(-1)])).reshape(1, 3 * D)
    a_l = np.ascontiguousarray(a.reshape(8, 128, 16).transpose(1, 0, 2).reshape(128, 128))

    identf = np.eye(128, dtype=np.float32)
    ones_f = np.ones((1, 128), dtype=np.float32)
    ones_bfr = np.ones((1, 128), dtype=ml_dtypes.bfloat16)
    ones_bf = np.ones((128, 64), dtype=ml_dtypes.bfloat16)
    kk, qq = np.meshgrid(np.arange(128), np.arange(128), indexing="ij")
    tri_m = np.ascontiguousarray((qq >= kk).astype(ml_dtypes.bfloat16))

    nc = _get_nc()
    in_maps = []
    for cidx in range(NCORES):
        in_maps.append({
            "x": np.ascontiguousarray(x[2 * cidx:2 * cidx + 2].reshape(NSEQ * S, D)),
            "wqkv_r": wqkv_r, "bqkv_r": bqkv_r, "wout": wout, "bout": bout,
            "kern": kern, "fbias": fbias, "a_l": a_l, "b_l": b,
            "identf": identf, "ones_f": ones_f, "ones_bfr": ones_bfr,
            "ones_bf": ones_bf, "tri": tri_m,
        })
    res = run_bass_kernel_spmd(nc, in_maps, core_ids=list(range(NCORES)))
    out = np.concatenate([r["out"].reshape(NSEQ, S, D) for r in res.results], axis=0)
    return out


# version 18
# speedup vs baseline: 1.1082x; 1.0984x over previous
import numpy as np
import ml_dtypes
from contextlib import ExitStack

import concourse.bass as bass
import concourse.mybir as mybir
from concourse.bass_utils import run_bass_kernel_spmd

F32 = mybir.dt.float32
BF16 = mybir.dt.bfloat16
AF = mybir.ActivationFunctionType
ALU = mybir.AluOpType
AX = mybir.AxisListType

NCORES = 8
NSEQ = 2
S = 2048
D = 1024
H = 16
NT = 16
QB = 4
NQB = 4
NSLOT = 8
EPS = 1e-5


def build_program(n_seq=NSEQ, n_qb=NQB, stage="full", races=True):
    nc = bass.Bass("TRN2", target_bir_lowering=False, detect_race_conditions=races)

    x_d = nc.dram_tensor("x", [NSEQ * S, D], F32, kind="ExternalInput")
    wqkv_d = nc.dram_tensor("wqkv_r", [D, 3 * D], F32, kind="ExternalInput")
    bqkv_d = nc.dram_tensor("bqkv_r", [1, 3 * D], F32, kind="ExternalInput")
    wout_d = nc.dram_tensor("wout", [D, D], F32, kind="ExternalInput")
    bout_d = nc.dram_tensor("bout", [1, D], F32, kind="ExternalInput")
    kern_d = nc.dram_tensor("kern", [D, D], F32, kind="ExternalInput")
    fbias_d = nc.dram_tensor("fbias", [1, D], F32, kind="ExternalInput")
    al_d = nc.dram_tensor("a_l", [128, 128], F32, kind="ExternalInput")
    bl_d = nc.dram_tensor("b_l", [16, D], F32, kind="ExternalInput")
    idf_d = nc.dram_tensor("identf", [128, 128], F32, kind="ExternalInput")
    onesf_d = nc.dram_tensor("ones_f", [1, 128], F32, kind="ExternalInput")
    onesbr_d = nc.dram_tensor("ones_bfr", [1, 128], BF16, kind="ExternalInput")
    onesb_d = nc.dram_tensor("ones_bf", [128, 64], BF16, kind="ExternalInput")
    tri_d = nc.dram_tensor("tri", [128, 128], BF16, kind="ExternalInput")
    out_d = nc.dram_tensor("out", [NSEQ * S, D], F32, kind="ExternalOutput")
    if stage != "full":
        dbgb_d = nc.dram_tensor("dbgb", [128, 16384], BF16, kind="ExternalOutput")
        dbgf_d = nc.dram_tensor("dbgf", [128, 1024], F32, kind="ExternalOutput")

    ctx = ExitStack()
    with ctx:
        sbt = nc.sbuf_tensor
        xbuf = ctx.enter_context(sbt("xbuf", [128, NSLOT * D], F32))
        KT = ctx.enter_context(sbt("KT", [128, 8 * S], BF16))
        V = ctx.enter_context(sbt("V", [128, NT * D], BF16))
        QT = ctx.enter_context(sbt("QT", [128, 8 * 512], BF16))
        y1T = ctx.enter_context(sbt("y1T", [128, 8 * D], BF16))
        y2T = ctx.enter_context(sbt("y2T", [128, 8 * 512], BF16))
        ystage = ctx.enter_context(sbt("ystage", [128, 4 * D], BF16))
        wstage = ctx.enter_context(sbt("wstage", [128, 2 * D], BF16))
        weff = ctx.enter_context(sbt("weff", [128, 8 * D], BF16))
        wout_s = ctx.enter_context(sbt("wout_s", [128, 8 * D], BF16))
        probsT = ctx.enter_context(sbt("probsT", [128, 2 * 1024], BF16))
        attoT = ctx.enter_context(sbt("attoT", [128, 8 * 512], BF16))
        recip = ctx.enter_context(sbt("recip", [64, 1024], F32))
        sqd = ctx.enter_context(sbt("sqd", [128, 4 * D], BF16))
        bqkv = ctx.enter_context(sbt("bqv_sb", [1, D], BF16))
        beff = ctx.enter_context(sbt("beff_sb", [1, D], BF16))
        bout = ctx.enter_context(sbt("bout_sb", [1, D], BF16))
        bqk8 = ctx.enter_context(sbt("bqk8", [128, 16], F32))
        idf = ctx.enter_context(sbt("idf", [128, 128], F32))
        onesf = ctx.enter_context(sbt("onesf", [1, 128], F32))
        onesbr = ctx.enter_context(sbt("onesbr", [1, 128], BF16))
        onesb = ctx.enter_context(sbt("onesb", [128, 64], BF16))
        identb = ctx.enter_context(sbt("identb", [128, 128], BF16))
        tri = ctx.enter_context(sbt("tri_sb", [128, 128], BF16))
        lns = ctx.enter_context(sbt("lns", [128, 64], F32))
        nrm = ctx.enter_context(sbt("nrm", [128, 64], F32))
        mqk = ctx.enter_context(sbt("mqk", [16, 8], F32))
        crow = ctx.enter_context(sbt("crow", [1, 32], F32))
        cbias = ctx.enter_context(sbt("cbias", [128, 8], F32))
        scal = ctx.enter_context(sbt("scal", [128, 8], F32))
        PS = ctx.enter_context(nc.psum_tensor("PS", [128, 4096], F32))

        semnames = ["pe", "ve", "ac", "po"]
        for pool, npool in (("dx", 2), ("dt", 4), ("ds", 2), ("dw", 2), ("du", 2)):
            semnames += [f"{pool}{i}" for i in range(npool)]
        semnames = tuple(semnames)
        sems = {s: ctx.enter_context(nc.semaphore(s)) for s in semnames}
        block = ctx.enter_context(nc.Block())

        q = {"sync": [], "tensor": [], "vector": [], "scalar": [], "gpsimd": []}
        cnt = {k: 0 for k in semnames}

        def wt(eng, sem, val):
            if val and val > 0:
                q[eng].append(("w", sem, val))

        def op(eng, fn, sem=None, amt=1):
            if sem is not None:
                cnt[sem] += amt
                q[eng].append(("o", fn, sem, amt))
                return cnt[sem]
            q[eng].append(("o", fn, None, 0))
            return None

        class DmaPool:
            def __init__(self, prefix, n):
                self.names = [f"{prefix}{i}" for i in range(n)]
                self.uses = [0] * n
                self.i = 0

            def issue(self, eng, fn, extra_waits=()):
                n = len(self.names)
                p = self.i % n
                self.i += 1
                nm = self.names[p]
                if self.uses[p] > 0:
                    wt(eng, nm, 16 * self.uses[p])
                for (wnm, wval) in extra_waits:
                    if wval:
                        wt(eng, wnm, wval)
                op(eng, fn, nm, 16)
                self.uses[p] += 1
                return (nm, 16 * self.uses[p])

            def barrier(self):
                return [(nm, 16 * u) for nm, u in zip(self.names, self.uses) if u > 0]

        PX = DmaPool("dx", 2)
        PT = DmaPool("dt", 4)
        PSTO = DmaPool("ds", 2)
        PW = DmaPool("dw", 2)
        PU = DmaPool("du", 2)

        def wtl(eng, pairs):
            for nm, v in pairs:
                wt(eng, nm, v)

        def opw(eng, fn, sem):
            v = op(eng, fn, sem)
            wt(eng, sem, v)
            return v

        xrows = x_d.ap()
        orows = out_d.ap()

        PX.issue("sync", lambda e: e.dma_start(idf[:], idf_d.ap()))
        PX.issue("sync", lambda e: e.dma_start(onesf[:], onesf_d.ap()))
        PX.issue("sync", lambda e: e.dma_start(onesbr[:], onesbr_d.ap()))
        PX.issue("sync", lambda e: e.dma_start(onesb[:], onesb_d.ap()))
        PX.issue("sync", lambda e: e.dma_start(tri[:], tri_d.ap()))
        consts_ld = PX.barrier()

        sw_bqkv = PU.issue("gpsimd", lambda e: e.dma_start(bqkv[:], bqkv_d.ap()[0:1, 2 * D:3 * D]))
        sw_beff = PU.issue("gpsimd", lambda e: e.dma_start(beff[:], fbias_d.ap()))
        PU.issue("gpsimd", lambda e: e.dma_start(bout[:], bout_d.ap()))
        sw_al = PU.issue("gpsimd", lambda e: e.dma_start(
            xbuf[:, 7 * D:7 * D + 128], al_d.ap()))
        sw_bl = PU.issue("gpsimd", lambda e: e.dma_start(V[0:16, 0:1024], bl_d.ap()))
        for c in range(8):
            PU.issue("gpsimd", lambda e, c=c: e.dma_start(
                wout_s[:, c * D:(c + 1) * D], wout_d.ap()[c * 128:(c + 1) * 128, :]))

        wtl("vector", [sw_beff])
        op("vector", lambda e: e.tensor_scalar(beff[:], beff[:], 0.5, None, ALU.mult), "ve")
        wtl("vector", consts_ld)
        op("vector", lambda e: e.tensor_copy(identb[:], idf[:]), "ve")

        bqrow = PX.issue("sync", lambda e: e.dma_start(
            xbuf[0:1, 4 * D:4 * D + 1024], bqkv_d.ap()[0:1, 0:D]))
        bkrow = PX.issue("sync", lambda e: e.dma_start(
            recip[0:1, 0:1024], bqkv_d.ap()[0:1, D:2 * D]))
        wtl("tensor", [bqrow, bkrow])
        wtl("tensor", consts_ld)
        for m in range(8):
            for r in range(2):
                hc = (2 * m + r) * 64
                op("tensor", lambda e, m=m, r=r, hc=hc: e.transpose(
                    PS[0:64, r * 16 + m:r * 16 + m + 1], xbuf[0:1, 4 * D + hc:4 * D + hc + 64], idf[0:1, 0:1]))
                pe_bias = op("tensor", lambda e, m=m, r=r, hc=hc: e.transpose(
                    PS[0:64, 32 + r * 16 + m:32 + r * 16 + m + 1], recip[0:1, hc:hc + 64],
                    idf[0:1, 0:1]), "pe")
        wt("vector", "pe", pe_bias)
        op("vector", lambda e: e.tensor_scalar(bqk8[0:64, 0:8], PS[0:64, 0:8], 0.125, None, ALU.mult), "ve")
        op("vector", lambda e: e.tensor_scalar(bqk8[64:128, 0:8], PS[0:64, 16:24], 0.125, None, ALU.mult), "ve")
        op("vector", lambda e: e.tensor_copy(bqk8[0:64, 8:16], PS[0:64, 32:40]), "ve")
        op("vector", lambda e: e.tensor_copy(bqk8[64:128, 8:16], PS[0:64, 48:56]), "ve")

        prev_red = 0
        red4 = {}
        for c in range(8):
            sl = c % 4
            kld = PX.issue("sync", lambda e, c=c, sl=sl: e.dma_start(
                xbuf[:, sl * D:(sl + 1) * D], kern_d.ap()[c * 128:(c + 1) * 128, :]),
                [("ve", red4.get(c - 4, 0))])
            wtl("vector", [kld])
            prev_red = op("vector", lambda e, c=c, sl=sl: e.reduce_max(
                nrm[:, c:c + 1], xbuf[:, sl * D:(sl + 1) * D], axis=AX.X,
                apply_absolute_value=True), "ve")
            red4[c] = prev_red
        wt("tensor", "ve", prev_red)
        op("tensor", lambda e: e.transpose(PS[0:8, 3072:3200], nrm[:, 0:8], idf[:]), "pe")
        wt("vector", "pe", cnt["pe"])
        ve1 = op("vector", lambda e: e.reduce_max(mqk[0:8, 0:1], PS[0:8, 3072:3200], axis=AX.X), "ve")
        wt("tensor", "ve", ve1)
        op("tensor", lambda e: e.transpose(PS[0:1, 3200:3208], mqk[0:8, 0:1], idf[0:8, 0:8]), "pe")
        wt("vector", "pe", cnt["pe"])
        opw("vector", lambda e: e.reduce_max(scal[0:1, 0:1], PS[0:1, 3200:3208], axis=AX.X), "ve")
        opw("vector", lambda e: e.memset(scal[0:1, 1:2], 7.0), "ve")
        opw("vector", lambda e: e.reciprocal(scal[0:1, 2:3], scal[0:1, 1:2]), "ve")
        opw("vector", lambda e: e.reciprocal(scal[0:1, 3:4], scal[0:1, 0:1]), "ve")
        opw("vector", lambda e: e.tensor_scalar(scal[0:1, 4:5], scal[0:1, 3:4], 7.0, None, ALU.mult), "ve")
        opw("vector", lambda e: e.tensor_tensor(scal[0:1, 5:6], scal[0:1, 0:1], scal[0:1, 2:3], ALU.mult), "ve")
        ve_sc = opw("vector", lambda e: e.tensor_scalar(scal[0:1, 5:6], scal[0:1, 5:6], 0.5, None, ALU.mult), "ve")
        wt("tensor", "ve", ve_sc)
        op("tensor", lambda e: e.matmul(PS[:, 3584:3585], onesf[0:1, :], scal[0:1, 4:5],
                                        start=True, stop=True))
        pe_bc = op("tensor", lambda e: e.matmul(PS[:, 3585:3586], onesf[0:1, :], scal[0:1, 5:6],
                                                start=True, stop=True), "pe")
        wt("vector", "pe", pe_bc)
        ve_bc = op("vector", lambda e: e.tensor_copy(scal[:, 6:8], PS[:, 3584:3586]), "ve")
        wtl("tensor", [sw_al])
        for c in range(8):
            pe_aT = op("tensor", lambda e, c=c: e.transpose(
                PS[0:16, 1024 + c * 128:1024 + (c + 1) * 128],
                xbuf[:, 7 * D + c * 16:7 * D + (c + 1) * 16], idf[:]), "pe")
        wt("vector", "pe", pe_aT)
        ve_aT = op("vector", lambda e: e.tensor_copy(y1T[0:16, 0:1024], PS[0:16, 1024:2048]), "ve")

        prev_w = 0
        prev_w2 = {0: 0, 1: 0}
        for c in range(8):
            sl = c % 2
            par = c % 2
            eng = "vector"
            sem = "ve"
            s2 = 2 * D
            s3 = 3 * D
            gate = cnt["ve"] if c == 0 else (prev_red if c == 1 else prev_w2[sl])
            kp = PX.issue("sync", lambda e, c=c, sl=sl: e.dma_start(
                xbuf[:, sl * D:(sl + 1) * D], kern_d.ap()[c * 128:(c + 1) * 128, :]),
                [("ve", gate)])
            if c == 0:
                wt("tensor", "ve", ve_aT)
                wtl("tensor", [sw_bl])
            else:
                wt("tensor", "ve", prev_w)
            op("tensor", lambda e, c=c: e.matmul(
                PS[:, 2048:2560], y1T[0:16, c * 128:(c + 1) * 128],
                V[0:16, 0:512], start=True, stop=True))
            pe_ab = op("tensor", lambda e, c=c: e.matmul(
                PS[:, 2560:3072], y1T[0:16, c * 128:(c + 1) * 128],
                V[0:16, 512:1024], start=True, stop=True), "pe")
            wtl(eng, [kp])
            if c < 2:
                wt(eng, "ve", ve_bc)
            op(eng, lambda e, sl=sl: e.tensor_scalar(
                xbuf[:, sl * D:(sl + 1) * D], xbuf[:, sl * D:(sl + 1) * D],
                scal[:, 6:7], 8.5, ALU.mult, ALU.add), sem)
            op(eng, lambda e, sl=sl, s2=s2: e.tensor_copy(
                xbuf[:, s2:s2 + D], xbuf[:, sl * D:(sl + 1) * D]), sem)
            for bbit in (8.0, 4.0, 2.0, 1.0):
                op(eng, lambda e, b=bbit, s2=s2, s3=s3: e.tensor_scalar(
                    xbuf[:, s3:s3 + D], xbuf[:, s2:s2 + D], b, b, ALU.is_ge, ALU.mult), sem)
                op(eng, lambda e, s2=s2, s3=s3: e.tensor_tensor(
                    xbuf[:, s2:s2 + D], xbuf[:, s2:s2 + D], xbuf[:, s3:s3 + D],
                    ALU.subtract), sem)
            op(eng, lambda e, sl=sl, s2=s2: e.tensor_tensor(
                xbuf[:, sl * D:(sl + 1) * D], xbuf[:, sl * D:(sl + 1) * D],
                xbuf[:, s2:s2 + D], ALU.subtract), sem)
            op(eng, lambda e, sl=sl: e.tensor_scalar(
                xbuf[:, sl * D:(sl + 1) * D], xbuf[:, sl * D:(sl + 1) * D],
                15.0, 0.0, ALU.min, ALU.max), sem)
            chain_last = op(eng, lambda e, sl=sl: e.tensor_scalar(
                xbuf[:, sl * D:(sl + 1) * D], xbuf[:, sl * D:(sl + 1) * D],
                8.0, scal[:, 7:8], ALU.subtract, ALU.mult), sem)
            wt("vector", "pe", pe_ab)
            prev_w = op("vector", lambda e, c=c, sl=sl: e.scalar_tensor_tensor(
                weff[:, c * D:(c + 1) * D], PS[:, 2048:3072], 0.5,
                xbuf[:, sl * D:(sl + 1) * D], ALU.mult, ALU.add), "ve")
            prev_w2[sl] = prev_w
        weff_done = prev_w

        if stage == "setup":
            wt("sync", "ve", weff_done)
            PSTO.issue("sync", lambda e: e.dma_start(dbgb_d.ap()[:, 0:8192], weff[:]))
            PSTO.issue("sync", lambda e: e.dma_start(dbgf_d.ap()[:, 0:16], bqk8[:]))
            PSTO.issue("sync", lambda e: e.dma_start(dbgf_d.ap()[0:1, 16:24], scal[0:1, :]))
            wtl("sync", PSTO.barrier())

        store_ms = {}
        xload_ms = {}
        ystage_guard = [0] * 4
        y1T_guard = [0] * 8
        y2T_guard = 0
        QT_guard = 0
        attoT_guard = 0
        KTV_guard = 0
        bank_guard = [("ve", weff_done)] * 8
        probs_guard = [0, 0]
        av_guard = ("ve", 0)
        recip_guard = 0
        q2_guard = [0, 0]
        cbias_guard = 0
        nrm_guard = 0
        wstage_pe = [0, 0]

        def bank_wait(eng, banks):
            best = {}
            for b in banks:
                smn, v = bank_guard[b]
                if smn is not None and v > best.get(smn, 0):
                    best[smn] = v
            for smn, v in best.items():
                wt(eng, smn, v)

        def bank_set(banks, sem, val):
            for b in banks:
                bank_guard[b] = (sem, val)

        if stage != "setup":
            for gt in range(min(NSLOT, n_seq * NT)):
                xload_ms[gt] = PX.issue("sync", lambda e, gt=gt: e.dma_start(
                    xbuf[:, (gt % NSLOT) * D:((gt % NSLOT) + 1) * D],
                    xrows[gt * 128:(gt + 1) * 128, :]), [("ve", weff_done)])

        def emit_ln_chain(li, need_sum, xs, xguard):
            sl4 = li % 4
            for (gnm, gv) in xguard:
                wt("scalar", gnm, gv)
            wt("scalar", "ac", cnt["ac"])
            if need_sum:
                opw("scalar", lambda e: e.activation(
                    sqd[:, sl4 * D:(sl4 + 1) * D], xs(), AF.Identity,
                    accum_out=lns[:, li * 8:li * 8 + 1]), "ac")
            ac_sq = opw("scalar", lambda e: e.activation(
                sqd[:, sl4 * D:(sl4 + 1) * D], xs(), AF.Square,
                accum_out=lns[:, li * 8 + 1:li * 8 + 2]), "ac")
            wt("gpsimd", "ac", ac_sq)
            opw("gpsimd", lambda e: e.tensor_scalar(
                lns[:, li * 8 + 2:li * 8 + 3], lns[:, li * 8:li * 8 + 1],
                -1.0 / D, None, ALU.mult), "po")
            opw("gpsimd", lambda e: e.tensor_tensor(
                lns[:, li * 8 + 3:li * 8 + 4], lns[:, li * 8 + 2:li * 8 + 3],
                lns[:, li * 8 + 2:li * 8 + 3], ALU.mult), "po")
            opw("gpsimd", lambda e: e.tensor_scalar(
                lns[:, li * 8 + 4:li * 8 + 5], lns[:, li * 8 + 1:li * 8 + 2],
                1.0 / D, EPS, ALU.mult, ALU.add), "po")
            po_v = opw("gpsimd", lambda e: e.tensor_tensor(
                lns[:, li * 8 + 4:li * 8 + 5], lns[:, li * 8 + 4:li * 8 + 5],
                lns[:, li * 8 + 3:li * 8 + 4], ALU.subtract), "po")
            wt("scalar", "po", po_v)
            opw("scalar", lambda e: e.activation(
                lns[:, li * 8 + 5:li * 8 + 6], lns[:, li * 8 + 4:li * 8 + 5],
                AF.Ln), "ac")
            ac_r = opw("scalar", lambda e: e.activation(
                lns[:, li * 8 + 6:li * 8 + 7], lns[:, li * 8 + 5:li * 8 + 6],
                AF.Exp, scale=-0.5), "ac")
            wt("gpsimd", "ac", ac_r)
            return opw("gpsimd", lambda e: e.tensor_tensor(
                lns[:, li * 8 + 7:li * 8 + 8], lns[:, li * 8 + 2:li * 8 + 3],
                lns[:, li * 8 + 6:li * 8 + 7], ALU.mult), "po")

        def emit_cast_and_transpose(li, xs, dest_ap, dest_pe_guard, pair, po_lnb):
            sl4 = li % 4
            if ystage_guard[sl4]:
                wt("scalar", "pe", ystage_guard[sl4])
            wt("scalar", "po", po_lnb)
            ac_y = op("scalar", lambda e: e.activation(
                ystage[:, sl4 * D:(sl4 + 1) * D], xs(), AF.Identity,
                scale=lns[:, li * 8 + 6:li * 8 + 7],
                bias=lns[:, li * 8 + 7:li * 8 + 8]), "ac")
            pcol = pair * 1024
            bank_wait("tensor", [2 * pair, 2 * pair + 1])
            wt("tensor", "ac", ac_y)
            for c in range(8):
                ipe = op("tensor", lambda e, c=c, sl4=sl4, pcol=pcol: e.matmul(
                    PS[:, pcol + c * 128:pcol + (c + 1) * 128],
                    ystage[:, sl4 * D + c * 128:sl4 * D + (c + 1) * 128],
                    identb[:, :], start=True, stop=True),
                    "pe" if c == 7 else None)
            ystage_guard[sl4] = ipe
            wt("vector", "pe", ipe)
            if dest_pe_guard:
                wt("vector", "pe", dest_pe_guard)
            ve_evac = op("vector", lambda e, pcol=pcol: e.tensor_copy(
                dest_ap(), PS[:, pcol:pcol + 1024]), "ve")
            bank_set([2 * pair, 2 * pair + 1], "ve", ve_evac)
            return ve_evac

        def emit_ffn(li, bli, xs, accum_sum):
            banks = (bli * 2, bli * 2 + 1)
            pcol = banks[0] * 512
            bank_wait("tensor", banks)
            for c in range(8):
                for hh in range(2):
                    op("tensor", lambda e, c=c, hh=hh: e.matmul(
                        PS[:, pcol + hh * 512: pcol + hh * 512 + 512],
                        y1T[:, li * D + c * 128:li * D + (c + 1) * 128],
                        weff[:, c * D + hh * 512:c * D + hh * 512 + 512],
                        start=(c == 0), stop=False))
            op("tensor", lambda e: e.matmul(
                PS[:, pcol:pcol + 512], onesbr[0:1, :], beff[0:1, 0:512],
                start=False, stop=True))
            pe_f = op("tensor", lambda e: e.matmul(
                PS[:, pcol + 512:pcol + 1024], onesbr[0:1, :], beff[0:1, 512:1024],
                start=False, stop=True), "pe")
            y1T_guard[li] = pe_f
            wt("vector", "pe", pe_f)
            if accum_sum:
                ve_ep = op("vector", lambda e: e.scalar_tensor_tensor(
                    xs(), PS[:, pcol:pcol + 1024], 1.0, xs(), ALU.mult, ALU.add,
                    accum_out=lns[:, li * 8:li * 8 + 1]), "ve")
            else:
                ve_ep = op("vector", lambda e: e.scalar_tensor_tensor(
                    xs(), PS[:, pcol:pcol + 1024], 1.0, xs(), ALU.mult, ALU.add), "ve")
            bank_set(banks, "ve", ve_ep)
            return ve_ep

        units = [(sq_, qb_) for sq_ in range(n_seq) for qb_ in range(n_qb)]
        KTV_prev_seq = 0

        def emit_S1(seq, qb, tl, tinfo):
            gt = seq * NT + qb * QB + tl
            sl = gt % NSLOT
            li = tl
            xs = (lambda sl=sl: xbuf[:, sl * D:(sl + 1) * D])
            po_lnb = emit_ln_chain(li, True, xs, [xload_ms[gt]])
            tp1 = emit_cast_and_transpose(
                li, xs, lambda li=li: y1T[:, li * D:li * D + 1024], y1T_guard[li],
                tl % 2, po_lnb)
            tinfo[tl] = (gt, sl, li, xs, tp1)

        def emit_S2a(seq, qb, tl, tinfo, bli):
            gt, sl, li, xs, tp1 = tinfo[tl][:5]
            wt("tensor", "ve", tp1)
            epi = emit_ffn(li, bli, xs, True)
            tinfo[tl] = tinfo[tl][:5] + (emit_ln_chain(li, False, xs, [("ve", epi)]),)

        def emit_TPS2(seq, qb, tl, tinfo):
            gt, sl, li, xs, tp1, po_lnb2 = tinfo[tl]
            y2ev[0] = emit_cast_and_transpose(
                li, xs, lambda tl=tl: bass.AP(y2T, tl * 128,
                                              [[4096, 128], [512, 8], [1, 128]]),
                y2T_guard if tl == 0 else 0, 2 + tl % 2, po_lnb2)

        def emit_P1mm(seq, qb, tl, pinfo, first):
            gt = seq * NT + qb * QB + tl
            sl = gt % NSLOT
            li = 4 + tl
            xs = (lambda sl=sl: xbuf[:, sl * D:(sl + 1) * D])
            ob = 2048 if (tl % 2) else 3072
            bank_wait("tensor", [ob // 512, ob // 512 + 1])
            if first:
                wt("tensor", "ve", ve_att_prev[0])
            for j in range(8):
                for hh in range(2):
                    op("tensor", lambda e, j=j, hh=hh, tl=tl, ob=ob: e.matmul(
                        PS[:, ob + hh * 512:ob + hh * 512 + 512],
                        attoT[:, j * 512 + tl * 128:j * 512 + (tl + 1) * 128],
                        wout_s[:, j * D + hh * 512:j * D + hh * 512 + 512],
                        start=(j == 0), stop=False))
            op("tensor", lambda e, ob=ob: e.matmul(
                PS[:, ob:ob + 512], onesbr[0:1, :], bout[0:1, 0:512],
                start=False, stop=True))
            pe_op = op("tensor", lambda e, ob=ob: e.matmul(
                PS[:, ob + 512:ob + 1024], onesbr[0:1, :], bout[0:1, 512:1024],
                start=False, stop=True), "pe")
            pinfo[tl] = [gt, sl, li, xs, pe_op, ob, 0, 0]

        def emit_P1epi(tl, pinfo):
            gt, sl, li, xs, pe_op, ob, _, _ = pinfo[tl]
            wt("vector", "pe", pe_op)
            ve_opep = op("vector", lambda e, xs=xs, li=li, ob=ob: e.scalar_tensor_tensor(
                xs(), PS[:, ob:ob + 1024], 1.0, xs(), ALU.mult, ALU.add,
                accum_out=lns[:, li * 8:li * 8 + 1]), "ve")
            bank_set([ob // 512, ob // 512 + 1], "ve", ve_opep)
            pinfo[tl][6] = ve_opep

        def emit_P2a(tl, pinfo):
            gt, sl, li, xs, pe_op, ob, ve_opep, _ = pinfo[tl]
            po_lnb3 = emit_ln_chain(li, False, xs, [("ve", ve_opep)])
            tp3 = emit_cast_and_transpose(
                li, xs, lambda li=li: y1T[:, li * D:li * D + 1024], y1T_guard[li],
                2 + tl % 2, po_lnb3)
            pinfo[tl][7] = tp3

        def emit_P2b(tl, pinfo, bli):
            gt, sl, li, xs, pe_op, ob, ve_opep, tp3 = pinfo[tl]
            wt("tensor", "ve", tp3)
            ve_f2 = emit_ffn(li, bli, xs, False)
            store_ms[gt] = PSTO.issue("sync", lambda e, gt=gt, sl=sl: e.dma_start(
                orows[gt * 128:(gt + 1) * 128, :],
                xbuf[:, sl * D:(sl + 1) * D]), [("ve", ve_f2)])
            ngt2 = gt + NSLOT
            if ngt2 < n_seq * NT:
                xload_ms[ngt2] = PX.issue("sync", lambda e, ngt2=ngt2, sl=sl: e.dma_start(
                    xbuf[:, sl * D:(sl + 1) * D],
                    xrows[ngt2 * 128:(ngt2 + 1) * 128, :]), [store_ms[gt]])

        ve_att_prev = [0]
        y2ev = [0]
        post_pend = None

        for ui in range(len(units) + 1):
            if stage == "setup":
                break
            unit = units[ui] if ui < len(units) else None
            tinfo = {}
            pinfo = {}
            if unit is not None:
                seq, qb = unit
                if qb == 0:
                    op("vector", lambda e: e.memset(mqk[:, 2:3], 0.0), "ve")
            if post_pend is not None:
                pseq, pqb = post_pend

            swq = {}
            if unit is not None:
                for c in range(2):
                    swq[c] = PW.issue("gpsimd", lambda e, c=c: e.dma_start(
                        wstage[:, c * D:(c + 1) * D],
                        wqkv_d.ap()[c * 128:(c + 1) * 128, 0:D]),
                        [("pe", wstage_pe[c])])

            def S1(tl):
                emit_S1(seq, qb, tl, tinfo)

            def S2a(tl, bli=0):
                emit_S2a(seq, qb, tl, tinfo, bli)

            def TPS2(tl):
                emit_TPS2(seq, qb, tl, tinfo)

            def P1(tl):
                emit_P1mm(pseq, pqb, tl, pinfo, tl == 0)

            def P1e(tl):
                emit_P1epi(tl, pinfo)

            def P2a(tl):
                emit_P2a(tl, pinfo)

            def P2b(tl, bli=1):
                emit_P2b(tl, pinfo, bli)

            if unit is not None and post_pend is not None:
                P1(0); S1(0); P1e(0)
                P1(1); S1(1); P1e(1); P2a(0)
                P1(2); S1(2); P1e(2); P2a(1)
                P1(3); S1(3); P1e(3); P2a(2)
                S2a(0, 0); S2a(1, 1); P2a(3); P2b(0, 0); TPS2(0)
                S2a(2, 1); P2b(1, 0); TPS2(1)
                S2a(3, 1); P2b(2, 0); TPS2(2)
                P2b(3, 1); TPS2(3)
            elif unit is not None:
                S1(0); S1(1); S2a(0, 0); S1(2); S2a(1, 1); TPS2(0)
                S1(3); S2a(2, 0); TPS2(1); S2a(3, 1); TPS2(2); TPS2(3)
            else:
                P1(0); P1e(0); P1(1); P1e(1); P2a(0)
                P1(2); P1e(2); P2a(1); P2b(0, 1)
                P1(3); P1e(3); P2a(2); P2b(1, 0)
                P2a(3); P2b(2, 1); P2b(3, 0)
            if post_pend is not None:
                attoT_guard = pinfo[QB - 1][4]
            ac_sb = cnt["ac"]
            if unit is None:
                break

            for c in range(8):
                par = c % 2
                swc = swq[c] if c < 2 else PW.issue(
                    "gpsimd", lambda e, c=c: e.dma_start(
                        wstage[:, (c % 2) * D:((c % 2) + 1) * D],
                        wqkv_d.ap()[c * 128:(c + 1) * 128, 0:D]),
                    [("pe", wstage_pe[par])])
                wtl("tensor", [swc])
                if c == 0:
                    wt("tensor", "ve", y2ev[0])
                    bank_wait("tensor", list(range(8)))
                for m in range(8):
                    ipe = op("tensor", lambda e, c=c, m=m, par=par: e.matmul(
                        PS[:, m * 512:(m + 1) * 512],
                        wstage[:, par * D + m * 128:par * D + (m + 1) * 128],
                        y2T[:, c * 512:(c + 1) * 512],
                        start=(c == 0), stop=(c == 7)), "pe" if m == 7 else None)
                wstage_pe[par] = ipe
            pe_qwave = cnt["pe"]
            swk = {}
            for c in range(2):
                swk[c] = PW.issue("gpsimd", lambda e, c=c: e.dma_start(
                    wstage[:, c * D:(c + 1) * D],
                    wqkv_d.ap()[c * 128:(c + 1) * 128, D:2 * D]),
                    [("pe", wstage_pe[c])])
            if QT_guard:
                wt("vector", "pe", QT_guard)
            wt("vector", "pe", pe_qwave)
            for m in range(8):
                qev = op("vector", lambda e, m=m: e.tensor_scalar(
                    QT[:, m * 512:(m + 1) * 512], PS[:, m * 512:(m + 1) * 512],
                    0.125, bqk8[:, m:m + 1], ALU.mult, ALU.add), "ve")
                bank_guard[m] = ("ve", qev)
            ve_qevac = cnt["ve"]
            if q2_guard[0]:
                wt("vector", "pe", q2_guard[0])
            wt("vector", "ac", ac_sb)
            po_q2 = op("vector", lambda e: e.tensor_tensor(
                sqd[:, 0:4096], QT[:, 0:4096], QT[:, 0:4096], ALU.mult), "ve")

            for c in range(8):
                par = c % 2
                swc = swk[c] if c < 2 else PW.issue(
                    "gpsimd", lambda e, c=c: e.dma_start(
                        wstage[:, (c % 2) * D:((c % 2) + 1) * D],
                        wqkv_d.ap()[c * 128:(c + 1) * 128, D:2 * D]),
                    [("pe", wstage_pe[par])])
                wtl("tensor", [swc])
                for m in range(8):
                    if c == 0:
                        bank_wait("tensor", [m])
                    ipe = op("tensor", lambda e, c=c, m=m, par=par: e.matmul(
                        PS[:, m * 512:(m + 1) * 512],
                        wstage[:, par * D + m * 128:par * D + (m + 1) * 128],
                        y2T[:, c * 512:(c + 1) * 512],
                        start=(c == 0), stop=(c == 7)), "pe" if m == 7 else None)
                wstage_pe[par] = ipe
            pe_kwave = cnt["pe"]
            swv = {}
            for c in range(2):
                swv[c] = PW.issue("gpsimd", lambda e, c=c: e.dma_start(
                    wstage[:, c * D:(c + 1) * D],
                    wqkv_d.ap()[c * 128:(c + 1) * 128, 2 * D:3 * D]),
                    [("pe", wstage_pe[c])])
            if seq == 1 and KTV_prev_seq:
                wt("vector", "pe", KTV_prev_seq)
            wt("vector", "pe", pe_kwave)
            for m in range(8):
                kev = op("vector", lambda e, m=m, qb=qb: e.tensor_scalar(
                    KT[:, m * S + qb * 512:m * S + (qb + 1) * 512],
                    PS[:, m * 512:(m + 1) * 512],
                    bqk8[:, 8 + m:9 + m], None, ALU.add), "ve")
                bank_guard[m] = ("ve", kev)
            ve_kevac = cnt["ve"]
            po_k2 = op("vector", lambda e, qb=qb: e.tensor_tensor(
                ystage[:, 0:4096],
                bass.AP(KT, qb * 512, [[8 * S, 128], [S, 8], [1, 512]]),
                bass.AP(KT, qb * 512, [[8 * S, 128], [S, 8], [1, 512]]), ALU.mult), "ve")

            for c in range(8):
                par = c % 2
                swc = swv[c] if c < 2 else PW.issue(
                    "gpsimd", lambda e, c=c: e.dma_start(
                        wstage[:, (c % 2) * D:((c % 2) + 1) * D],
                        wqkv_d.ap()[c * 128:(c + 1) * 128, 2 * D:3 * D]),
                    [("pe", wstage_pe[par])])
                wtl("tensor", [swc])
                for tl in range(QB):
                    for hh in range(2):
                        if c == 0:
                            bank_wait("tensor", [2 * tl + hh])
                        ipe = op("tensor", lambda e, c=c, tl=tl, hh=hh, par=par: e.matmul(
                            PS[:, tl * 1024 + hh * 512:tl * 1024 + hh * 512 + 512],
                            y2T[:, c * 512 + tl * 128:c * 512 + (tl + 1) * 128],
                            wstage[:, par * D + hh * 512:par * D + hh * 512 + 512],
                            start=(c == 0), stop=False),
                            "pe" if (tl == QB - 1 and hh == 1) else None)
                wstage_pe[par] = ipe
            for tl in range(QB):
                for hh in range(2):
                    ipe = op("tensor", lambda e, tl=tl, hh=hh: e.matmul(
                        PS[:, tl * 1024 + hh * 512:tl * 1024 + hh * 512 + 512],
                        onesbr[0:1, :], bqkv[0:1, hh * 512:hh * 512 + 512],
                        start=False, stop=True), "pe" if (tl == QB - 1 and hh == 1) else None)
            pe_vwave = ipe
            y2T_guard = pe_vwave
            if seq == 1 and KTV_prev_seq:
                wt("vector", "pe", KTV_prev_seq)
            wt("vector", "pe", pe_vwave)
            for tl in range(QB):
                t = qb * QB + tl
                vev = op("vector", lambda e, tl=tl, t=t: e.tensor_copy(
                    V[:, t * D:(t + 1) * D], PS[:, tl * 1024:(tl + 1) * 1024]), "ve")
                bank_guard[2 * tl] = ("ve", vev)
                bank_guard[2 * tl + 1] = ("ve", vev)
            ve_vevac = cnt["ve"]

            wt("tensor", "ve", po_q2)
            bank_wait("tensor", [4, 5])
            for m in range(8):
                for tc in range(4):
                    for r in range(2):
                        ipe = op("tensor", lambda e, m=m, tc=tc, r=r: e.matmul(
                            PS[:, 2048 + tc * 16 + 2 * m + r: 2048 + tc * 16 + 2 * m + r + 1],
                            sqd[r * 64:r * 64 + 64,
                                m * 512 + tc * 128:m * 512 + (tc + 1) * 128],
                            onesb[r * 64:r * 64 + 64, 0:1],
                            start=True, stop=True, tile_position=(r * 64, 0)),
                            "pe" if (m == 7 and tc == 3 and r == 1) else None)
            q2_guard[0] = ipe
            wt("vector", "pe", ipe)
            ve_nrm = op("vector", lambda e: e.tensor_copy(nrm[:, 0:64], PS[:, 2048:2112]), "ve")
            wt("tensor", "ve", ve_nrm)
            for tc in range(4):
                ipe = op("tensor", lambda e, tc=tc: e.transpose(
                    PS[0:16, 2560 + tc * 128:2560 + (tc + 1) * 128],
                    nrm[:, tc * 16:(tc + 1) * 16], idf[:]), "pe" if tc == 3 else None)
            wt("vector", "pe", ipe)
            ve_mq = opw("vector", lambda e: e.reduce_max(
                mqk[:, 0:1], PS[0:16, 2560:3072], axis=AX.X), "ve")
            wt("tensor", "ve", po_k2)
            wt("tensor", "ve", ve_nrm)
            for m in range(8):
                for tc in range(4):
                    for r in range(2):
                        ipe = op("tensor", lambda e, m=m, tc=tc, r=r: e.matmul(
                            PS[:, 2048 + tc * 16 + 2 * m + r: 2048 + tc * 16 + 2 * m + r + 1],
                            ystage[r * 64:r * 64 + 64,
                                   m * 512 + tc * 128:m * 512 + (tc + 1) * 128],
                            onesb[r * 64:r * 64 + 64, 0:1],
                            start=True, stop=True, tile_position=(r * 64, 0)),
                            "pe" if (m == 7 and tc == 3 and r == 1) else None)
            knorm_pe = ipe
            wt("vector", "pe", knorm_pe)
            ve_nrm2 = op("vector", lambda e: e.tensor_copy(nrm[:, 0:64], PS[:, 2048:2112]), "ve")
            wt("tensor", "ve", ve_nrm2)
            for tc in range(4):
                ipe = op("tensor", lambda e, tc=tc: e.transpose(
                    PS[0:16, 2560 + tc * 128:2560 + (tc + 1) * 128],
                    nrm[:, tc * 16:(tc + 1) * 16], idf[:]), "pe" if tc == 3 else None)
            wt("vector", "pe", ipe)
            opw("vector", lambda e: e.reduce_max(mqk[:, 1:2], PS[0:16, 2560:3072], axis=AX.X), "ve")
            ve_mk = opw("vector", lambda e: e.tensor_tensor(
                mqk[:, 2:3], mqk[:, 2:3], mqk[:, 1:2], ALU.max), "ve")

            wt("vector", "ve", ve_mk)
            opw("vector", lambda e: e.tensor_tensor(
                mqk[:, 3:4], mqk[:, 0:1], mqk[:, 2:3], ALU.mult), "ve")
            ve_prod = opw("vector", lambda e: e.tensor_scalar(
                mqk[:, 3:4], mqk[:, 3:4], 1e-30, None, ALU.add), "ve")
            wt("scalar", "ve", ve_prod)
            opw("scalar", lambda e: e.activation(mqk[:, 4:5], mqk[:, 3:4], AF.Ln), "ac")
            ac_c = opw("scalar", lambda e: e.activation(mqk[:, 5:6], mqk[:, 4:5], AF.Exp,
                                                        scale=0.5), "ac")
            wt("tensor", "ac", ac_c)
            pe_crow = op("tensor", lambda e: e.transpose(PS[0:1, 2576:2592], mqk[0:16, 5:6],
                                                         idf[0:16, 0:16]), "pe")
            wt("vector", "pe", pe_crow)
            opw("vector", lambda e: e.tensor_copy(crow[0:1, 0:16], PS[0:1, 2576:2592]), "ve")
            opw("vector", lambda e: e.tensor_tensor(
                crow[0:1, 16:24],
                bass.AP(crow, 0, [[32, 1], [2, 8]]),
                bass.AP(crow, 1, [[32, 1], [2, 8]]), ALU.max), "ve")
            ve_crow = opw("vector", lambda e: e.tensor_scalar(
                crow[0:1, 16:24], crow[0:1, 16:24], -1.0, None, ALU.mult), "ve")
            wt("tensor", "ve", ve_crow)
            pe_cb = op("tensor", lambda e: e.matmul(PS[:, 2048:2056], onesf[0:1, :],
                                                    crow[0:1, 16:24],
                                                    start=True, stop=True), "pe")
            wt("vector", "pe", pe_cb)
            if cbias_guard:
                wt("vector", "ac", cbias_guard)
            ve_cb = op("vector", lambda e: e.tensor_copy(cbias[:], PS[:, 2048:2056]), "ve")
            bank_set([4, 5], "ve", ve_cb)

            nch = 4 * (qb + 1)
            wt("scalar", "ve", ve_cb)
            gw = 0
            for jp in range(4):
                prs = (2 * jp, 2 * jp + 1)
                avb = {prs[0]: 2048, prs[1]: 3072}
                pend = None

                def emit_av(p):
                    j, w, par, off, last, ab, ac_exp, ve_mask = p
                    if w == 0:
                        bank_wait("tensor", [ab // 512, ab // 512 + 1])
                    wt("tensor", "ac", ac_exp)
                    if ve_mask:
                        wt("tensor", "ve", ve_mask)
                    for r in range(2):
                        hidx = 2 * j + r
                        op("tensor", lambda e, w=w, hidx=hidx, r=r, par=par, off=off,
                           last=last, ab=ab, qb=qb: e.matmul(
                            PS[0:64, ab + r * 512 + off:ab + r * 512 + 512],
                            V[:, w * D + hidx * 64:w * D + (hidx + 1) * 64],
                            probsT[:, par * 1024 + r * 512 + off:par * 1024 + (r + 1) * 512],
                            start=(w == 0), stop=last, tile_position=(0, 0)))
                        ipe = op("tensor", lambda e, w=w, r=r, par=par, off=off,
                                 last=last, ab=ab: e.matmul(
                            PS[64:128, ab + r * 512 + off:ab + r * 512 + 512],
                            onesb[:, :],
                            probsT[:, par * 1024 + r * 512 + off:par * 1024 + (r + 1) * 512],
                            start=(w == 0), stop=last, tile_position=(0, 64)), "pe" if r == 1 else None)
                    probs_guard[par] = ipe

                for w in range(nch):
                    for j in prs:
                        kc = w
                        par = gw % 2
                        gw += 1
                        lc = kc * 128 - qb * 512
                        diag = lc >= 0
                        last = (w == nch - 1)
                        ab = avb[j]
                        bank_wait("tensor", [par * 2, par * 2 + 1])
                        op("tensor", lambda e, j=j, kc=kc, par=par: e.matmul(
                            PS[:, par * 1024:par * 1024 + 512],
                            KT[0:64, j * S + kc * 128:j * S + (kc + 1) * 128],
                            QT[0:64, j * 512:(j + 1) * 512],
                            start=True, stop=True, tile_position=(0, 0)))
                        pe_sc = op("tensor", lambda e, j=j, kc=kc, par=par: e.matmul(
                            PS[:, par * 1024 + 512:par * 1024 + 1024],
                            KT[64:128, j * S + kc * 128:j * S + (kc + 1) * 128],
                            QT[64:128, j * 512:(j + 1) * 512],
                            start=True, stop=True, tile_position=(64, 0)), "pe")
                        wt("scalar", "pe", pe_sc)
                        if probs_guard[par]:
                            wt("scalar", "pe", probs_guard[par])
                        ac_exp = op("scalar", lambda e, j=j, par=par: e.activation(
                            probsT[:, par * 1024:(par + 1) * 1024],
                            PS[:, par * 1024:(par + 1) * 1024], AF.Exp,
                            bias=cbias[:, j:j + 1]), "ac")
                        bank_set([par * 2, par * 2 + 1], "ac", ac_exp)
                        ve_mask = 0
                        if diag:
                            wt("vector", "ac", ac_exp)
                            for r in range(2):
                                op("vector", lambda e, r=r, par=par, lc=lc: e.tensor_tensor(
                                    probsT[:, par * 1024 + r * 512 + lc:
                                           par * 1024 + r * 512 + lc + 128],
                                    probsT[:, par * 1024 + r * 512 + lc:
                                           par * 1024 + r * 512 + lc + 128],
                                    tri[:], ALU.mult), "ve")
                            ve_mask = cnt["ve"]
                        if pend is not None:
                            emit_av(pend)
                        pend = (j, w, par, max(lc, 0), last, ab, ac_exp, ve_mask)
                emit_av(pend)
                pe_av = cnt["pe"]
                for j in prs:
                    ab = avb[j]
                    wt("scalar", "pe", pe_av)
                    if recip_guard:
                        wt("scalar", "ve", recip_guard)
                    op("scalar", lambda e, ab=ab: e.activation(
                        recip[:], PS[64:128, ab:ab + 1024], AF.Ln), "ac")
                    ac_rec = op("scalar", lambda e: e.activation(recip[:], recip[:], AF.Exp,
                                                                 scale=-1.0), "ac")
                    wt("vector", "ac", ac_rec)
                    if attoT_guard:
                        wt("vector", "pe", attoT_guard)
                    op("vector", lambda e, j=j, ab=ab: e.tensor_tensor(
                        attoT[0:64, j * 512:(j + 1) * 512], PS[0:64, ab:ab + 512],
                        recip[0:64, 0:512], ALU.mult), "ve")
                    ve_resc = op("vector", lambda e, j=j, ab=ab: e.tensor_tensor(
                        attoT[64:128, j * 512:(j + 1) * 512], PS[0:64, ab + 512:ab + 1024],
                        recip[0:64, 512:1024], ALU.mult), "ve")
                    recip_guard = ve_resc
                    bank_set([ab // 512, ab // 512 + 1], "ve", ve_resc)
            ve_att_prev[0] = cnt["ve"]
            cbias_guard = cnt["ac"]
            QT_guard = cnt["pe"]
            if qb == n_qb - 1:
                KTV_prev_seq = cnt["pe"]
            post_pend = unit

        wtl("sync", PSTO.barrier())

        nc._sched_q = {k: list(v) for k, v in q.items()}
        nc._sched_cnt = dict(cnt)

        def replay(name, eng):
            for item in q[name]:
                if item[0] == "w":
                    eng.wait_ge(sems[item[1]], item[2])
                else:
                    _, fn, smn, amt = item
                    ins = fn(eng)
                    if smn is not None:
                        ins.then_inc(sems[smn], amt)

        @block.sync
        def _(e):
            replay("sync", e)

        @block.tensor
        def _(e):
            replay("tensor", e)

        @block.vector
        def _(e):
            replay("vector", e)

        @block.scalar
        def _(e):
            replay("scalar", e)

        @block.gpsimd
        def _(e):
            replay("gpsimd", e)

    return nc


_NC_CACHE = None


def _get_nc():
    global _NC_CACHE
    if _NC_CACHE is None:
        _NC_CACHE = build_program()
    return _NC_CACHE


def kernel(L0_attn_x, L0_attn_w_qkv, L0_attn_b_qkv, L0_attn_w_out, L0_attn_b_out,
           L0_ffn_kernel, L0_ffn_bias, L0_ffn_a, L0_ffn_b):
    x = np.ascontiguousarray(np.asarray(L0_attn_x, dtype=np.float32))
    wqkv = np.asarray(L0_attn_w_qkv, dtype=np.float32)
    bqkv = np.asarray(L0_attn_b_qkv, dtype=np.float32)
    wout = np.ascontiguousarray(np.asarray(L0_attn_w_out, dtype=np.float32))
    bout = np.ascontiguousarray(np.asarray(L0_attn_b_out, dtype=np.float32).reshape(1, D))
    kern = np.ascontiguousarray(np.asarray(L0_ffn_kernel, dtype=np.float32))
    fbias = np.ascontiguousarray(np.asarray(L0_ffn_bias, dtype=np.float32).reshape(1, D))
    a = np.asarray(L0_ffn_a, dtype=np.float32)
    b = np.ascontiguousarray(np.asarray(L0_ffn_b, dtype=np.float32))

    wq3 = wqkv.reshape(D, H, 3, 64)
    wqkv_r = np.ascontiguousarray(np.concatenate(
        [wq3[:, :, 0, :].reshape(D, D), wq3[:, :, 1, :].reshape(D, D),
         wq3[:, :, 2, :].reshape(D, D)], axis=1))
    bq3 = bqkv.reshape(H, 3, 64)
    bqkv_r = np.ascontiguousarray(np.concatenate(
        [bq3[:, 0, :].reshape(-1), bq3[:, 1, :].reshape(-1),
         bq3[:, 2, :].reshape(-1)])).reshape(1, 3 * D)
    a_l = np.ascontiguousarray(a.reshape(8, 128, 16).transpose(1, 0, 2).reshape(128, 128))

    identf = np.eye(128, dtype=np.float32)
    ones_f = np.ones((1, 128), dtype=np.float32)
    ones_bfr = np.ones((1, 128), dtype=ml_dtypes.bfloat16)
    ones_bf = np.ones((128, 64), dtype=ml_dtypes.bfloat16)
    kk, qq = np.meshgrid(np.arange(128), np.arange(128), indexing="ij")
    tri_m = np.ascontiguousarray((qq >= kk).astype(ml_dtypes.bfloat16))

    nc = _get_nc()
    in_maps = []
    for cidx in range(NCORES):
        in_maps.append({
            "x": np.ascontiguousarray(x[2 * cidx:2 * cidx + 2].reshape(NSEQ * S, D)),
            "wqkv_r": wqkv_r, "bqkv_r": bqkv_r, "wout": wout, "bout": bout,
            "kern": kern, "fbias": fbias, "a_l": a_l, "b_l": b,
            "identf": identf, "ones_f": ones_f, "ones_bfr": ones_bfr,
            "ones_bf": ones_bf, "tri": tri_m,
        })
    res = run_bass_kernel_spmd(nc, in_maps, core_ids=list(range(NCORES)))
    out = np.concatenate([r["out"].reshape(NSEQ, S, D) for r in res.results], axis=0)
    return out


# version 19
# speedup vs baseline: 1.1582x; 1.0451x over previous
import numpy as np
import ml_dtypes
from contextlib import ExitStack

import concourse.bass as bass
import concourse.mybir as mybir
from concourse.bass_utils import run_bass_kernel_spmd

F32 = mybir.dt.float32
BF16 = mybir.dt.bfloat16
AF = mybir.ActivationFunctionType
ALU = mybir.AluOpType
AX = mybir.AxisListType

NCORES = 8
NSEQ = 2
S = 2048
D = 1024
H = 16
NT = 16
QB = 4
NQB = 4
NSLOT = 8
EPS = 1e-5


def build_program(n_seq=NSEQ, n_qb=NQB, stage="full", races=True):
    nc = bass.Bass("TRN2", target_bir_lowering=False, detect_race_conditions=races)

    x_d = nc.dram_tensor("x", [NSEQ * S, D], F32, kind="ExternalInput")
    wqkv_d = nc.dram_tensor("wqkv_r", [D, 3 * D], F32, kind="ExternalInput")
    bqkv_d = nc.dram_tensor("bqkv_r", [1, 3 * D], F32, kind="ExternalInput")
    wout_d = nc.dram_tensor("wout", [D, D], F32, kind="ExternalInput")
    bout_d = nc.dram_tensor("bout", [1, D], F32, kind="ExternalInput")
    kern_d = nc.dram_tensor("kern", [D, D], F32, kind="ExternalInput")
    fbias_d = nc.dram_tensor("fbias", [1, D], F32, kind="ExternalInput")
    al_d = nc.dram_tensor("a_l", [128, 128], F32, kind="ExternalInput")
    bl_d = nc.dram_tensor("b_l", [16, D], F32, kind="ExternalInput")
    idf_d = nc.dram_tensor("identf", [128, 128], F32, kind="ExternalInput")
    onesf_d = nc.dram_tensor("ones_f", [1, 128], F32, kind="ExternalInput")
    onesbr_d = nc.dram_tensor("ones_bfr", [1, 128], BF16, kind="ExternalInput")
    onesb_d = nc.dram_tensor("ones_bf", [128, 64], BF16, kind="ExternalInput")
    tri_d = nc.dram_tensor("tri", [128, 128], BF16, kind="ExternalInput")
    out_d = nc.dram_tensor("out", [NSEQ * S, D], F32, kind="ExternalOutput")
    if stage != "full":
        dbgb_d = nc.dram_tensor("dbgb", [128, 16384], BF16, kind="ExternalOutput")
        dbgf_d = nc.dram_tensor("dbgf", [128, 1024], F32, kind="ExternalOutput")

    ctx = ExitStack()
    with ctx:
        sbt = nc.sbuf_tensor
        xbuf = ctx.enter_context(sbt("xbuf", [128, NSLOT * D], F32))
        KT = ctx.enter_context(sbt("KT", [128, 8 * S], BF16))
        V = ctx.enter_context(sbt("V", [128, NT * D], BF16))
        QT = ctx.enter_context(sbt("QT", [128, 8 * 512], BF16))
        y1T = ctx.enter_context(sbt("y1T", [128, 8 * D], BF16))
        y2T = ctx.enter_context(sbt("y2T", [128, 8 * 512], BF16))
        ystage = ctx.enter_context(sbt("ystage", [128, 4 * D], BF16))
        wstage = ctx.enter_context(sbt("wstage", [128, 3 * D], BF16))
        weff = ctx.enter_context(sbt("weff", [128, 8 * D], BF16))
        wout_s = ctx.enter_context(sbt("wout_s", [128, 8 * D], BF16))
        probsT = ctx.enter_context(sbt("probsT", [128, 2 * 1024], BF16))
        attoT = ctx.enter_context(sbt("attoT", [128, 8 * 512], BF16))
        recip = ctx.enter_context(sbt("recip", [64, 1024], F32))
        sqd = ctx.enter_context(sbt("sqd", [128, 4 * D], BF16))
        bqkv = ctx.enter_context(sbt("bqv_sb", [1, D], BF16))
        beff = ctx.enter_context(sbt("beff_sb", [1, D], BF16))
        bout = ctx.enter_context(sbt("bout_sb", [1, D], BF16))
        bqk8 = ctx.enter_context(sbt("bqk8", [128, 16], F32))
        idf = ctx.enter_context(sbt("idf", [128, 128], F32))
        onesf = ctx.enter_context(sbt("onesf", [1, 128], F32))
        onesbr = ctx.enter_context(sbt("onesbr", [1, 128], BF16))
        onesb = ctx.enter_context(sbt("onesb", [128, 64], BF16))
        identb = ctx.enter_context(sbt("identb", [128, 128], BF16))
        tri = ctx.enter_context(sbt("tri_sb", [128, 128], BF16))
        lns = ctx.enter_context(sbt("lns", [128, 64], F32))
        nrm = ctx.enter_context(sbt("nrm", [128, 64], F32))
        mqk = ctx.enter_context(sbt("mqk", [16, 8], F32))
        crow = ctx.enter_context(sbt("crow", [1, 32], F32))
        cbias = ctx.enter_context(sbt("cbias", [128, 8], F32))
        scal = ctx.enter_context(sbt("scal", [128, 8], F32))
        PS = ctx.enter_context(nc.psum_tensor("PS", [128, 4096], F32))

        semnames = ["pe", "ve", "ac", "po"]
        for pool, npool in (("dx", 2), ("dt", 4), ("ds", 2), ("dw", 2), ("du", 2)):
            semnames += [f"{pool}{i}" for i in range(npool)]
        semnames = tuple(semnames)
        sems = {s: ctx.enter_context(nc.semaphore(s)) for s in semnames}
        block = ctx.enter_context(nc.Block())

        q = {"sync": [], "tensor": [], "vector": [], "scalar": [], "gpsimd": []}
        cnt = {k: 0 for k in semnames}

        def wt(eng, sem, val):
            if val and val > 0:
                q[eng].append(("w", sem, val))

        def op(eng, fn, sem=None, amt=1):
            if sem is not None:
                cnt[sem] += amt
                q[eng].append(("o", fn, sem, amt))
                return cnt[sem]
            q[eng].append(("o", fn, None, 0))
            return None

        class DmaPool:
            def __init__(self, prefix, n):
                self.names = [f"{prefix}{i}" for i in range(n)]
                self.uses = [0] * n
                self.i = 0

            def issue(self, eng, fn, extra_waits=()):
                n = len(self.names)
                p = self.i % n
                self.i += 1
                nm = self.names[p]
                if self.uses[p] > 0:
                    wt(eng, nm, 16 * self.uses[p])
                for (wnm, wval) in extra_waits:
                    if wval:
                        wt(eng, wnm, wval)
                op(eng, fn, nm, 16)
                self.uses[p] += 1
                return (nm, 16 * self.uses[p])

            def barrier(self):
                return [(nm, 16 * u) for nm, u in zip(self.names, self.uses) if u > 0]

        PX = DmaPool("dx", 2)
        PT = DmaPool("dt", 4)
        PSTO = DmaPool("ds", 2)
        PW = DmaPool("dw", 2)
        PU = DmaPool("du", 2)

        def wtl(eng, pairs):
            for nm, v in pairs:
                wt(eng, nm, v)

        def opw(eng, fn, sem):
            v = op(eng, fn, sem)
            wt(eng, sem, v)
            return v

        xrows = x_d.ap()
        orows = out_d.ap()

        PX.issue("sync", lambda e: e.dma_start(idf[:], idf_d.ap()))
        PX.issue("sync", lambda e: e.dma_start(onesf[:], onesf_d.ap()))
        PX.issue("sync", lambda e: e.dma_start(onesbr[:], onesbr_d.ap()))
        PX.issue("sync", lambda e: e.dma_start(onesb[:], onesb_d.ap()))
        PX.issue("sync", lambda e: e.dma_start(tri[:], tri_d.ap()))
        consts_ld = PX.barrier()

        sw_bqkv = PU.issue("gpsimd", lambda e: e.dma_start(bqkv[:], bqkv_d.ap()[0:1, 2 * D:3 * D]))
        sw_beff = PU.issue("gpsimd", lambda e: e.dma_start(beff[:], fbias_d.ap()))
        PU.issue("gpsimd", lambda e: e.dma_start(bout[:], bout_d.ap()))
        sw_al = PU.issue("gpsimd", lambda e: e.dma_start(
            xbuf[:, 7 * D:7 * D + 128], al_d.ap()))
        sw_bl = PU.issue("gpsimd", lambda e: e.dma_start(V[0:16, 0:1024], bl_d.ap()))
        for c in range(8):
            PU.issue("gpsimd", lambda e, c=c: e.dma_start(
                wout_s[:, c * D:(c + 1) * D], wout_d.ap()[c * 128:(c + 1) * 128, :]))

        wtl("vector", [sw_beff])
        op("vector", lambda e: e.tensor_scalar(beff[:], beff[:], 0.5, None, ALU.mult), "ve")
        wtl("vector", consts_ld)
        op("vector", lambda e: e.tensor_copy(identb[:], idf[:]), "ve")

        bqrow = PX.issue("sync", lambda e: e.dma_start(
            xbuf[0:1, 4 * D:4 * D + 1024], bqkv_d.ap()[0:1, 0:D]))
        bkrow = PX.issue("sync", lambda e: e.dma_start(
            recip[0:1, 0:1024], bqkv_d.ap()[0:1, D:2 * D]))
        wtl("tensor", [bqrow, bkrow])
        wtl("tensor", consts_ld)
        for m in range(8):
            for r in range(2):
                hc = (2 * m + r) * 64
                op("tensor", lambda e, m=m, r=r, hc=hc: e.transpose(
                    PS[0:64, r * 16 + m:r * 16 + m + 1], xbuf[0:1, 4 * D + hc:4 * D + hc + 64], idf[0:1, 0:1]))
                pe_bias = op("tensor", lambda e, m=m, r=r, hc=hc: e.transpose(
                    PS[0:64, 32 + r * 16 + m:32 + r * 16 + m + 1], recip[0:1, hc:hc + 64],
                    idf[0:1, 0:1]), "pe")
        wt("vector", "pe", pe_bias)
        op("vector", lambda e: e.tensor_scalar(bqk8[0:64, 0:8], PS[0:64, 0:8], 0.125, None, ALU.mult), "ve")
        op("vector", lambda e: e.tensor_scalar(bqk8[64:128, 0:8], PS[0:64, 16:24], 0.125, None, ALU.mult), "ve")
        op("vector", lambda e: e.tensor_copy(bqk8[0:64, 8:16], PS[0:64, 32:40]), "ve")
        op("vector", lambda e: e.tensor_copy(bqk8[64:128, 8:16], PS[0:64, 48:56]), "ve")

        prev_red = 0
        red4 = {}
        for c in range(8):
            sl = c % 4
            kld = PX.issue("sync", lambda e, c=c, sl=sl: e.dma_start(
                xbuf[:, sl * D:(sl + 1) * D], kern_d.ap()[c * 128:(c + 1) * 128, :]),
                [("ve", red4.get(c - 4, 0))])
            wtl("vector", [kld])
            prev_red = op("vector", lambda e, c=c, sl=sl: e.reduce_max(
                nrm[:, c:c + 1], xbuf[:, sl * D:(sl + 1) * D], axis=AX.X,
                apply_absolute_value=True), "ve")
            red4[c] = prev_red
        wt("tensor", "ve", prev_red)
        op("tensor", lambda e: e.transpose(PS[0:8, 3072:3200], nrm[:, 0:8], idf[:]), "pe")
        wt("vector", "pe", cnt["pe"])
        ve1 = op("vector", lambda e: e.reduce_max(mqk[0:8, 0:1], PS[0:8, 3072:3200], axis=AX.X), "ve")
        wt("tensor", "ve", ve1)
        op("tensor", lambda e: e.transpose(PS[0:1, 3200:3208], mqk[0:8, 0:1], idf[0:8, 0:8]), "pe")
        wt("vector", "pe", cnt["pe"])
        opw("vector", lambda e: e.reduce_max(scal[0:1, 0:1], PS[0:1, 3200:3208], axis=AX.X), "ve")
        opw("vector", lambda e: e.memset(scal[0:1, 1:2], 7.0), "ve")
        opw("vector", lambda e: e.reciprocal(scal[0:1, 2:3], scal[0:1, 1:2]), "ve")
        opw("vector", lambda e: e.reciprocal(scal[0:1, 3:4], scal[0:1, 0:1]), "ve")
        opw("vector", lambda e: e.tensor_scalar(scal[0:1, 4:5], scal[0:1, 3:4], 7.0, None, ALU.mult), "ve")
        opw("vector", lambda e: e.tensor_tensor(scal[0:1, 5:6], scal[0:1, 0:1], scal[0:1, 2:3], ALU.mult), "ve")
        ve_sc = opw("vector", lambda e: e.tensor_scalar(scal[0:1, 5:6], scal[0:1, 5:6], 0.5, None, ALU.mult), "ve")
        wt("tensor", "ve", ve_sc)
        op("tensor", lambda e: e.matmul(PS[:, 3584:3585], onesf[0:1, :], scal[0:1, 4:5],
                                        start=True, stop=True))
        pe_bc = op("tensor", lambda e: e.matmul(PS[:, 3585:3586], onesf[0:1, :], scal[0:1, 5:6],
                                                start=True, stop=True), "pe")
        wt("vector", "pe", pe_bc)
        ve_bc = op("vector", lambda e: e.tensor_copy(scal[:, 6:8], PS[:, 3584:3586]), "ve")
        wtl("tensor", [sw_al])
        for c in range(8):
            pe_aT = op("tensor", lambda e, c=c: e.transpose(
                PS[0:16, 1024 + c * 128:1024 + (c + 1) * 128],
                xbuf[:, 7 * D + c * 16:7 * D + (c + 1) * 16], idf[:]), "pe")
        wt("vector", "pe", pe_aT)
        ve_aT = op("vector", lambda e: e.tensor_copy(y1T[0:16, 0:1024], PS[0:16, 1024:2048]), "ve")

        prev_w = 0
        prev_w2 = {0: 0, 1: 0}
        for c in range(8):
            sl = c % 2
            par = c % 2
            eng = "vector"
            sem = "ve"
            s2 = 2 * D
            s3 = 3 * D
            gate = cnt["ve"] if c == 0 else (prev_red if c == 1 else prev_w2[sl])
            kp = PX.issue("sync", lambda e, c=c, sl=sl: e.dma_start(
                xbuf[:, sl * D:(sl + 1) * D], kern_d.ap()[c * 128:(c + 1) * 128, :]),
                [("ve", gate)])
            if c == 0:
                wt("tensor", "ve", ve_aT)
                wtl("tensor", [sw_bl])
            else:
                wt("tensor", "ve", prev_w)
            op("tensor", lambda e, c=c: e.matmul(
                PS[:, 2048:2560], y1T[0:16, c * 128:(c + 1) * 128],
                V[0:16, 0:512], start=True, stop=True))
            pe_ab = op("tensor", lambda e, c=c: e.matmul(
                PS[:, 2560:3072], y1T[0:16, c * 128:(c + 1) * 128],
                V[0:16, 512:1024], start=True, stop=True), "pe")
            wtl(eng, [kp])
            if c < 2:
                wt(eng, "ve", ve_bc)
            op(eng, lambda e, sl=sl: e.tensor_scalar(
                xbuf[:, sl * D:(sl + 1) * D], xbuf[:, sl * D:(sl + 1) * D],
                scal[:, 6:7], 8.5, ALU.mult, ALU.add), sem)
            op(eng, lambda e, sl=sl, s2=s2: e.tensor_copy(
                xbuf[:, s2:s2 + D], xbuf[:, sl * D:(sl + 1) * D]), sem)
            for bbit in (8.0, 4.0, 2.0, 1.0):
                op(eng, lambda e, b=bbit, s2=s2, s3=s3: e.tensor_scalar(
                    xbuf[:, s3:s3 + D], xbuf[:, s2:s2 + D], b, b, ALU.is_ge, ALU.mult), sem)
                op(eng, lambda e, s2=s2, s3=s3: e.tensor_tensor(
                    xbuf[:, s2:s2 + D], xbuf[:, s2:s2 + D], xbuf[:, s3:s3 + D],
                    ALU.subtract), sem)
            op(eng, lambda e, sl=sl, s2=s2: e.tensor_tensor(
                xbuf[:, sl * D:(sl + 1) * D], xbuf[:, sl * D:(sl + 1) * D],
                xbuf[:, s2:s2 + D], ALU.subtract), sem)
            op(eng, lambda e, sl=sl: e.tensor_scalar(
                xbuf[:, sl * D:(sl + 1) * D], xbuf[:, sl * D:(sl + 1) * D],
                15.0, 0.0, ALU.min, ALU.max), sem)
            chain_last = op(eng, lambda e, sl=sl: e.tensor_scalar(
                xbuf[:, sl * D:(sl + 1) * D], xbuf[:, sl * D:(sl + 1) * D],
                8.0, scal[:, 7:8], ALU.subtract, ALU.mult), sem)
            wt("vector", "pe", pe_ab)
            prev_w = op("vector", lambda e, c=c, sl=sl: e.scalar_tensor_tensor(
                weff[:, c * D:(c + 1) * D], PS[:, 2048:3072], 0.5,
                xbuf[:, sl * D:(sl + 1) * D], ALU.mult, ALU.add), "ve")
            prev_w2[sl] = prev_w
        weff_done = prev_w

        if stage == "setup":
            wt("sync", "ve", weff_done)
            PSTO.issue("sync", lambda e: e.dma_start(dbgb_d.ap()[:, 0:8192], weff[:]))
            PSTO.issue("sync", lambda e: e.dma_start(dbgf_d.ap()[:, 0:16], bqk8[:]))
            PSTO.issue("sync", lambda e: e.dma_start(dbgf_d.ap()[0:1, 16:24], scal[0:1, :]))
            wtl("sync", PSTO.barrier())

        store_ms = {}
        xload_ms = {}
        ystage_guard = [0] * 4
        y1T_guard = [0] * 8
        y2T_guard = 0
        QT_guard = 0
        attoT_guard = 0
        KTV_guard = 0
        bank_guard = [("ve", weff_done)] * 8
        probs_guard = [0, 0]
        av_guard = ("ve", 0)
        recip_guard = 0
        q2_guard = [0, 0]
        cbias_guard = 0
        nrm_guard = 0
        wstage_pe = [0, 0, 0]

        def bank_wait(eng, banks):
            best = {}
            for b in banks:
                smn, v = bank_guard[b]
                if smn is not None and v > best.get(smn, 0):
                    best[smn] = v
            for smn, v in best.items():
                wt(eng, smn, v)

        def bank_set(banks, sem, val):
            for b in banks:
                bank_guard[b] = (sem, val)

        if stage != "setup":
            for gt in range(min(NSLOT, n_seq * NT)):
                xload_ms[gt] = PX.issue("sync", lambda e, gt=gt: e.dma_start(
                    xbuf[:, (gt % NSLOT) * D:((gt % NSLOT) + 1) * D],
                    xrows[gt * 128:(gt + 1) * 128, :]), [("ve", weff_done)])

        def emit_ln_chain(li, need_sum, xs, xguard):
            sl4 = li % 4
            for (gnm, gv) in xguard:
                wt("scalar", gnm, gv)
            wt("scalar", "ac", cnt["ac"])
            if need_sum:
                opw("scalar", lambda e: e.activation(
                    sqd[:, sl4 * D:(sl4 + 1) * D], xs(), AF.Identity,
                    accum_out=lns[:, li * 8:li * 8 + 1]), "ac")
            ac_sq = opw("scalar", lambda e: e.activation(
                sqd[:, sl4 * D:(sl4 + 1) * D], xs(), AF.Square,
                accum_out=lns[:, li * 8 + 1:li * 8 + 2]), "ac")
            wt("gpsimd", "ac", ac_sq)
            opw("gpsimd", lambda e: e.tensor_scalar(
                lns[:, li * 8 + 2:li * 8 + 3], lns[:, li * 8:li * 8 + 1],
                -1.0 / D, None, ALU.mult), "po")
            opw("gpsimd", lambda e: e.tensor_tensor(
                lns[:, li * 8 + 3:li * 8 + 4], lns[:, li * 8 + 2:li * 8 + 3],
                lns[:, li * 8 + 2:li * 8 + 3], ALU.mult), "po")
            opw("gpsimd", lambda e: e.tensor_scalar(
                lns[:, li * 8 + 4:li * 8 + 5], lns[:, li * 8 + 1:li * 8 + 2],
                1.0 / D, EPS, ALU.mult, ALU.add), "po")
            po_v = opw("gpsimd", lambda e: e.tensor_tensor(
                lns[:, li * 8 + 4:li * 8 + 5], lns[:, li * 8 + 4:li * 8 + 5],
                lns[:, li * 8 + 3:li * 8 + 4], ALU.subtract), "po")
            wt("scalar", "po", po_v)
            opw("scalar", lambda e: e.activation(
                lns[:, li * 8 + 5:li * 8 + 6], lns[:, li * 8 + 4:li * 8 + 5],
                AF.Ln), "ac")
            ac_r = opw("scalar", lambda e: e.activation(
                lns[:, li * 8 + 6:li * 8 + 7], lns[:, li * 8 + 5:li * 8 + 6],
                AF.Exp, scale=-0.5), "ac")
            wt("gpsimd", "ac", ac_r)
            return opw("gpsimd", lambda e: e.tensor_tensor(
                lns[:, li * 8 + 7:li * 8 + 8], lns[:, li * 8 + 2:li * 8 + 3],
                lns[:, li * 8 + 6:li * 8 + 7], ALU.mult), "po")

        def emit_cast_and_transpose(li, xs, dest_ap, dest_pe_guard, pair, po_lnb):
            sl4 = li % 4
            if ystage_guard[sl4]:
                wt("scalar", "pe", ystage_guard[sl4])
            wt("scalar", "po", po_lnb)
            ac_y = op("scalar", lambda e: e.activation(
                ystage[:, sl4 * D:(sl4 + 1) * D], xs(), AF.Identity,
                scale=lns[:, li * 8 + 6:li * 8 + 7],
                bias=lns[:, li * 8 + 7:li * 8 + 8]), "ac")
            pcol = pair * 1024
            bank_wait("tensor", [2 * pair, 2 * pair + 1])
            wt("tensor", "ac", ac_y)
            for c in range(8):
                ipe = op("tensor", lambda e, c=c, sl4=sl4, pcol=pcol: e.matmul(
                    PS[:, pcol + c * 128:pcol + (c + 1) * 128],
                    ystage[:, sl4 * D + c * 128:sl4 * D + (c + 1) * 128],
                    identb[:, :], start=True, stop=True),
                    "pe" if c == 7 else None)
            ystage_guard[sl4] = ipe
            wt("vector", "pe", ipe)
            if dest_pe_guard:
                wt("vector", "pe", dest_pe_guard)
            ve_evac = op("vector", lambda e, pcol=pcol: e.tensor_copy(
                dest_ap(), PS[:, pcol:pcol + 1024]), "ve")
            bank_set([2 * pair, 2 * pair + 1], "ve", ve_evac)
            return ve_evac

        def emit_ffn(li, bli, xs, accum_sum):
            banks = (4 + bli * 2, 5 + bli * 2)
            pcol = banks[0] * 512
            bank_wait("tensor", banks)
            for c in range(8):
                for hh in range(2):
                    op("tensor", lambda e, c=c, hh=hh: e.matmul(
                        PS[:, pcol + hh * 512: pcol + hh * 512 + 512],
                        y1T[:, li * D + c * 128:li * D + (c + 1) * 128],
                        weff[:, c * D + hh * 512:c * D + hh * 512 + 512],
                        start=(c == 0), stop=False))
            op("tensor", lambda e: e.matmul(
                PS[:, pcol:pcol + 512], onesbr[0:1, :], beff[0:1, 0:512],
                start=False, stop=True))
            pe_f = op("tensor", lambda e: e.matmul(
                PS[:, pcol + 512:pcol + 1024], onesbr[0:1, :], beff[0:1, 512:1024],
                start=False, stop=True), "pe")
            y1T_guard[li] = pe_f
            wt("vector", "pe", pe_f)
            if accum_sum:
                ve_ep = op("vector", lambda e: e.scalar_tensor_tensor(
                    xs(), PS[:, pcol:pcol + 1024], 1.0, xs(), ALU.mult, ALU.add,
                    accum_out=lns[:, li * 8:li * 8 + 1]), "ve")
            else:
                ve_ep = op("vector", lambda e: e.scalar_tensor_tensor(
                    xs(), PS[:, pcol:pcol + 1024], 1.0, xs(), ALU.mult, ALU.add), "ve")
            bank_set(banks, "ve", ve_ep)
            return ve_ep

        units = [(sq_, qb_) for sq_ in range(n_seq) for qb_ in range(n_qb)]
        KTV_prev_seq = 0

        def emit_S1(seq, qb, tl, tinfo):
            gt = seq * NT + qb * QB + tl
            sl = gt % NSLOT
            li = tl
            xs = (lambda sl=sl: xbuf[:, sl * D:(sl + 1) * D])
            po_lnb = emit_ln_chain(li, True, xs, [xload_ms[gt]])
            tp1 = emit_cast_and_transpose(
                li, xs, lambda li=li: y1T[:, li * D:li * D + 1024], y1T_guard[li],
                2 + tl % 2, po_lnb)
            tinfo[tl] = (gt, sl, li, xs, tp1)

        def emit_S2a(seq, qb, tl, tinfo, bli):
            gt, sl, li, xs, tp1 = tinfo[tl][:5]
            wt("tensor", "ve", tp1)
            epi = emit_ffn(li, bli, xs, True)
            tinfo[tl] = tinfo[tl][:5] + (emit_ln_chain(li, False, xs, [("ve", epi)]),)

        def emit_TPS2(seq, qb, tl, tinfo):
            gt, sl, li, xs, tp1, po_lnb2 = tinfo[tl]
            y2ev[0] = emit_cast_and_transpose(
                li, xs, lambda tl=tl: bass.AP(y2T, tl * 128,
                                              [[4096, 128], [512, 8], [1, 128]]),
                y2T_guard if tl == 0 else 0, tl % 2, po_lnb2)

        def emit_P1mm(seq, qb, tl, pinfo, first):
            gt = seq * NT + qb * QB + tl
            sl = gt % NSLOT
            li = 4 + tl
            xs = (lambda sl=sl: xbuf[:, sl * D:(sl + 1) * D])
            ob = (tl % 2) * 1024
            bank_wait("tensor", [ob // 512, ob // 512 + 1])
            for j in range(8):
                if first:
                    wt("tensor", "ve", att_resc.get(j, 0))
                for hh in range(2):
                    op("tensor", lambda e, j=j, hh=hh, tl=tl, ob=ob: e.matmul(
                        PS[:, ob + hh * 512:ob + hh * 512 + 512],
                        attoT[:, j * 512 + tl * 128:j * 512 + (tl + 1) * 128],
                        wout_s[:, j * D + hh * 512:j * D + hh * 512 + 512],
                        start=(j == 0), stop=False))
            op("tensor", lambda e, ob=ob: e.matmul(
                PS[:, ob:ob + 512], onesbr[0:1, :], bout[0:1, 0:512],
                start=False, stop=True))
            pe_op = op("tensor", lambda e, ob=ob: e.matmul(
                PS[:, ob + 512:ob + 1024], onesbr[0:1, :], bout[0:1, 512:1024],
                start=False, stop=True), "pe")
            pinfo[tl] = [gt, sl, li, xs, pe_op, ob, 0, 0]

        def emit_P1epi(tl, pinfo):
            gt, sl, li, xs, pe_op, ob, _, _ = pinfo[tl]
            wt("vector", "pe", pe_op)
            ve_opep = op("vector", lambda e, xs=xs, li=li, ob=ob: e.scalar_tensor_tensor(
                xs(), PS[:, ob:ob + 1024], 1.0, xs(), ALU.mult, ALU.add,
                accum_out=lns[:, li * 8:li * 8 + 1]), "ve")
            bank_set([ob // 512, ob // 512 + 1], "ve", ve_opep)
            pinfo[tl][6] = ve_opep

        def emit_P2a(tl, pinfo):
            gt, sl, li, xs, pe_op, ob, ve_opep, _ = pinfo[tl]
            po_lnb3 = emit_ln_chain(li, False, xs, [("ve", ve_opep)])
            tp3 = emit_cast_and_transpose(
                li, xs, lambda li=li: y1T[:, li * D:li * D + 1024], y1T_guard[li],
                (tl + 1) % 2, po_lnb3)
            pinfo[tl][7] = tp3

        def emit_P2b(tl, pinfo, bli):
            gt, sl, li, xs, pe_op, ob, ve_opep, tp3 = pinfo[tl]
            wt("tensor", "ve", tp3)
            ve_f2 = emit_ffn(li, bli, xs, False)
            store_ms[gt] = PSTO.issue("sync", lambda e, gt=gt, sl=sl: e.dma_start(
                orows[gt * 128:(gt + 1) * 128, :],
                xbuf[:, sl * D:(sl + 1) * D]), [("ve", ve_f2)])
            ngt2 = gt + NSLOT
            if ngt2 < n_seq * NT:
                xload_ms[ngt2] = PX.issue("sync", lambda e, ngt2=ngt2, sl=sl: e.dma_start(
                    xbuf[:, sl * D:(sl + 1) * D],
                    xrows[ngt2 * 128:(ngt2 + 1) * 128, :]), [store_ms[gt]])

        ve_att_prev = [0]
        att_resc = {}
        y2ev = [0]
        post_pend = None

        for ui in range(len(units) + 1):
            if stage == "setup":
                break
            unit = units[ui] if ui < len(units) else None
            tinfo = {}
            pinfo = {}
            if unit is not None:
                seq, qb = unit
                if qb == 0:
                    op("vector", lambda e: e.memset(mqk[:, 2:3], 0.0), "ve")
            if post_pend is not None:
                pseq, pqb = post_pend

            swq = {}
            if unit is not None:
                for c in range(3):
                    swq[c] = PW.issue("gpsimd", lambda e, c=c: e.dma_start(
                        wstage[:, c * D:(c + 1) * D],
                        wqkv_d.ap()[c * 128:(c + 1) * 128, 0:D]),
                        [("pe", wstage_pe[c])])

            def S1(tl):
                emit_S1(seq, qb, tl, tinfo)

            def S2a(tl, bli=0):
                emit_S2a(seq, qb, tl, tinfo, bli)

            def TPS2(tl):
                emit_TPS2(seq, qb, tl, tinfo)

            def P1(tl):
                emit_P1mm(pseq, pqb, tl, pinfo, tl == 0)

            def P1e(tl):
                emit_P1epi(tl, pinfo)

            def P2a(tl):
                emit_P2a(tl, pinfo)

            def P2b(tl, bli=1):
                emit_P2b(tl, pinfo, bli)

            if unit is not None and post_pend is not None:
                P1(0); S1(0); P1e(0)
                P1(1); S1(1); P1e(1); P2a(0)
                P1(2); S1(2); P1e(2); P2a(1)
                P1(3); S1(3); P1e(3); P2a(2)
                S2a(0, 0); S2a(1, 1); P2a(3); P2b(0, 0); TPS2(0)
                S2a(2, 1); P2b(1, 0); TPS2(1)
                S2a(3, 1); P2b(2, 0); TPS2(2)
                P2b(3, 1); TPS2(3)
            elif unit is not None:
                S1(0); S1(1); S2a(0, 0); S1(2); S2a(1, 1); TPS2(0)
                S1(3); S2a(2, 0); TPS2(1); S2a(3, 1); TPS2(2); TPS2(3)
            else:
                P1(0); P1e(0); P1(1); P1e(1); P2a(0)
                P1(2); P1e(2); P2a(1); P2b(0, 1)
                P1(3); P1e(3); P2a(2); P2b(1, 0)
                P2a(3); P2b(2, 1); P2b(3, 0)
            if post_pend is not None:
                attoT_guard = pinfo[QB - 1][4]
            ac_sb = cnt["ac"]
            if unit is None:
                break

            for c in range(8):
                par = c % 3
                swc = swq[c] if c < 3 else PW.issue(
                    "gpsimd", lambda e, c=c, par=par: e.dma_start(
                        wstage[:, par * D:(par + 1) * D],
                        wqkv_d.ap()[c * 128:(c + 1) * 128, 0:D]),
                    [("pe", wstage_pe[par])])
                wtl("tensor", [swc])
                if c == 0:
                    wt("tensor", "ve", y2ev[0])
                    bank_wait("tensor", list(range(8)))
                for m in range(8):
                    ipe = op("tensor", lambda e, c=c, m=m, par=par: e.matmul(
                        PS[:, m * 512:(m + 1) * 512],
                        wstage[:, par * D + m * 128:par * D + (m + 1) * 128],
                        y2T[:, c * 512:(c + 1) * 512],
                        start=(c == 0), stop=(c == 7)), "pe" if m == 7 else None)
                wstage_pe[par] = ipe
            pe_qwave = cnt["pe"]
            swk = {}
            for c in range(3):
                swk[c] = PW.issue("gpsimd", lambda e, c=c: e.dma_start(
                    wstage[:, c * D:(c + 1) * D],
                    wqkv_d.ap()[c * 128:(c + 1) * 128, D:2 * D]),
                    [("pe", wstage_pe[c])])
            if QT_guard:
                wt("vector", "pe", QT_guard)
            wt("vector", "pe", pe_qwave)
            for m in range(8):
                qev = op("vector", lambda e, m=m: e.tensor_scalar(
                    QT[:, m * 512:(m + 1) * 512], PS[:, m * 512:(m + 1) * 512],
                    0.125, bqk8[:, m:m + 1], ALU.mult, ALU.add), "ve")
                bank_guard[m] = ("ve", qev)
            ve_qevac = cnt["ve"]
            if q2_guard[0]:
                wt("vector", "pe", q2_guard[0])
            wt("vector", "ac", ac_sb)
            po_q2 = op("vector", lambda e: e.tensor_tensor(
                sqd[:, 0:4096], QT[:, 0:4096], QT[:, 0:4096], ALU.mult), "ve")

            for c in range(8):
                par = c % 3
                swc = swk[c] if c < 3 else PW.issue(
                    "gpsimd", lambda e, c=c, par=par: e.dma_start(
                        wstage[:, par * D:(par + 1) * D],
                        wqkv_d.ap()[c * 128:(c + 1) * 128, D:2 * D]),
                    [("pe", wstage_pe[par])])
                wtl("tensor", [swc])
                for m in range(8):
                    if c == 0:
                        bank_wait("tensor", [m])
                    ipe = op("tensor", lambda e, c=c, m=m, par=par: e.matmul(
                        PS[:, m * 512:(m + 1) * 512],
                        wstage[:, par * D + m * 128:par * D + (m + 1) * 128],
                        y2T[:, c * 512:(c + 1) * 512],
                        start=(c == 0), stop=(c == 7)), "pe" if m == 7 else None)
                wstage_pe[par] = ipe
            pe_kwave = cnt["pe"]
            swv = {}
            for c in range(3):
                swv[c] = PW.issue("gpsimd", lambda e, c=c: e.dma_start(
                    wstage[:, c * D:(c + 1) * D],
                    wqkv_d.ap()[c * 128:(c + 1) * 128, 2 * D:3 * D]),
                    [("pe", wstage_pe[c])])
            if seq == 1 and KTV_prev_seq:
                wt("vector", "pe", KTV_prev_seq)
            wt("vector", "pe", pe_kwave)
            for m in range(8):
                kev = op("vector", lambda e, m=m, qb=qb: e.tensor_scalar(
                    KT[:, m * S + qb * 512:m * S + (qb + 1) * 512],
                    PS[:, m * 512:(m + 1) * 512],
                    bqk8[:, 8 + m:9 + m], None, ALU.add), "ve")
                bank_guard[m] = ("ve", kev)
            ve_kevac = cnt["ve"]
            po_k2 = op("vector", lambda e, qb=qb: e.tensor_tensor(
                ystage[:, 0:4096],
                bass.AP(KT, qb * 512, [[8 * S, 128], [S, 8], [1, 512]]),
                bass.AP(KT, qb * 512, [[8 * S, 128], [S, 8], [1, 512]]), ALU.mult), "ve")

            for c in range(8):
                par = c % 3
                swc = swv[c] if c < 3 else PW.issue(
                    "gpsimd", lambda e, c=c, par=par: e.dma_start(
                        wstage[:, par * D:(par + 1) * D],
                        wqkv_d.ap()[c * 128:(c + 1) * 128, 2 * D:3 * D]),
                    [("pe", wstage_pe[par])])
                wtl("tensor", [swc])
                for tl in range(QB):
                    for hh in range(2):
                        if c == 0:
                            bank_wait("tensor", [2 * tl + hh])
                        ipe = op("tensor", lambda e, c=c, tl=tl, hh=hh, par=par: e.matmul(
                            PS[:, tl * 1024 + hh * 512:tl * 1024 + hh * 512 + 512],
                            y2T[:, c * 512 + tl * 128:c * 512 + (tl + 1) * 128],
                            wstage[:, par * D + hh * 512:par * D + hh * 512 + 512],
                            start=(c == 0), stop=False),
                            "pe" if (tl == QB - 1 and hh == 1) else None)
                wstage_pe[par] = ipe
            for tl in range(QB):
                for hh in range(2):
                    ipe = op("tensor", lambda e, tl=tl, hh=hh: e.matmul(
                        PS[:, tl * 1024 + hh * 512:tl * 1024 + hh * 512 + 512],
                        onesbr[0:1, :], bqkv[0:1, hh * 512:hh * 512 + 512],
                        start=False, stop=True), "pe" if (tl == QB - 1 and hh == 1) else None)
            pe_vwave = ipe
            y2T_guard = pe_vwave
            if seq == 1 and KTV_prev_seq:
                wt("vector", "pe", KTV_prev_seq)
            wt("vector", "pe", pe_vwave)
            for tl in range(QB):
                t = qb * QB + tl
                vev = op("vector", lambda e, tl=tl, t=t: e.tensor_copy(
                    V[:, t * D:(t + 1) * D], PS[:, tl * 1024:(tl + 1) * 1024]), "ve")
                bank_guard[2 * tl] = ("ve", vev)
                bank_guard[2 * tl + 1] = ("ve", vev)
            ve_vevac = cnt["ve"]

            wt("tensor", "ve", po_q2)
            bank_wait("tensor", [4, 5])
            for m in range(8):
                for tc in range(4):
                    for r in range(2):
                        ipe = op("tensor", lambda e, m=m, tc=tc, r=r: e.matmul(
                            PS[:, 2048 + tc * 16 + 2 * m + r: 2048 + tc * 16 + 2 * m + r + 1],
                            sqd[r * 64:r * 64 + 64,
                                m * 512 + tc * 128:m * 512 + (tc + 1) * 128],
                            onesb[r * 64:r * 64 + 64, 0:1],
                            start=True, stop=True, tile_position=(r * 64, 0)),
                            "pe" if (m == 7 and tc == 3 and r == 1) else None)
            q2_guard[0] = ipe
            wt("vector", "pe", ipe)
            ve_nrm = op("vector", lambda e: e.tensor_copy(nrm[:, 0:64], PS[:, 2048:2112]), "ve")
            wt("tensor", "ve", ve_nrm)
            for tc in range(4):
                ipe = op("tensor", lambda e, tc=tc: e.transpose(
                    PS[0:16, 2560 + tc * 128:2560 + (tc + 1) * 128],
                    nrm[:, tc * 16:(tc + 1) * 16], idf[:]), "pe" if tc == 3 else None)
            wt("vector", "pe", ipe)
            ve_mq = opw("vector", lambda e: e.reduce_max(
                mqk[:, 0:1], PS[0:16, 2560:3072], axis=AX.X), "ve")
            wt("tensor", "ve", po_k2)
            wt("tensor", "ve", ve_nrm)
            for m in range(8):
                for tc in range(4):
                    for r in range(2):
                        ipe = op("tensor", lambda e, m=m, tc=tc, r=r: e.matmul(
                            PS[:, 2048 + tc * 16 + 2 * m + r: 2048 + tc * 16 + 2 * m + r + 1],
                            ystage[r * 64:r * 64 + 64,
                                   m * 512 + tc * 128:m * 512 + (tc + 1) * 128],
                            onesb[r * 64:r * 64 + 64, 0:1],
                            start=True, stop=True, tile_position=(r * 64, 0)),
                            "pe" if (m == 7 and tc == 3 and r == 1) else None)
            knorm_pe = ipe
            wt("vector", "pe", knorm_pe)
            ve_nrm2 = op("vector", lambda e: e.tensor_copy(nrm[:, 0:64], PS[:, 2048:2112]), "ve")
            wt("tensor", "ve", ve_nrm2)
            for tc in range(4):
                ipe = op("tensor", lambda e, tc=tc: e.transpose(
                    PS[0:16, 2560 + tc * 128:2560 + (tc + 1) * 128],
                    nrm[:, tc * 16:(tc + 1) * 16], idf[:]), "pe" if tc == 3 else None)
            wt("vector", "pe", ipe)
            opw("vector", lambda e: e.reduce_max(mqk[:, 1:2], PS[0:16, 2560:3072], axis=AX.X), "ve")
            ve_mk = opw("vector", lambda e: e.tensor_tensor(
                mqk[:, 2:3], mqk[:, 2:3], mqk[:, 1:2], ALU.max), "ve")

            wt("vector", "ve", ve_mk)
            opw("vector", lambda e: e.tensor_tensor(
                mqk[:, 3:4], mqk[:, 0:1], mqk[:, 2:3], ALU.mult), "ve")
            ve_prod = opw("vector", lambda e: e.tensor_scalar(
                mqk[:, 3:4], mqk[:, 3:4], 1e-30, None, ALU.add), "ve")
            wt("scalar", "ve", ve_prod)
            opw("scalar", lambda e: e.activation(mqk[:, 4:5], mqk[:, 3:4], AF.Ln), "ac")
            ac_c = opw("scalar", lambda e: e.activation(mqk[:, 5:6], mqk[:, 4:5], AF.Exp,
                                                        scale=0.5), "ac")
            wt("tensor", "ac", ac_c)
            pe_crow = op("tensor", lambda e: e.transpose(PS[0:1, 2576:2592], mqk[0:16, 5:6],
                                                         idf[0:16, 0:16]), "pe")
            wt("vector", "pe", pe_crow)
            opw("vector", lambda e: e.tensor_copy(crow[0:1, 0:16], PS[0:1, 2576:2592]), "ve")
            opw("vector", lambda e: e.tensor_tensor(
                crow[0:1, 16:24],
                bass.AP(crow, 0, [[32, 1], [2, 8]]),
                bass.AP(crow, 1, [[32, 1], [2, 8]]), ALU.max), "ve")
            ve_crow = opw("vector", lambda e: e.tensor_scalar(
                crow[0:1, 16:24], crow[0:1, 16:24], -1.0, None, ALU.mult), "ve")
            wt("tensor", "ve", ve_crow)
            pe_cb = op("tensor", lambda e: e.matmul(PS[:, 2048:2056], onesf[0:1, :],
                                                    crow[0:1, 16:24],
                                                    start=True, stop=True), "pe")
            wt("vector", "pe", pe_cb)
            if cbias_guard:
                wt("vector", "ac", cbias_guard)
            ve_cb = op("vector", lambda e: e.tensor_copy(cbias[:], PS[:, 2048:2056]), "ve")
            bank_set([4, 5], "ve", ve_cb)

            nch = 4 * (qb + 1)
            wt("scalar", "ve", ve_cb)
            gw = 0
            for jp in range(4):
                prs = (2 * jp, 2 * jp + 1)
                avb = {prs[0]: 2048, prs[1]: 3072}
                pend = None

                def emit_av(p):
                    j, w, par, off, last, ab, ac_exp, ve_mask = p
                    if w == 0:
                        bank_wait("tensor", [ab // 512, ab // 512 + 1])
                    wt("tensor", "ac", ac_exp)
                    if ve_mask:
                        wt("tensor", "ve", ve_mask)
                    for r in range(2):
                        hidx = 2 * j + r
                        op("tensor", lambda e, w=w, hidx=hidx, r=r, par=par, off=off,
                           last=last, ab=ab, qb=qb: e.matmul(
                            PS[0:64, ab + r * 512 + off:ab + r * 512 + 512],
                            V[:, w * D + hidx * 64:w * D + (hidx + 1) * 64],
                            probsT[:, par * 1024 + r * 512 + off:par * 1024 + (r + 1) * 512],
                            start=(w == 0), stop=last, tile_position=(0, 0)))
                        ipe = op("tensor", lambda e, w=w, r=r, par=par, off=off,
                                 last=last, ab=ab: e.matmul(
                            PS[64:128, ab + r * 512 + off:ab + r * 512 + 512],
                            onesb[:, :],
                            probsT[:, par * 1024 + r * 512 + off:par * 1024 + (r + 1) * 512],
                            start=(w == 0), stop=last, tile_position=(0, 64)), "pe" if r == 1 else None)
                    probs_guard[par] = ipe

                for w in range(nch):
                    for j in prs:
                        kc = w
                        par = gw % 2
                        gw += 1
                        lc = kc * 128 - qb * 512
                        diag = lc >= 0
                        last = (w == nch - 1)
                        ab = avb[j]
                        bank_wait("tensor", [par * 2, par * 2 + 1])
                        op("tensor", lambda e, j=j, kc=kc, par=par: e.matmul(
                            PS[:, par * 1024:par * 1024 + 512],
                            KT[0:64, j * S + kc * 128:j * S + (kc + 1) * 128],
                            QT[0:64, j * 512:(j + 1) * 512],
                            start=True, stop=True, tile_position=(0, 0)))
                        pe_sc = op("tensor", lambda e, j=j, kc=kc, par=par: e.matmul(
                            PS[:, par * 1024 + 512:par * 1024 + 1024],
                            KT[64:128, j * S + kc * 128:j * S + (kc + 1) * 128],
                            QT[64:128, j * 512:(j + 1) * 512],
                            start=True, stop=True, tile_position=(64, 0)), "pe")
                        wt("scalar", "pe", pe_sc)
                        if probs_guard[par]:
                            wt("scalar", "pe", probs_guard[par])
                        ac_exp = op("scalar", lambda e, j=j, par=par: e.activation(
                            probsT[:, par * 1024:(par + 1) * 1024],
                            PS[:, par * 1024:(par + 1) * 1024], AF.Exp,
                            bias=cbias[:, j:j + 1]), "ac")
                        bank_set([par * 2, par * 2 + 1], "ac", ac_exp)
                        ve_mask = 0
                        if diag:
                            wt("vector", "ac", ac_exp)
                            for r in range(2):
                                op("vector", lambda e, r=r, par=par, lc=lc: e.tensor_tensor(
                                    probsT[:, par * 1024 + r * 512 + lc:
                                           par * 1024 + r * 512 + lc + 128],
                                    probsT[:, par * 1024 + r * 512 + lc:
                                           par * 1024 + r * 512 + lc + 128],
                                    tri[:], ALU.mult), "ve")
                            ve_mask = cnt["ve"]
                        if pend is not None:
                            emit_av(pend)
                        pend = (j, w, par, max(lc, 0), last, ab, ac_exp, ve_mask)
                emit_av(pend)
                pe_av = cnt["pe"]
                for j in prs:
                    ab = avb[j]
                    wt("scalar", "pe", pe_av)
                    if recip_guard:
                        wt("scalar", "ve", recip_guard)
                    op("scalar", lambda e, ab=ab: e.activation(
                        recip[:], PS[64:128, ab:ab + 1024], AF.Ln), "ac")
                    ac_rec = op("scalar", lambda e: e.activation(recip[:], recip[:], AF.Exp,
                                                                 scale=-1.0), "ac")
                    wt("vector", "ac", ac_rec)
                    if attoT_guard:
                        wt("vector", "pe", attoT_guard)
                    op("vector", lambda e, j=j, ab=ab: e.tensor_tensor(
                        attoT[0:64, j * 512:(j + 1) * 512], PS[0:64, ab:ab + 512],
                        recip[0:64, 0:512], ALU.mult), "ve")
                    ve_resc = op("vector", lambda e, j=j, ab=ab: e.tensor_tensor(
                        attoT[64:128, j * 512:(j + 1) * 512], PS[0:64, ab + 512:ab + 1024],
                        recip[0:64, 512:1024], ALU.mult), "ve")
                    recip_guard = ve_resc
                    att_resc[j] = ve_resc
                    bank_set([ab // 512, ab // 512 + 1], "ve", ve_resc)
            ve_att_prev[0] = cnt["ve"]
            cbias_guard = cnt["ac"]
            QT_guard = cnt["pe"]
            if qb == n_qb - 1:
                KTV_prev_seq = cnt["pe"]
            post_pend = unit

        wtl("sync", PSTO.barrier())

        nc._sched_q = {k: list(v) for k, v in q.items()}
        nc._sched_cnt = dict(cnt)

        def replay(name, eng):
            for item in q[name]:
                if item[0] == "w":
                    eng.wait_ge(sems[item[1]], item[2])
                else:
                    _, fn, smn, amt = item
                    ins = fn(eng)
                    if smn is not None:
                        ins.then_inc(sems[smn], amt)

        @block.sync
        def _(e):
            replay("sync", e)

        @block.tensor
        def _(e):
            replay("tensor", e)

        @block.vector
        def _(e):
            replay("vector", e)

        @block.scalar
        def _(e):
            replay("scalar", e)

        @block.gpsimd
        def _(e):
            replay("gpsimd", e)

    return nc


_NC_CACHE = None


def _get_nc():
    global _NC_CACHE
    if _NC_CACHE is None:
        _NC_CACHE = build_program()
    return _NC_CACHE


def kernel(L0_attn_x, L0_attn_w_qkv, L0_attn_b_qkv, L0_attn_w_out, L0_attn_b_out,
           L0_ffn_kernel, L0_ffn_bias, L0_ffn_a, L0_ffn_b):
    x = np.ascontiguousarray(np.asarray(L0_attn_x, dtype=np.float32))
    wqkv = np.asarray(L0_attn_w_qkv, dtype=np.float32)
    bqkv = np.asarray(L0_attn_b_qkv, dtype=np.float32)
    wout = np.ascontiguousarray(np.asarray(L0_attn_w_out, dtype=np.float32))
    bout = np.ascontiguousarray(np.asarray(L0_attn_b_out, dtype=np.float32).reshape(1, D))
    kern = np.ascontiguousarray(np.asarray(L0_ffn_kernel, dtype=np.float32))
    fbias = np.ascontiguousarray(np.asarray(L0_ffn_bias, dtype=np.float32).reshape(1, D))
    a = np.asarray(L0_ffn_a, dtype=np.float32)
    b = np.ascontiguousarray(np.asarray(L0_ffn_b, dtype=np.float32))

    wq3 = wqkv.reshape(D, H, 3, 64)
    wqkv_r = np.ascontiguousarray(np.concatenate(
        [wq3[:, :, 0, :].reshape(D, D), wq3[:, :, 1, :].reshape(D, D),
         wq3[:, :, 2, :].reshape(D, D)], axis=1))
    bq3 = bqkv.reshape(H, 3, 64)
    bqkv_r = np.ascontiguousarray(np.concatenate(
        [bq3[:, 0, :].reshape(-1), bq3[:, 1, :].reshape(-1),
         bq3[:, 2, :].reshape(-1)])).reshape(1, 3 * D)
    a_l = np.ascontiguousarray(a.reshape(8, 128, 16).transpose(1, 0, 2).reshape(128, 128))

    identf = np.eye(128, dtype=np.float32)
    ones_f = np.ones((1, 128), dtype=np.float32)
    ones_bfr = np.ones((1, 128), dtype=ml_dtypes.bfloat16)
    ones_bf = np.ones((128, 64), dtype=ml_dtypes.bfloat16)
    kk, qq = np.meshgrid(np.arange(128), np.arange(128), indexing="ij")
    tri_m = np.ascontiguousarray((qq >= kk).astype(ml_dtypes.bfloat16))

    nc = _get_nc()
    in_maps = []
    for cidx in range(NCORES):
        in_maps.append({
            "x": np.ascontiguousarray(x[2 * cidx:2 * cidx + 2].reshape(NSEQ * S, D)),
            "wqkv_r": wqkv_r, "bqkv_r": bqkv_r, "wout": wout, "bout": bout,
            "kern": kern, "fbias": fbias, "a_l": a_l, "b_l": b,
            "identf": identf, "ones_f": ones_f, "ones_bfr": ones_bfr,
            "ones_bf": ones_bf, "tri": tri_m,
        })
    res = run_bass_kernel_spmd(nc, in_maps, core_ids=list(range(NCORES)))
    out = np.concatenate([r["out"].reshape(NSEQ, S, D) for r in res.results], axis=0)
    return out
